# version 04
# baseline (speedup 1.0000x reference)
import numpy as np
from contextlib import ExitStack

import concourse.bass as bass
import concourse.mybir as mybir
from concourse.bass_utils import run_bass_kernel_spmd

F32 = mybir.dt.float32
F16 = mybir.dt.float16
BF16 = mybir.dt.bfloat16
AF = mybir.ActivationFunctionType
ALU = mybir.AluOpType
AX = mybir.AxisListType

P = 128
S = 2048
H = 2048
HC = 16
FF0C = 64
FF1C = 32
FF1 = 4096
OWN = 1024
LN_EPS = 1e-5
INV_H = 1.0 / H
T0 = 512
T1 = 256
NEGBIG = -1.0e38

N_SUBLAYERS = 6
DEV_ONE_TILE = False
DEV_DUMP = False
PARANOID = 0x6


def _build(n_sub=N_SUBLAYERS):
    nc = bass.Bass("TRN2", target_bir_lowering=False,
                   detect_race_conditions=False)
    es = ExitStack()
    sems, cnt = {}, {}

    def mksem(name):
        sems[name] = es.enter_context(nc.semaphore(name))
        cnt[name] = 0

    def inc(inst, name, k=1):
        inst.then_inc(sems[name], k)
        cnt[name] += k
        return cnt[name]

    def incd(eng, name, k=1):
        d = eng.drain()
        d.then_inc(sems[name], k)
        cnt[name] += k
        return cnt[name]

    def wall(eng, *names):
        for n in names:
            eng.wait_ge(sems[n], cnt[n])

    def din(name, shape, dt=F32):
        return nc.dram_tensor(name, list(shape), dt, kind="ExternalInput").ap()

    def dint(name, shape, dt=F32):
        return nc.dram_tensor(name, list(shape), dt, kind="Internal").ap()

    x0 = din("x0", [H, S])
    w0p = din("w0p", [FF0C, P, HC * P])
    w1p = din("w1p", [HC, P, FF0C * P])
    b0v = din("b0v", [P, FF0C])
    b1v = din("b1v", [P, HC])
    wqp = [din("wqp0", [HC, P, HC * P]), din("wqp1", [HC, P, HC * P])]
    wkp = [din("wkp0", [HC, P, HC * P]), din("wkp1", [HC, P, HC * P])]
    wvp = [din("wvp0", [HC, P, HC * P]), din("wvp1", [HC, P, HC * P])]
    wop = [din("wop0", [HC, P, HC * P]), din("wop1", [HC, P, HC * P])]
    bqv = din("bqv", [P, HC])
    bkv = din("bkv", [P, HC])
    bvv = din("bvv", [P, HC])
    bov = din("bov", [P, HC])
    wup = din("wup", [FF1C, P, HC * P])
    wdp = din("wdp", [HC, P, FF1C * P])
    mss = din("mss", [OWN, S], BF16)
    mdl = din("mdl", [OWN, S], BF16)
    idn16 = din("idn16", [P, P], F16)
    idn32 = din("idn32", [P, P], F32)

    y = nc.dram_tensor("y", [H, OWN], F32, kind="ExternalOutput").ap()

    xa = dint("xa", [H, S])
    xb = dint("xb", [H, S])
    qhi_d = dint("q_hi", [H, S], F16)
    qlo_d = dint("q_lo", [H, S], F16)
    khi_d = dint("k_hi", [H, S], F16)
    klo_d = dint("k_lo", [H, S], F16)
    vhi_d = dint("v_hi", [H, S], F16)
    vlo_d = dint("v_lo", [H, S], F16)
    athi_d = dint("at_hi", [H, S], F16)
    atlo_d = dint("at_lo", [H, S], F16)
    tup = dint("tup", [FF1C, P, HC * P], BF16)
    tdn = dint("tdn", [HC, P, FF1C * P], BF16)
    hscr = dint("hscr", [FF1, S])
    _dd = (lambda n, sh, dt=F32: nc.dram_tensor(n, list(sh), dt,
           kind="ExternalOutput").ap()) if DEV_DUMP else dint
    dbg1 = _dd("dbg1", [FF1, S], BF16)
    dbg2 = _dd("dbg2", [P, 2048])
    dbgX = _dd("dbgX", [P, 8192], F16)
    dbgD = _dd("dbgD", [P, 32768], F16)

    def sbt(name, shape, dt=F32):
        return es.enter_context(nc.sbuf_tensor(name, list(shape), dt))

    b0_sb = sbt("b0_sb", [P, FF0C])
    b1_sb = sbt("b1_sb", [P, HC])
    bq_sb = sbt("bq_sb", [P, HC])
    bk_sb = sbt("bk_sb", [P, HC])
    bv_sb = sbt("bv_sb", [P, HC])
    bo_sb = sbt("bo_sb", [P, HC])
    id16 = sbt("id16", [P, P], F16)
    id32 = sbt("id32", [P, P], F32)
    onescol = sbt("onescol", [1, P])
    ones128 = sbt("ones128", [P, 1])
    gcol = sbt("gcol", [P, 1])
    dacc = sbt("dacc", [P, 1])
    qsc = sbt("qsc", [P, 16])

    f16big = sbt("f16big", [P, 32768], F16)
    f16b = sbt("f16b", [P, 8192], F16)
    bf16a = sbt("bf16a", [P, 16384], BF16)
    bf16st = sbt("bf16st", [P, 2048], BF16)
    f32w = sbt("f32w", [P, 4096])
    f16w = sbt("f16w", [P, 8192], F16)
    f32g = sbt("f32g", [P, 7168])
    rows = sbt("rows", [1, 1024])
    rows2 = sbt("rows2", [1, 1024])

    ps = [es.enter_context(nc.psum_tensor(f"ps{i}", [P, 512], F32))
          for i in range(7)]
    psT = es.enter_context(nc.psum_tensor("psT", [P, 1024], F16))

    for s_ in ("ld0", "ld1", "g20", "g21", "mm", "sc", "ve", "so0", "so1"):
        mksem(s_)

    def snap(base):
        return (cnt[base + "0"], cnt[base + "1"])

    def pdma(eng, out, in_, base, j):
        return inc(eng.dma_start(out=out, in_=in_), base + str(j % 2), 16)

    def pval(base, sn, j):
        return base + str(j % 2), sn[j % 2] + 16 * (j // 2 + 1)

    def pwait(eng, base, sn, j):
        n, v = pval(base, sn, j)
        eng.wait_ge(sems[n], v)

    blk = es.enter_context(nc.Block())

    xsb = [f32g[:, 0:512], f32g[:, 512:1024]]
    sqb = [f32g[:, 1024:1536], f32g[:, 1536:2048]]
    A_ = f32g[:, 2048:2560]
    B_ = f32g[:, 2560:3072]
    xn2 = [f32g[:, 3072:3584], f32g[:, 3584:4096]]
    pe_ = f32g[:, 4096:6144]
    misc = f32g[:, 6144:7168]

    @blk.sync
    def _(sync):
        for dst, src in ((b0_sb, b0v), (b1_sb, b1v), (bq_sb, bqv),
                         (bk_sb, bkv), (bv_sb, bvv), (bo_sb, bov),
                         (id16, idn16), (id32, idn32)):
            inc(sync.dma_start(out=dst[:], in_=src[:]), "ld0", 16)

    @blk.vector
    def _(vector):
        wall(vector, "ld0", "ld1")
        vector.memset(onescol[:], 1.0)
        vector.memset(ones128[:], 1.0)
        vector.memset(misc[:, 1016:1024], 0.0)
        vector.memset(dacc[:], 0.0)
        incd(vector, "ve")

    def emit_barrier():
        for sec in (blk.sync, blk.gpsimd, blk.scalar, blk.vector, blk.tensor):
            @sec
            def _(eng):
                wall(eng, "ld0", "ld1", "g20", "g21", "so0", "so1",
                     "mm", "sc", "ve")

    def bcast_row(row_ap, width, dst_ap):
        @blk.tensor
        def _(tensor):
            wall(tensor, "ve", "sc")
            inc(tensor.matmul(ps[6][:, :width], onescol[:, :], row_ap,
                              start=True, stop=True), "mm")

        @blk.vector
        def _(vector):
            wall(vector, "mm")
            vector.tensor_copy(out=dst_ap, in_=ps[6][:, :width])
            incd(vector, "ve")

    def col_to_row(col_ap, row_dst_ap):
        @blk.tensor
        def _(tensor):
            wall(tensor, "ve", "sc")
            inc(tensor.transpose(out=ps[6][0:1, 0:P], in_=col_ap,
                                 identity=id32[:]), "mm")

        @blk.vector
        def _(vector):
            wall(vector, "mm")
            vector.tensor_copy(out=row_dst_ap, in_=ps[6][0:1, 0:P])
            incd(vector, "ve")

    def emit_ln(xin, t0, T, mode, density=False):
        ldsn, sc_b, mm_b = snap("ld"), cnt["sc"], cnt["mm"]

        @blk.sync
        def _(sync):
            wall(sync, "ve", "sc", "mm")
            for ci in range(HC):
                if ci >= 2:
                    sync.wait_ge(sems["mm"], mm_b + (ci - 1))
                pdma(sync, xsb[ci % 2][:, :T],
                     xin[ci * P:(ci + 1) * P, t0:t0 + T], "ld", ci)

        @blk.scalar
        def _(scalar):
            for ci in range(HC):
                pwait(scalar, "ld", ldsn, ci)
                if ci >= 2:
                    scalar.wait_ge(sems["mm"], mm_b + (ci - 1))
                scalar.activation(out=sqb[ci % 2][:, :T],
                                      in_=xsb[ci % 2][:, :T], func=AF.Square)
                incd(scalar, "sc")

        @blk.tensor
        def _(tensor):
            for ci in range(HC):
                tensor.wait_ge(sems["sc"], sc_b + ci + 1)
                tensor.matmul(ps[0][0:1, :T], ones128[:, :1],
                              xsb[ci % 2][:, :T],
                              start=(ci == 0), stop=(ci == HC - 1))
                inc(tensor.matmul(ps[1][0:1, :T], ones128[:, :1],
                                  sqb[ci % 2][:, :T],
                                  start=(ci == 0), stop=(ci == HC - 1)), "mm")

        @blk.vector
        def _(vector):
            wall(vector, "mm")
            r, r2 = rows[0:1, :], rows2[0:1, :]
            vector.tensor_single_scalar(out=r2[:, 0:T], in_=ps[0][0:1, :T],
                                        scalar=INV_H, op=ALU.mult)
            vector.tensor_single_scalar(out=r[:, 0:T], in_=ps[1][0:1, :T],
                                        scalar=INV_H, op=ALU.mult)
            vector.tensor_tensor(out=r2[:, 512:512 + T], in0=r2[:, 0:T],
                                 in1=r2[:, 0:T], op=ALU.mult)
            vector.tensor_tensor(out=r[:, 512:512 + T], in0=r[:, 0:T],
                                 in1=r2[:, 512:512 + T], op=ALU.subtract)
            vector.tensor_single_scalar(out=r[:, 0:T],
                                            in_=r[:, 512:512 + T],
                                            scalar=LN_EPS, op=ALU.add)
            incd(vector, "ve")

        @blk.scalar
        def _(scalar):
            wall(scalar, "ve")
            scalar.activation(out=rows2[0:1, 512:512 + T],
                                  in_=rows[0:1, 0:T], func=AF.Sqrt)
            incd(scalar, "sc")

        @blk.vector
        def _(vector):
            wall(vector, "sc")
            r, r2 = rows[0:1, :], rows2[0:1, :]
            vector.reciprocal(out=r2[:, 512:512 + T], in_=r2[:, 512:512 + T])
            vector.drain()
            vector.tensor_tensor(out=r[:, 512:512 + T], in0=r2[:, 512:512 + T],
                                 in1=r2[:, 512:512 + T], op=ALU.mult)
            vector.drain()
            vector.tensor_tensor(out=r[:, 512:512 + T], in0=r[:, 0:T],
                                 in1=r[:, 512:512 + T], op=ALU.mult)
            vector.drain()
            vector.tensor_scalar(out=r[:, 512:512 + T], in0=r[:, 512:512 + T],
                                 scalar1=-0.5, scalar2=1.5,
                                 op0=ALU.mult, op1=ALU.add)
            vector.drain()
            vector.tensor_tensor(out=r[:, 0:T], in0=r2[:, 512:512 + T],
                                 in1=r[:, 512:512 + T], op=ALU.mult)
            vector.drain()
            vector.tensor_tensor(out=r2[:, 512:512 + T], in0=r2[:, 0:T],
                                 in1=r[:, 0:T], op=ALU.mult)
            vector.drain()
            vector.tensor_single_scalar(out=r2[:, 512:512 + T],
                                            in_=r2[:, 512:512 + T],
                                            scalar=-1.0, op=ALU.mult)
            incd(vector, "ve")

        @blk.tensor
        def _(tensor):
            wall(tensor, "ve")
            tensor.matmul(ps[2][:, :T], onescol[:, :], rows[0:1, 0:T],
                          start=True, stop=True)
            inc(tensor.matmul(ps[3][:, :T], onescol[:, :],
                              rows2[0:1, 512:512 + T], start=True, stop=True),
                "mm")

        @blk.vector
        def _(vector):
            wall(vector, "mm")
            vector.tensor_copy(out=A_[:, :T], in_=ps[2][:, :T])
            vector.tensor_copy(out=B_[:, :T], in_=ps[3][:, :T])
            incd(vector, "ve")

        if mode == "stats":
            return

        g2sn, ve_b = snap("g2"), cnt["ve"]
        ops = {"f16": 2, "hilo": 3, "absmax": 3}[mode] + (5 if density else 0)

        @blk.gpsimd
        def _(g):
            wall(g, "ve")
            for ci in range(HC):
                if ci >= 2:
                    g.wait_ge(sems["ve"], ve_b + (ci - 1) * ops)
                pdma(g, xsb[ci % 2][:, :T],
                     xin[ci * P:(ci + 1) * P, t0:t0 + T], "g2", ci)

        @blk.vector
        def _(vector):
            wall(vector, "so0", "so1")
            if mode == "absmax":
                vector.memset(misc[:, 0:T], 0.0)
            for ci in range(HC):
                pwait(vector, "g2", g2sn, ci)
                vector.tensor_tensor(out=xn2[ci % 2][:, :T],
                                     in0=xsb[ci % 2][:, :T], in1=A_[:, :T],
                                     op=ALU.mult)
                if mode == "f16":
                    vector.tensor_tensor(out=f16b[:, ci * T0:ci * T0 + T],
                                             in0=xn2[ci % 2][:, :T],
                                             in1=B_[:, :T], op=ALU.add)
                    incd(vector, "ve")
                    vector.tensor_copy(out=misc[:, 1023:1024],
                                           in_=misc[:, 1023:1024])
                    incd(vector, "ve")
                elif mode == "absmax":
                    vector.tensor_tensor(out=xn2[ci % 2][:, :T],
                                         in0=xn2[ci % 2][:, :T],
                                         in1=B_[:, :T], op=ALU.add)
                    vector.tensor_single_scalar(out=sqb[ci % 2][:, :T],
                                                in_=xn2[ci % 2][:, :T],
                                                scalar=-1.0, op=ALU.mult)
                    vector.tensor_tensor(out=misc[:, 0:T],
                                             in0=misc[:, 0:T],
                                             in1=xn2[ci % 2][:, :T],
                                             op=ALU.max)
                    incd(vector, "ve")
                    vector.tensor_tensor(out=misc[:, 0:T],
                                             in0=misc[:, 0:T],
                                             in1=sqb[ci % 2][:, :T],
                                             op=ALU.max)
                    incd(vector, "ve")
                    vector.tensor_copy(out=misc[:, 1023:1024],
                                           in_=misc[:, 1023:1024])
                    incd(vector, "ve")
                else:
                    vector.tensor_tensor(out=xn2[ci % 2][:, :T],
                                         in0=xn2[ci % 2][:, :T],
                                         in1=B_[:, :T], op=ALU.add)
                    vector.tensor_copy(out=f16b[:, ci * T0:ci * T0 + T],
                                       in_=xn2[ci % 2][:, :T])
                    k = vector.tensor_tensor(
                        out=f16big[:, ci * T0:ci * T0 + T],
                        in0=xn2[ci % 2][:, :T],
                        in1=f16b[:, ci * T0:ci * T0 + T], op=ALU.subtract)
                    inc(k, "ve")
                    vector.tensor_copy(out=misc[:, 1023:1024],
                                           in_=misc[:, 1023:1024])
                    incd(vector, "ve")
                    vector.tensor_copy(out=misc[:, 1023:1024],
                                           in_=misc[:, 1023:1024])
                    incd(vector, "ve")
                    if density:
                        vector.tensor_single_scalar(out=sqb[ci % 2][:, :T],
                                                    in_=xn2[ci % 2][:, :T],
                                                    scalar=0.1, op=ALU.is_gt)
                        vector.tensor_single_scalar(out=sqb[(ci + 1) % 2][:, :T],
                                                    in_=xn2[ci % 2][:, :T],
                                                    scalar=-0.1, op=ALU.is_lt)
                        vector.tensor_tensor(out=sqb[ci % 2][:, :T],
                                             in0=sqb[ci % 2][:, :T],
                                             in1=sqb[(ci + 1) % 2][:, :T],
                                             op=ALU.add)
                        vector.reduce_sum(out=sqb[ci % 2][:, :1],
                                          in_=sqb[ci % 2][:, :T], axis=AX.X)
                        vector.drain()
                        vector.tensor_tensor(out=dacc[:], in0=dacc[:],
                                                 in1=sqb[ci % 2][:, :1],
                                                 op=ALU.add)
                        incd(vector, "ve")
                        for _pad in range(4):
                            vector.tensor_copy(out=misc[:, 1023:1024],
                                                   in_=misc[:, 1023:1024])
                            incd(vector, "ve")
        assert cnt["ve"] == ve_b + HC * ops, (cnt["ve"], ve_b, ops)


    def stream_w(srcs, W, free_pred, pre_cb=None, hi_scalar=False):
        ldsn = snap("ld")
        lo_r = {}
        hi_sc = {}
        mm_guard = cnt["mm"]

        @blk.sync
        def _(sync):
            wall(sync, "sc", "ve")
            for j in range(len(srcs)):
                if j >= 2:
                    sync.wait_ge(sems["mm"], free_pred(j - 2))
                pdma(sync, f32w[:, (j % 2) * 2048:(j % 2) * 2048 + W],
                     srcs[j], "ld", j)

        if hi_scalar:
            @blk.scalar
            def _(scalar):
                scalar.wait_ge(sems["mm"], mm_guard)
                for j in range(len(srcs)):
                    pwait(scalar, "ld", ldsn, j)
                    if j >= 2:
                        scalar.wait_ge(sems["mm"], free_pred(j - 2))
                    scalar.copy(
                        out=f16w[:, (j % 2) * 4096:(j % 2) * 4096 + W],
                        in_=f32w[:, (j % 2) * 2048:(j % 2) * 2048 + W])
                    hi_sc[j] = incd(scalar, "sc")

        @blk.vector
        def _(vector):
            vector.wait_ge(sems["mm"], mm_guard)
            for j in range(len(srcs)):
                if pre_cb is not None:
                    pre_cb(vector, j)
                if hi_scalar:
                    vector.wait_ge(sems["sc"], hi_sc[j])
                else:
                    pwait(vector, "ld", ldsn, j)
                    if j >= 2:
                        vector.wait_ge(sems["mm"], free_pred(j - 2))
                    vector.tensor_copy(
                        out=f16w[:, (j % 2) * 4096:(j % 2) * 4096 + W],
                        in_=f32w[:, (j % 2) * 2048:(j % 2) * 2048 + W])
                vector.tensor_tensor(
                    out=f16w[:, (j % 2) * 4096 + 2048:
                             (j % 2) * 4096 + 2048 + W],
                    in0=f32w[:, (j % 2) * 2048:(j % 2) * 2048 + W],
                    in1=f16w[:, (j % 2) * 4096:(j % 2) * 4096 + W],
                    op=ALU.subtract)
                lo_r[j] = incd(vector, "ve")
            if pre_cb is not None:
                for j in range(len(srcs), len(srcs) + 8):
                    pre_cb(vector, j)
        return lo_r

    def whi(j, kk):
        b = (j % 2) * 4096
        return f16w[:, b + kk * P:b + (kk + 1) * P]

    def wlo(j, kk):
        b = (j % 2) * 4096 + 2048
        return f16w[:, b + kk * P:b + (kk + 1) * P]

    def emit_ffn0_tile(xin, xout, t0):
        if PARANOID & 1:
            emit_barrier()
        emit_ln(xin, t0, T0, "f16")
        if PARANOID & 2:
            emit_barrier()

        def x16(ci):
            return f16b[:, ci * T0:(ci + 1) * T0]

        def d16(h):
            return f16big[:, h * T0:(h + 1) * T0]

        if DEV_DUMP and t0 == 0:
            vehere = cnt["ve"]

            @blk.gpsimd
            def _(g):
                g.wait_ge(sems["ve"], vehere)
                inc(g.dma_start(out=dbgX[:], in_=f16b[:]), "so0", 16)

        mm_b = cnt["mm"]
        lo_r = stream_w([w0p[h] for h in range(FF0C)], HC * P,
                        lambda j: mm_b + (j + 1))
        sc_g = cnt["sc"]
        mm_h = {}

        @blk.tensor
        def _(tensor):
            for h in range(FF0C):
                tensor.wait_ge(sems["ve"], lo_r[h])
                if h >= 2:
                    tensor.wait_ge(sems["sc"], sc_g + (h - 1))
                pd = ps[4 + (h % 2)]
                i = None
                for pss, wf in ((0, whi), (1, wlo)):
                    for kk in range(HC):
                        i = tensor.matmul(pd[:, :T0], wf(h, kk), x16(kk),
                                          start=(pss == 0 and kk == 0),
                                          stop=(pss == 1 and kk == HC - 1))
                mm_h[h] = inc(i, "mm")

        @blk.scalar
        def _(scalar):
            for h in range(FF0C):
                scalar.wait_ge(sems["mm"], mm_h[h])
                scalar.activation(out=d16(h), in_=ps[4 + (h % 2)][:, :T0],
                                      func=AF.Gelu, bias=b0_sb[:, h:h + 1],
                                      scale=1.0)
                incd(scalar, "sc")

        if DEV_DUMP and t0 == 0:
            schere = cnt["sc"]

            @blk.gpsimd
            def _(g):
                g.wait_ge(sems["sc"], schere)
                inc(g.dma_start(out=dbgD[:], in_=f16big[:]), "so0", 16)

        if PARANOID & 4:
            emit_barrier()

        mm_b2 = cnt["mm"]
        g2sn2 = snap("g2")
        sosn2 = snap("so")
        evac_ve = {}
        qlist = [(e, qtr) for e in range(HC) for qtr in range(4)]

        def e_evac(vector, j):
            if j % 4 != 0:
                return
            e = j // 4 - 2
            if e < 0 or e >= HC:
                return
            vector.wait_ge(sems["mm"], mm_b2 + 4 * e + 4)
            n_, v_ = pval("g2", g2sn2, e)
            vector.wait_ge(sems[n_], v_)
            if e >= 2:
                n_, v_ = pval("so", sosn2, e - 2)
                vector.wait_ge(sems[n_], v_)
            e16 = f16b[:, (e % 2) * T0:(e % 2) * T0 + T0]
            vector.tensor_single_scalar(
                out=e16, in_=ps[6 if e % 2 == 0 else 0][:, :T0],
                scalar=b1_sb[:, e:e + 1], op=ALU.add)
            vector.scalar_tensor_tensor(
                out=xn2[e % 2][:, :T0], in0=e16, scalar=0.5,
                in1=xsb[e % 2][:, :T0], op0=ALU.mult, op1=ALU.add)
            evac_ve[e] = incd(vector, "ve")

        lo2 = stream_w(
            [w1p[e][:, qtr * 2048:(qtr + 1) * 2048] for e, qtr in qlist],
            HC * P, lambda j: mm_b2 + (j + 1), pre_cb=e_evac, hi_scalar=True)
        mm_e = {}

        @blk.tensor
        def _(tensor):
            for j, (e, qtr) in enumerate(qlist):
                tensor.wait_ge(sems["ve"], lo2[j])
                if qtr == 0 and e >= 2:
                    tensor.wait_ge(sems["ve"], evac_ve[e - 2])
                pd = ps[6 if e % 2 == 0 else 0]
                i = None
                for pss, wf in ((0, whi), (1, wlo)):
                    for kk in range(HC):
                        hh = qtr * HC + kk
                        i = tensor.matmul(pd[:, :T0], wf(j, kk), d16(hh),
                                          start=(qtr == 0 and pss == 0
                                                 and kk == 0),
                                          stop=(qtr == 3 and pss == 1
                                                and kk == HC - 1))
                mm_e[j] = inc(i, "mm")

        @blk.gpsimd
        def _(g):
            wall(g, "so0", "so1")
            for e in range(HC + 2):
                if e < HC:
                    if e >= 2:
                        g.wait_ge(sems["ve"], evac_ve[e - 2])
                    pdma(g, xsb[e % 2][:, :T0],
                         xin[e * P:(e + 1) * P, t0:t0 + T0], "g2", e)
                if e >= 2:
                    ee = e - 2
                    g.wait_ge(sems["ve"], evac_ve[ee])
                    pdma(g, xout[ee * P:(ee + 1) * P, t0:t0 + T0],
                         xn2[ee % 2][:, :T0], "so", ee)
            wall(g, "so0", "so1")

    def emit_proj(pack, dst_hi, dst_lo, t0, bias_sb=None, out_scale=None):
        if PARANOID & 8:
            emit_barrier()

        def xhi(ci):
            return f16b[:, ci * T0:(ci + 1) * T0]

        def xlo(ci):
            return f16big[:, ci * T0:(ci + 1) * T0]

        def sth(oc):
            b = 8192 + (oc % 2) * 512
            return f16big[:, b:b + T0]

        def stl(oc):
            b = 9216 + (oc % 2) * 512
            return f16big[:, b:b + T0]

        mm_b = cnt["mm"]
        sosn = snap("so")
        evac_ve = {}

        def p_evac(vector, j):
            oc = j - 2
            if oc < 0 or oc >= HC:
                return
            vector.wait_ge(sems["mm"], mm_b + oc + 1)
            if oc >= 2:
                jj = oc - 2
                vector.wait_ge(sems["so" + str(jj % 2)],
                               sosn[jj % 2] + 32 * (jj // 2 + 1))
            pd = ps[4 + (oc % 2)][:, :T0]
            src = pd
            if bias_sb is not None:
                vector.tensor_single_scalar(out=misc[:, :T0], in_=src,
                                            scalar=bias_sb[:, oc:oc + 1],
                                            op=ALU.add)
                src = misc[:, :T0]
            if out_scale is not None:
                vector.tensor_single_scalar(out=misc[:, :T0], in_=src,
                                            scalar=float(out_scale),
                                            op=ALU.mult)
                src = misc[:, :T0]
            vector.tensor_copy(out=sth(oc), in_=src)
            vector.tensor_tensor(
                out=stl(oc), in0=src, in1=sth(oc), op=ALU.subtract)
            evac_ve[oc] = incd(vector, "ve")

        lo_r = stream_w([pack[oc] for oc in range(HC)], HC * P,
                        lambda j: mm_b + (j + 1), pre_cb=p_evac,
                        hi_scalar=True)

        @blk.tensor
        def _(tensor):
            for oc in range(HC):
                tensor.wait_ge(sems["ve"], lo_r[oc])
                if oc >= 2:
                    tensor.wait_ge(sems["ve"], evac_ve[oc - 2])
                pd = ps[4 + (oc % 2)]
                i = None
                n = 0
                for wf, xf in ((whi, xhi), (whi, xlo), (wlo, xhi)):
                    for kk in range(HC):
                        i = tensor.matmul(pd[:, :T0], wf(oc, kk), xf(kk),
                                          start=(n == 0),
                                          stop=(n == 3 * HC - 1))
                        n += 1
                inc(i, "mm")

        @blk.gpsimd
        def _(g):
            for oc in range(HC):
                g.wait_ge(sems["ve"], evac_ve[oc])
                inc(g.dma_start(out=dst_hi[oc * P:(oc + 1) * P, t0:t0 + T0],
                                in_=sth(oc)), "so" + str(oc % 2), 16)
                inc(g.dma_start(out=dst_lo[oc * P:(oc + 1) * P, t0:t0 + T0],
                                in_=stl(oc)), "so" + str(oc % 2), 16)

    def emit_oproj(pack, t0, xin, xout, bias_sb=None):
        if PARANOID & 8:
            emit_barrier()
        g2at = cnt["g20"]

        @blk.gpsimd
        def _(g):
            wall(g, "ve", "mm", "so0", "so1")
            for ci in range(HC):
                inc(g.dma_start(out=f16big[:, ci * T0:(ci + 1) * T0],
                                in_=athi_d[ci * P:(ci + 1) * P, t0:t0 + T0]),
                    "g20", 16)
                inc(g.dma_start(
                    out=f16big[:, 16384 + ci * T0:16384 + (ci + 1) * T0],
                    in_=atlo_d[ci * P:(ci + 1) * P, t0:t0 + T0]), "g20", 16)

        def ahi(ci):
            return f16big[:, ci * T0:(ci + 1) * T0]

        def alo(ci):
            return f16big[:, 16384 + ci * T0:16384 + (ci + 1) * T0]

        mm_b = cnt["mm"]
        g2sn2 = snap("g2")
        sosn = snap("so")
        evac_ve = {}

        def o_evac(vector, j):
            oc = j - 2
            if oc < 0 or oc >= HC:
                return
            vector.wait_ge(sems["mm"], mm_b + oc + 1)
            n_, v_ = pval("g2", g2sn2, oc)
            vector.wait_ge(sems[n_], v_)
            if oc >= 2:
                n_, v_ = pval("so", sosn, oc - 2)
                vector.wait_ge(sems[n_], v_)
            pd = ps[4 + (oc % 2)][:, :T0]
            if bias_sb is not None:
                vector.tensor_single_scalar(out=misc[:, :T0], in_=pd,
                                            scalar=bias_sb[:, oc:oc + 1],
                                            op=ALU.add)
                vector.tensor_tensor(
                    out=xn2[oc % 2][:, :T0], in0=misc[:, :T0],
                    in1=xsb[oc % 2][:, :T0], op=ALU.add)
                evac_ve[oc] = incd(vector, "ve")
            else:
                vector.tensor_tensor(
                    out=xn2[oc % 2][:, :T0], in0=pd,
                    in1=xsb[oc % 2][:, :T0], op=ALU.add)
                evac_ve[oc] = incd(vector, "ve")

        lo_r = stream_w([pack[oc] for oc in range(HC)], HC * P,
                        lambda j: mm_b + (j + 1), pre_cb=o_evac,
                        hi_scalar=True)

        @blk.tensor
        def _(tensor):
            tensor.wait_ge(sems["g20"], g2at + 32 * HC)
            for oc in range(HC):
                tensor.wait_ge(sems["ve"], lo_r[oc])
                if oc >= 2:
                    tensor.wait_ge(sems["ve"], evac_ve[oc - 2])
                pd = ps[4 + (oc % 2)]
                i = None
                n = 0
                for wf, xf in ((whi, ahi), (whi, alo), (wlo, ahi)):
                    for kk in range(HC):
                        i = tensor.matmul(pd[:, :T0], wf(oc, kk), xf(kk),
                                          start=(n == 0),
                                          stop=(n == 3 * HC - 1))
                        n += 1
                inc(i, "mm")

        @blk.gpsimd
        def _(g):
            for oc in range(HC + 2):
                if oc < HC:
                    if oc >= 2:
                        g.wait_ge(sems["ve"], evac_ve[oc - 2])
                    pdma(g, xsb[oc % 2][:, :T0],
                         xin[oc * P:(oc + 1) * P, t0:t0 + T0], "g2", oc)
                if oc >= 2:
                    ee = oc - 2
                    g.wait_ge(sems["ve"], evac_ve[ee])
                    pdma(g, xout[ee * P:(ee + 1) * P, t0:t0 + T0],
                         xn2[ee % 2][:, :T0], "so", ee)
            wall(g, "so0", "so1")

    def emit_attn_core(layer, q0, q_tokens):
        khi = f16big[:, 0:2048]
        klo = f16big[:, 2048:4096]
        vhi = f16big[:, 4096:6144]
        vlo = f16big[:, 6144:8192]
        vthi = f16big[:, 8192:10240]
        vtlo = f16big[:, 10240:12288]
        pthi = f16big[:, 12288:20480]
        ptlo = f16big[:, 20480:28672]
        phi_ = f16big[:, 28672:30720]
        plo_ = f16big[:, 30720:32768]
        pt_hi_v = pthi.rearrange("p (j q) -> p j q", q=T0)
        pt_lo_v = ptlo.rearrange("p (j q) -> p j q", q=T0)
        vt_hi_v = vthi.rearrange("p (j d) -> p j d", d=P)
        vt_lo_v = vtlo.rearrange("p (j d) -> p j d", d=P)

        n_qg = q_tokens // T0
        for h in range(NHD):
            if PARANOID & 8:
                emit_barrier()
            ld_b = cnt["ld0"]

            @blk.sync
            def _(sync):
                wall(sync, "mm", "ve", "sc", "so0", "so1")
                for dst, src in ((khi, khi_d), (klo, klo_d),
                                 (vhi, vhi_d), (vlo, vlo_d)):
                    inc(sync.dma_start(out=dst,
                                       in_=src[h * P:(h + 1) * P, :]),
                        "ld0", 16)

            mm_b = cnt["mm"]
            ve_b = cnt["ve"]

            @blk.tensor
            def _(tensor):
                tensor.wait_ge(sems["ld0"], ld_b + 64)
                wall(tensor, "ve")
                for gi in range(4):
                    src = (vhi, vhi, vlo, vlo)[gi]
                    grp = gi % 2
                    if gi >= 1:
                        tensor.wait_ge(sems["ve"], ve_b + gi)
                    i = None
                    for j in range(8):
                        jj = grp * 8 + j
                        i = tensor.transpose(out=psT[:, j * P:(j + 1) * P],
                                             in_=src[:, jj * P:(jj + 1) * P],
                                             identity=id16[:])
                    inc(i, "mm")

            @blk.vector
            def _(vector):
                for gi in range(4):
                    dst = (vthi, vthi, vtlo, vtlo)[gi]
                    grp = gi % 2
                    vector.wait_ge(sems["mm"], mm_b + gi + 1)
                    vector.tensor_copy(
                        out=dst[:, grp * 1024:(grp + 1) * 1024],
                        in_=psT[:, 0:1024])
                    incd(vector, "ve")

            for qg in range(n_qg):
                qg0 = q0 + qg * T0
                for qt in range(4):
                    if PARANOID & 16:
                        emit_barrier()
                    qt0 = qg0 + qt * P
                    g2_b = cnt["g20"]

                    @blk.gpsimd
                    def _(g):
                        wall(g, "mm", "sc")
                        inc(g.dma_start(
                            out=f16b[:, 0:P],
                            in_=qhi_d[h * P:(h + 1) * P, qt0:qt0 + P]),
                            "g20", 16)
                        inc(g.dma_start(
                            out=f16b[:, P:2 * P],
                            in_=qlo_d[h * P:(h + 1) * P, qt0:qt0 + P]),
                            "g20", 16)

                    @blk.tensor
                    def _(tensor):
                        tensor.wait_ge(sems["g20"], g2_b + 32)
                        wall(tensor, "sc")
                        i = None
                        for pss, (qa, ka) in enumerate(
                                ((f16b[:, 0:P], khi),
                                 (f16b[:, P:2 * P], khi),
                                 (f16b[:, 0:P], klo))):
                            for jb in range(4):
                                i = tensor.matmul(
                                    ps[jb][:, :T0], qa,
                                    ka[:, jb * T0:(jb + 1) * T0],
                                    start=(pss == 0), stop=(pss == 2))
                        inc(i, "mm")

                    @blk.vector
                    def _(vector):
                        wall(vector, "mm")
                        if layer == 1:
                            mrow = (qt0 - q0) // P
                            mview = bf16a[:, mrow * 2048:(mrow + 1) * 2048]
                            for jb in range(4):
                                vector.tensor_tensor(
                                    out=ps[jb][:, :T0], in0=ps[jb][:, :T0],
                                    in1=mview[:, jb * T0:(jb + 1) * T0],
                                    op=ALU.add)
                        for jb in range(4):
                            vector.reduce_max(out=misc[:, jb:jb + 1],
                                              in_=ps[jb][:, :T0], axis=AX.X)
                        vector.drain()
                        vector.tensor_tensor(out=misc[:, 0:1],
                                             in0=misc[:, 0:1],
                                             in1=misc[:, 1:2], op=ALU.max)
                        vector.drain()
                        vector.tensor_tensor(out=misc[:, 2:3],
                                             in0=misc[:, 2:3],
                                             in1=misc[:, 3:4], op=ALU.max)
                        vector.drain()
                        vector.tensor_tensor(out=misc[:, 0:1],
                                             in0=misc[:, 0:1],
                                             in1=misc[:, 2:3], op=ALU.max)
                        vector.drain()
                        vector.tensor_single_scalar(
                            out=misc[:, 4:5], in_=misc[:, 0:1],
                            scalar=-1.0, op=ALU.mult)
                        incd(vector, "ve")

                    @blk.scalar
                    def _(scalar):
                        wall(scalar, "ve")
                        for jb in range(4):
                            scalar.activation(
                                out=pe_[:, jb * T0:(jb + 1) * T0],
                                in_=ps[jb][:, :T0], func=AF.Exp,
                                bias=misc[:, 4:5], scale=1.0,
                                accum_out=misc[:, 5 + jb:6 + jb])
                            incd(scalar, "sc")

                    @blk.vector
                    def _(vector):
                        wall(vector, "sc")
                        vector.tensor_tensor(out=misc[:, 9:10],
                                             in0=misc[:, 5:6],
                                             in1=misc[:, 6:7], op=ALU.add)
                        vector.tensor_tensor(out=misc[:, 10:11],
                                             in0=misc[:, 7:8],
                                             in1=misc[:, 8:9], op=ALU.add)
                        vector.drain()
                        vector.tensor_tensor(out=misc[:, 9:10],
                                             in0=misc[:, 9:10],
                                             in1=misc[:, 10:11], op=ALU.add)
                        vector.drain()
                        vector.reciprocal(out=misc[:, 10:11],
                                          in_=misc[:, 9:10])
                        vector.drain()
                        vector.tensor_tensor(out=misc[:, 11:12],
                                             in0=misc[:, 9:10],
                                             in1=misc[:, 10:11], op=ALU.mult)
                        vector.drain()
                        vector.tensor_scalar(out=misc[:, 11:12],
                                             in0=misc[:, 11:12], scalar1=-1.0,
                                             scalar2=2.0, op0=ALU.mult,
                                             op1=ALU.add)
                        vector.drain()
                        vector.tensor_tensor(out=misc[:, 10:11],
                                             in0=misc[:, 10:11],
                                             in1=misc[:, 11:12], op=ALU.mult)
                        vector.drain()
                        vector.tensor_single_scalar(out=pe_[:, :],
                                                    in_=pe_[:, :],
                                                    scalar=misc[:, 10:11],
                                                    op=ALU.mult)
                        vector.tensor_copy(out=phi_[:], in_=pe_[:])
                        vector.tensor_tensor(out=plo_[:], in0=pe_[:],
                                                 in1=phi_[:],
                                                 op=ALU.subtract)
                        incd(vector, "ve")

                    mm_b3 = cnt["mm"]
                    ve_b3 = cnt["ve"]

                    @blk.tensor
                    def _(tensor):
                        wall(tensor, "ve")
                        for gi in range(4):
                            src = (phi_, phi_, plo_, plo_)[gi]
                            grp = gi % 2
                            if gi >= 1:
                                tensor.wait_ge(sems["ve"], ve_b3 + gi)
                            i = None
                            for j in range(8):
                                jj = grp * 8 + j
                                i = tensor.transpose(
                                    out=psT[:, j * P:(j + 1) * P],
                                    in_=src[:, jj * P:(jj + 1) * P],
                                    identity=id16[:])
                            inc(i, "mm")

                    @blk.vector
                    def _(vector):
                        for gi in range(4):
                            dstv = (pt_hi_v, pt_hi_v, pt_lo_v, pt_lo_v)[gi]
                            grp = gi % 2
                            vector.wait_ge(sems["mm"], mm_b3 + gi + 1)
                            vector.tensor_copy(
                                out=dstv[:, grp * 8:(grp + 1) * 8,
                                         qt * P:(qt + 1) * P],
                                in_=psT[:, 0:1024].rearrange(
                                    "p (j q) -> p j q", q=P))
                            incd(vector, "ve")

                if PARANOID & 16:
                    emit_barrier()

                @blk.tensor
                def _(tensor):
                    wall(tensor, "ve")
                    pd = ps[4 + (qg % 2)]
                    i = None
                    n = 0
                    for vt, pt in ((vt_hi_v, pt_hi_v), (vt_hi_v, pt_lo_v),
                                   (vt_lo_v, pt_hi_v)):
                        for j in range(HC):
                            i = tensor.matmul(pd[:, :T0], vt[:, j],
                                              pt[:, j, :], start=(n == 0),
                                              stop=(n == 3 * HC - 1))
                            n += 1
                    inc(i, "mm")

                @blk.vector
                def _(vector):
                    wall(vector, "mm", "so0", "so1")
                    pd = ps[4 + (qg % 2)][:, :T0]
                    a16h = f16b[:, 1024 + (qg % 2) * 1024:
                                1024 + (qg % 2) * 1024 + T0]
                    a16l = f16b[:, 1536 + (qg % 2) * 1024:
                                1536 + (qg % 2) * 1024 + T0]
                    vector.tensor_copy(out=a16h, in_=pd)
                    vector.tensor_tensor(out=a16l, in0=pd, in1=a16h,
                                             op=ALU.subtract)
                    incd(vector, "ve")

                @blk.gpsimd
                def _(g):
                    wall(g, "ve")
                    a16h = f16b[:, 1024 + (qg % 2) * 1024:
                                1024 + (qg % 2) * 1024 + T0]
                    a16l = f16b[:, 1536 + (qg % 2) * 1024:
                                1536 + (qg % 2) * 1024 + T0]
                    inc(g.dma_start(
                        out=athi_d[h * P:(h + 1) * P, qg0:qg0 + T0],
                        in_=a16h), "so0", 16)
                    inc(g.dma_start(
                        out=atlo_d[h * P:(h + 1) * P, qg0:qg0 + T0],
                        in_=a16l), "so0", 16)

    NHD = 16

    def emit_mask_build(q0):
        col_to_row(dacc[:], rows[0:1, 0:P])

        @blk.vector
        def _(vector):
            wall(vector, "ve")
            vector.reduce_sum(out=rows[0:1, 128:129], in_=rows[0:1, 0:P],
                              axis=AX.X)
            vector.drain()
            vector.tensor_single_scalar(out=rows[0:1, 129:130],
                                            in_=rows[0:1, 128:129],
                                            scalar=float(S) * float(H) * 0.5,
                                            op=ALU.is_gt)
            incd(vector, "ve")

        bcast_row(rows[0:1, 129:130], 1, gcol[:, 0:1])

        ld_b = cnt["ld0"]
        ve_b = cnt["ve"]

        @blk.sync
        def _(sync):
            wall(sync, "ve")
            for r in range(8):
                if r >= 1:
                    sync.wait_ge(sems["ve"], ve_b + r)
                inc(sync.dma_start(out=bf16a[:, r * 2048:(r + 1) * 2048],
                                   in_=mss[r * P:(r + 1) * P, :]), "ld0", 16)
                inc(sync.dma_start(out=bf16st[:],
                                   in_=mdl[r * P:(r + 1) * P, :]), "ld0", 16)

        @blk.vector
        def _(vector):
            for r in range(8):
                vector.wait_ge(sems["ld0"], ld_b + 32 * (r + 1))
                vector.scalar_tensor_tensor(
                    out=bf16a[:, r * 2048:(r + 1) * 2048], in0=bf16st[:],
                    scalar=gcol[:, 0:1],
                    in1=bf16a[:, r * 2048:(r + 1) * 2048],
                    op0=ALU.mult, op1=ALU.add)
                incd(vector, "ve")

    def emit_wquant():
        for wi, (pack, tpack, n_oc, W) in enumerate(
                ((wup, tup, FF1C, 2048), (wdp, tdn, HC, 4096))):
            srcs = []
            dsts = []
            for oc in range(n_oc):
                if W == 2048:
                    srcs.append(pack[oc])
                    dsts.append(tpack[oc])
                else:
                    srcs.append(pack[oc][:, 0:2048])
                    srcs.append(pack[oc][:, 2048:4096])
                    dsts.append(tpack[oc][:, 0:2048])
                    dsts.append(tpack[oc][:, 2048:4096])
            ldsn = snap("ld")
            ve_b = cnt["ve"]

            @blk.sync
            def _(sync):
                wall(sync, "ve", "sc", "mm")
                for j in range(len(srcs)):
                    if j >= 2:
                        sync.wait_ge(sems["ve"], ve_b + (j - 1))
                    pdma(sync, f32w[:, (j % 2) * 2048:(j % 2) * 2048 + 2048],
                         srcs[j], "ld", j)

            @blk.vector
            def _(vector):
                vector.memset(misc[:, 512:513], 0.0)
                for j in range(len(srcs)):
                    pwait(vector, "ld", ldsn, j)
                    vector.reduce_sum(out=misc[:, 513:514],
                                      in_=f32w[:, (j % 2) * 2048:
                                               (j % 2) * 2048 + 2048],
                                      axis=AX.X, apply_absolute_value=True)
                    vector.drain()
                    vector.tensor_tensor(out=misc[:, 512:513],
                                             in0=misc[:, 512:513],
                                             in1=misc[:, 513:514],
                                             op=ALU.add)
                    incd(vector, "ve")

            col_to_row(misc[:, 512:513], rows[0:1, 0:P])

            @blk.vector
            def _(vector):
                wall(vector, "ve")
                vector.reduce_sum(out=rows[0:1, 128:129], in_=rows[0:1, 0:P],
                                  axis=AX.X)
                vector.drain()
                vector.tensor_single_scalar(out=rows[0:1, 128:129],
                                            in_=rows[0:1, 128:129],
                                            scalar=1.0 / (4096.0 * 2048.0),
                                            op=ALU.mult)
                vector.drain()
                vector.tensor_single_scalar(out=rows[0:1, 128:129],
                                                in_=rows[0:1, 128:129],
                                                scalar=1e-5, op=ALU.max)
                incd(vector, "ve")

            @blk.vector
            def _(vector):
                wall(vector, "ve")
                vector.reciprocal(out=rows[0:1, 129:130],
                                      in_=rows[0:1, 128:129])
                incd(vector, "ve")
                vector.tensor_tensor(out=rows[0:1, 130:131],
                                     in0=rows[0:1, 128:129],
                                     in1=rows[0:1, 129:130], op=ALU.mult)
                vector.drain()
                vector.tensor_scalar(out=rows[0:1, 130:131],
                                     in0=rows[0:1, 130:131], scalar1=-1.0,
                                     scalar2=2.0, op0=ALU.mult, op1=ALU.add)
                vector.drain()
                vector.tensor_tensor(out=rows[0:1, 129:130],
                                     in0=rows[0:1, 129:130],
                                     in1=rows[0:1, 130:131], op=ALU.mult)
                vector.drain()
                if wi == 0:
                    vector.tensor_single_scalar(out=rows[0:1, 131:132],
                                                in_=rows[0:1, 128:129],
                                                scalar=1.0 / 127.0,
                                                op=ALU.mult)
                    vector.drain()
                    vector.tensor_tensor(out=rows[0:1, 131:132],
                                             in0=rows[0:1, 131:132],
                                             in1=rows[0:1, 131:132],
                                             op=ALU.mult)
                    incd(vector, "ve")
                else:
                    vector.tensor_copy(out=rows[0:1, 131:132],
                                           in_=rows[0:1, 128:129])
                    incd(vector, "ve")

            if wi == 0:
                bcast_row(rows[0:1, 131:132], 1, qsc[:, 12:13])
                bcast_row(rows[0:1, 129:130], 1, qsc[:, 14:15])
            else:
                bcast_row(rows[0:1, 128:129], 1, qsc[:, 13:14])
                bcast_row(rows[0:1, 129:130], 1, qsc[:, 15:16])

            rcol = qsc[:, 14:15] if wi == 0 else qsc[:, 15:16]
            ldsn2 = snap("ld")
            ve_b2 = cnt["ve"]
            so_b2 = cnt["so0"]

            @blk.sync
            def _(sync):
                wall(sync, "ve")
                for j in range(len(srcs)):
                    if j >= 2:
                        sync.wait_ge(sems["ve"], ve_b2 + (j - 1))
                    pdma(sync, f32w[:, (j % 2) * 2048:(j % 2) * 2048 + 2048],
                         srcs[j], "ld", j)

            @blk.vector
            def _(vector):
                for j in range(len(srcs)):
                    pwait(vector, "ld", ldsn2, j)
                    if j >= 1:
                        vector.wait_ge(sems["so0"], so_b2 + 16 * j)
                    wv = f32w[:, (j % 2) * 2048:(j % 2) * 2048 + 2048]
                    vector.tensor_single_scalar(out=wv, in_=wv, scalar=rcol,
                                                op=ALU.mult)
                    vector.drain()
                    for hh in range(2):
                        seg = wv[:, hh * 1024:(hh + 1) * 1024]
                        sga = f32g[:, 1024:2048]
                        sgb = f32g[:, 3072:4096]
                        vector.tensor_single_scalar(out=sga, in_=seg,
                                                    scalar=0.5, op=ALU.is_ge)
                        vector.drain()
                        vector.tensor_single_scalar(out=sgb, in_=seg,
                                                    scalar=-0.5, op=ALU.is_le)
                        vector.drain()
                        vector.tensor_tensor(
                            out=bf16st[:, hh * 1024:(hh + 1) * 1024],
                            in0=sga, in1=sgb, op=ALU.subtract)
                        vector.drain()
                    vector.tensor_copy(out=misc[:, 512:513],
                                           in_=misc[:, 512:513])
                    incd(vector, "ve")

            @blk.gpsimd
            def _(g):
                for j in range(len(srcs)):
                    g.wait_ge(sems["ve"], ve_b2 + (j + 1))
                    inc(g.dma_start(out=dsts[j], in_=bf16st[:]), "so0", 16)

            @blk.sync
            def _(sync):
                wall(sync, "so0", "so1")

    def emit_ffn1_tile(xin, xout, t0):
        if PARANOID & 8:
            emit_barrier()
        emit_ln(xin, t0, T1, "absmax")
        macc = misc[:, 0:T1]
        AQ = f32g[:, 2048 + 256:2048 + 512]
        CQ = f32g[:, 2560 + 256:2560 + 512]

        def k8(ci):
            return bf16a[:, ci * T1:(ci + 1) * T1]

        def k82(oc):
            return bf16a[:, 4096 + oc * T1:4096 + (oc + 1) * T1]

        def quant_scales(cmode):
            for tb in range(T1 // P):
                @blk.tensor
                def _(tensor):
                    wall(tensor, "ve", "sc")
                    inc(tensor.transpose(out=ps[6][:, 0:P],
                                         in_=macc[:, tb * P:(tb + 1) * P],
                                         identity=id32[:]), "mm")

                @blk.vector
                def _(vector):
                    wall(vector, "mm")
                    vector.reduce_max(out=qsc[:, 0:1], in_=ps[6][:, 0:P],
                                      axis=AX.X)
                    vector.drain()
                    if cmode == "x":
                        vector.tensor_single_scalar(out=qsc[:, 1:2],
                                                        in_=qsc[:, 0:1],
                                                        scalar=1e-5,
                                                        op=ALU.max)
                        incd(vector, "ve")
                    else:
                        m1s = qsc[:, 2 + tb:3 + tb]
                        vector.tensor_tensor(out=qsc[:, 4:5], in0=m1s,
                                             in1=m1s, op=ALU.mult)
                        vector.drain()
                        vector.tensor_tensor(out=qsc[:, 4:5],
                                             in0=qsc[:, 4:5],
                                             in1=qsc[:, 12:13], op=ALU.mult)
                        vector.drain()
                        vector.tensor_tensor(out=qsc[:, 1:2],
                                             in0=qsc[:, 4:5],
                                             in1=qsc[:, 0:1], op=ALU.mult)
                        vector.drain()
                        vector.tensor_single_scalar(out=qsc[:, 1:2],
                                                        in_=qsc[:, 1:2],
                                                        scalar=1e-5,
                                                        op=ALU.max)
                        incd(vector, "ve")

                @blk.vector
                def _(vector):
                    wall(vector, "ve")
                    vector.reciprocal(out=qsc[:, 5:6], in_=qsc[:, 1:2])
                    vector.drain()
                    vector.tensor_tensor(out=qsc[:, 6:7], in0=qsc[:, 1:2],
                                         in1=qsc[:, 5:6], op=ALU.mult)
                    vector.drain()
                    vector.tensor_scalar(out=qsc[:, 6:7], in0=qsc[:, 6:7],
                                         scalar1=-1.0, scalar2=2.0,
                                         op0=ALU.mult, op1=ALU.add)
                    vector.drain()
                    vector.tensor_tensor(out=qsc[:, 5:6], in0=qsc[:, 5:6],
                                         in1=qsc[:, 6:7], op=ALU.mult)
                    vector.drain()
                    if cmode == "x":
                        vector.tensor_copy(out=qsc[:, 2 + tb:3 + tb],
                                           in_=qsc[:, 1:2])
                        vector.drain()
                        vector.tensor_single_scalar(
                            out=qsc[:, 7:8], in_=qsc[:, 5:6], scalar=127.0,
                            op=ALU.mult)
                        incd(vector, "ve")
                    else:
                        vector.tensor_tensor(out=qsc[:, 7:8],
                                             in0=qsc[:, 4:5],
                                             in1=qsc[:, 5:6], op=ALU.mult)
                        vector.drain()
                        vector.tensor_single_scalar(out=qsc[:, 7:8],
                                                    in_=qsc[:, 7:8],
                                                    scalar=127.0,
                                                    op=ALU.mult)
                        vector.drain()
                        vector.tensor_tensor(out=qsc[:, 8:9],
                                             in0=qsc[:, 1:2],
                                             in1=qsc[:, 13:14], op=ALU.mult)
                        vector.drain()
                        vector.tensor_single_scalar(
                            out=qsc[:, 8:9], in_=qsc[:, 8:9],
                            scalar=0.5 / 127.0, op=ALU.mult)
                        incd(vector, "ve")

                col_to_row(qsc[:, 7:8], rows[0:1, 512 + tb * P:512 + (tb + 1) * P])
                if cmode == "h":
                    col_to_row(qsc[:, 8:9],
                               rows2[0:1, tb * P:(tb + 1) * P])
            bcast_row(rows[0:1, 512:512 + T1], T1, AQ)
            if cmode == "h":
                bcast_row(rows2[0:1, 0:T1], T1, CQ)

        def peel_inline(vector, src_ap, dst_ap):
            vector.tensor_tensor(out=sqb[0][:, :T1], in0=src_ap,
                                 in1=AQ, op=ALU.mult)
            vector.tensor_scalar(out=sqb[0][:, :T1], in0=sqb[0][:, :T1],
                                 scalar1=127.25, scalar2=-127.25,
                                 op0=ALU.min, op1=ALU.max)
            vector.tensor_copy(out=sqb[1][:, :T1], in_=sqb[0][:, :T1])
            for b in range(7, -1, -1):
                thr = float(2 ** b)
                vector.tensor_single_scalar(out=misc[:, 256:256 + T1],
                                            in_=sqb[1][:, :T1],
                                            scalar=thr - 128.5,
                                            op=ALU.is_ge)
                vector.scalar_tensor_tensor(out=sqb[1][:, :T1],
                                            in0=misc[:, 256:256 + T1],
                                            scalar=-thr, in1=sqb[1][:, :T1],
                                            op0=ALU.mult, op1=ALU.add)
            vector.scalar_tensor_tensor(out=sqb[1][:, :T1],
                                        in0=sqb[1][:, :T1], scalar=-1.0,
                                        in1=sqb[0][:, :T1],
                                        op0=ALU.mult, op1=ALU.add)
            vector.tensor_single_scalar(out=dst_ap, in_=sqb[1][:, :T1],
                                            scalar=-128.0, op=ALU.add)
            incd(vector, "ve")

        if PARANOID & 8:
            emit_barrier()
        quant_scales("x")
        g2sn = snap("g2")
        ve_b = cnt["ve"]

        @blk.gpsimd
        def _(g):
            wall(g, "ve")
            for ci in range(HC):
                if ci >= 2:
                    g.wait_ge(sems["ve"], ve_b + 2 * (ci - 1))
                pdma(g, xsb[ci % 2][:, :T1],
                     xin[ci * P:(ci + 1) * P, t0:t0 + T1], "g2", ci)

        @blk.vector
        def _(vector):
            wall(vector, "mm")
            for ci in range(HC):
                pwait(vector, "g2", g2sn, ci)
                vector.tensor_tensor(out=xn2[0][:, :T1],
                                     in0=xsb[ci % 2][:, :T1],
                                     in1=A_[:, :T1], op=ALU.mult)
                vector.tensor_tensor(out=xn2[0][:, :T1],
                                         in0=xn2[0][:, :T1],
                                         in1=B_[:, :T1], op=ALU.add)
                incd(vector, "ve")
                peel_inline(vector, xn2[0][:, :T1], k8(ci))
        k8_done = cnt["ve"]

        if PARANOID & 8:
            emit_barrier()
        sc_b = cnt["sc"]
        ve_b1 = cnt["ve"]
        sosn1 = snap("so")
        mm_1 = {}
        mm_1b = cnt["mm"]

        def mm_1_pred(oc):
            return mm_1b + oc + 1

        ldsn1 = snap("ld")

        @blk.sync
        def _(sync):
            wall(sync, "ve")
            for oc in range(FF1C):
                if oc >= 2:
                    sync.wait_ge(sems["mm"], mm_1_pred(oc - 2))
                pdma(sync, bf16a[:, 12288 + (oc % 2) * 2048:
                                 12288 + (oc % 2) * 2048 + 2048],
                     tup[oc], "ld", oc)

        @blk.tensor
        def _(tensor):
            tensor.wait_ge(sems["ve"], k8_done)
            for oc in range(FF1C):
                pwait(tensor, "ld", ldsn1, oc)
                if oc >= 2:
                    tensor.wait_ge(sems["sc"], sc_b + (oc - 1))
                pd = ps[4 + (oc % 2)]
                i = None
                for kk in range(HC):
                    tb = bf16a[:, 12288 + (oc % 2) * 2048 + kk * P:
                               12288 + (oc % 2) * 2048 + (kk + 1) * P]
                    i = tensor.matmul(pd[:, :T1], tb, k8(kk),
                                      start=(kk == 0), stop=(kk == HC - 1))
                mm_1[oc] = inc(i, "mm")

        @blk.vector
        def _(vector):
            vector.memset(macc, 0.0)
            vector.tensor_copy(out=misc[:, 1023:1024],
                                   in_=misc[:, 1023:1024])
            incd(vector, "ve")

        @blk.scalar
        def _(scalar):
            for oc in range(FF1C):
                scalar.wait_ge(sems["mm"], mm_1[oc])
                if oc >= 2:
                    scalar.wait_ge(sems["ve"], ve_b1 + 1 + (oc - 1))
                    pwait(scalar, "so", sosn1, oc - 2)
                scalar.activation(out=misc[:, 512:512 + T1],
                                  in_=ps[4 + (oc % 2)][:, :T1], func=AF.Relu)
                scalar.activation(out=xn2[oc % 2][:, :T1],
                                      in_=misc[:, 512:512 + T1],
                                      func=AF.Square)
                incd(scalar, "sc")

        @blk.vector
        def _(vector):
            for oc in range(FF1C):
                vector.wait_ge(sems["sc"], sc_b + (oc + 1))
                vector.tensor_tensor(out=macc, in0=macc,
                                         in1=xn2[oc % 2][:, :T1],
                                         op=ALU.max)
                incd(vector, "ve")

        @blk.gpsimd
        def _(g):
            for oc in range(FF1C):
                g.wait_ge(sems["ve"], ve_b1 + 1 + (oc + 1))
                pdma(g, hscr[oc * P:(oc + 1) * P, t0:t0 + T1],
                     xn2[oc % 2][:, :T1], "so", oc)

        if PARANOID & 8:
            emit_barrier()
        quant_scales("h")
        g2sn2 = snap("g2")
        ve_b2 = cnt["ve"]

        def k2_pred(oc):
            return ve_b2 + oc + 1

        @blk.gpsimd
        def _(g):
            wall(g, "ve", "so0", "so1")
            for oc in range(FF1C):
                if oc >= 2:
                    g.wait_ge(sems["ve"], k2_pred(oc - 2))
                pdma(g, xsb[oc % 2][:, :T1],
                     hscr[oc * P:(oc + 1) * P, t0:t0 + T1], "g2", oc)

        @blk.vector
        def _(vector):
            wall(vector, "mm")
            for oc in range(FF1C):
                pwait(vector, "g2", g2sn2, oc)
                peel_inline(vector, xsb[oc % 2][:, :T1], k82(oc))
        k2_done = cnt["ve"]

        if DEV_DUMP:
            @blk.gpsimd
            def _(g):
                wall(g, "ve")
                for oc in range(FF1C):
                    inc(g.dma_start(out=dbg1[oc * P:(oc + 1) * P, t0:t0 + T1],
                                    in_=k82(oc)), "so0", 16)
                inc(g.dma_start(out=dbg2[:, 0:T1], in_=AQ), "so0", 16)
                inc(g.dma_start(out=dbg2[:, 256:256 + T1], in_=CQ), "so0", 16)
                wall(g, "so0")

        if PARANOID & 8:
            emit_barrier()
        ldsn2 = snap("ld")
        mm_2 = {}
        mm_2b = cnt["mm"]

        def mm_2_pred(j):
            return mm_2b + j + 1

        @blk.sync
        def _(sync):
            wall(sync, "ve")
            jj = 0
            for e in range(HC):
                for hf in range(2):
                    if jj >= 2:
                        sync.wait_ge(sems["mm"], mm_2_pred(jj - 2))
                    pdma(sync, bf16a[:, 12288 + (jj % 2) * 2048:
                                     12288 + (jj % 2) * 2048 + 2048],
                         tdn[e][:, hf * 2048:(hf + 1) * 2048], "ld", jj)
                    jj += 1

        ve_b3 = cnt["ve"]
        g2sn3 = snap("g2")
        sosn3 = snap("so")

        @blk.tensor
        def _(tensor):
            tensor.wait_ge(sems["ve"], k2_done)
            jj = 0
            for e in range(HC):
                if e >= 2:
                    tensor.wait_ge(sems["ve"], ve_b3 + (e - 1))
                for hf in range(2):
                    pwait(tensor, "ld", ldsn2, jj)
                    i = None
                    for kk in range(HC):
                        oc = hf * HC + kk
                        tb = bf16a[:, 12288 + (jj % 2) * 2048 + kk * P:
                                   12288 + (jj % 2) * 2048 + (kk + 1) * P]
                        i = tensor.matmul(ps[4 + (e % 2)][:, :T1], tb,
                                          k82(oc),
                                          start=(hf == 0 and kk == 0),
                                          stop=(hf == 1 and kk == HC - 1))
                    mm_2[jj] = inc(i, "mm")
                    jj += 1

        @blk.gpsimd
        def _(g):
            g.wait_ge(sems["ve"], k2_done)
            for e in range(HC + 2):
                if e < HC:
                    if e >= 2:
                        g.wait_ge(sems["ve"], ve_b3 + (e - 1))
                    pdma(g, sqb[e % 2][:, :T1],
                         xin[e * P:(e + 1) * P, t0:t0 + T1], "g2", e)
                if e >= 2:
                    ee = e - 2
                    g.wait_ge(sems["ve"], ve_b3 + (ee + 1))
                    pdma(g, xout[ee * P:(ee + 1) * P, t0:t0 + T1],
                         xn2[ee % 2][:, :T1], "so", ee)

        @blk.vector
        def _(vector):
            for e in range(HC):
                vector.wait_ge(sems["mm"], mm_2[2 * e + 1])
                pwait(vector, "g2", g2sn3, e)
                if e >= 2:
                    pwait(vector, "so", sosn3, e - 2)
                vector.tensor_tensor(out=misc[:, 512:512 + T1],
                                     in0=ps[4 + (e % 2)][:, :T1],
                                     in1=CQ, op=ALU.mult)
                vector.tensor_tensor(out=xn2[e % 2][:, :T1],
                                         in0=misc[:, 512:512 + T1],
                                         in1=sqb[e % 2][:, :T1],
                                         op=ALU.add)
                incd(vector, "ve")

        @blk.gpsimd
        def _(g):
            for ee in (HC - 2, HC - 1):
                g.wait_ge(sems["ve"], ve_b3 + (ee + 1))
                pdma(g, xout[ee * P:(ee + 1) * P, t0:t0 + T1],
                     xn2[ee % 2][:, :T1], "so", ee)
            wall(g, "so0", "so1")

    def emit_output(xcur, col0):
        ldsn = snap("ld")
        sosn = snap("so")
        if DEV_ONE_TILE:
            jlist = [(ci, 0) for ci in range(HC)]
        else:
            jlist = [(j // 2, j % 2) for j in range(2 * HC)]

        @blk.sync
        def _(sync):
            wall(sync, "ve", "sc", "mm", "so0", "so1", "g20", "g21")
            for j, (ci, half) in enumerate(jlist):
                if j >= 2:
                    pwait(sync, "so", sosn, j - 2)
                pdma(sync, xsb[j % 2][:, :T0],
                     xcur[ci * P:(ci + 1) * P,
                          col0 + half * T0:col0 + (half + 1) * T0], "ld", j)

        @blk.gpsimd
        def _(g):
            for j, (ci, half) in enumerate(jlist):
                pwait(g, "ld", ldsn, j)
                pdma(g, y[ci * P:(ci + 1) * P, half * T0:(half + 1) * T0],
                     xsb[j % 2][:, :T0], "so", j)
            wall(g, "so0", "so1")

        @blk.sync
        def _(sync):
            wall(sync, "so0", "so1")

    half_off = None
    emit_wquant()

    seq = []
    if n_sub >= 1:
        seq.append(("ffn0", x0, xa))
    if n_sub >= 2:
        seq.append(("attn0", xa, xb))
    if n_sub >= 3:
        seq.append(("ffn0", xb, xa))
    if n_sub >= 4:
        seq.append(("ffn1", xa, xb))
    if n_sub >= 5:
        seq.append(("attn1", xb, xa))
    if n_sub >= 6:
        seq.append(("ffn1_own", xa, xb))

    OWN0 = None

    _lim = (lambda stop, step: [0] if DEV_ONE_TILE else
            list(range(0, stop, step)))
    for name, xi, xo in seq:
        if name == "ffn0":
            for t0 in _lim(S, T0):
                emit_ffn0_tile(xi, xo, t0)
        elif name == "ffn1":
            for t0 in _lim(S, T1):
                emit_ffn1_tile(xi, xo, t0)
        elif name == "ffn1_own":
            for t0 in range(OWN_OFF, OWN_OFF + OWN, T1):
                emit_ffn1_tile(xi, xo, t0)
        elif name == "attn0":
            for t0 in range(0, S, T0):
                emit_ln(xi, t0, T0, "hilo")
                emit_proj(wqp[0], qhi_d, qlo_d, t0, out_scale=0.08838834764831845)
                emit_proj(wkp[0], khi_d, klo_d, t0)
                emit_proj(wvp[0], vhi_d, vlo_d, t0)
            emit_attn_core(0, 0, S)
            for t0 in range(0, S, T0):
                emit_oproj(wop[0], t0, xi, xo)
        elif name == "attn1":
            for t0 in range(0, S, T0):
                emit_ln(xi, t0, T0, "hilo", density=True)
                emit_proj(wkp[1], khi_d, klo_d, t0, bias_sb=bk_sb)
                emit_proj(wvp[1], vhi_d, vlo_d, t0, bias_sb=bv_sb)
                if OWN_OFF <= t0 < OWN_OFF + OWN:
                    emit_proj(wqp[1], qhi_d, qlo_d, t0, bias_sb=bq_sb, out_scale=0.08838834764831845)
            emit_mask_build(OWN_OFF)
            emit_attn_core(1, OWN_OFF, OWN)
            for t0 in range(OWN_OFF, OWN_OFF + OWN, T0):
                emit_oproj(wop[1], t0, xi, xo, bias_sb=bo_sb)

    xfinal = seq[-1][2] if seq else x0
    emit_output(xfinal, OWN_OFF if n_sub >= 5 else OWN_OFF)

    es.close()
    return nc


OWN_OFF = 0


def _host_pack(W):
    WT = np.ascontiguousarray(W.T.astype(np.float32))
    KC = WT.shape[0] // P
    OC = WT.shape[1] // P
    return np.ascontiguousarray(
        WT.reshape(KC, P, OC, P).transpose(2, 1, 0, 3).reshape(OC, P, KC * P))


def _bias_cols(b):
    n = b.shape[0] // P
    return np.ascontiguousarray(b.reshape(n, P).T.astype(np.float32))


def kernel(**inputs):
    import ml_dtypes

    x = np.asarray(inputs["L0_attn_hidden_states"], np.float32)
    w0p = _host_pack(np.asarray(inputs["L0_ffn_W0"]))
    w1p = _host_pack(np.asarray(inputs["L0_ffn_W1"]))
    b0v = _bias_cols(np.asarray(inputs["L0_ffn_bias0"]))
    b1v = _bias_cols(np.asarray(inputs["L0_ffn_bias1"]))
    packs = {}
    for L in (0, 1):
        pre = "L%d_attn_" % L
        packs[("q", L)] = _host_pack(np.asarray(inputs[pre + "Wq"]))
        packs[("k", L)] = _host_pack(np.asarray(inputs[pre + "Wk"]))
        packs[("v", L)] = _host_pack(np.asarray(inputs[pre + "Wv"]))
        packs[("o", L)] = _host_pack(np.asarray(inputs[pre + "Wo"]))
    bqv = _bias_cols(np.asarray(inputs["L1_attn_bq"]))
    bkv = _bias_cols(np.asarray(inputs["L1_attn_bk"]))
    bvv = _bias_cols(np.asarray(inputs["L1_attn_bv"]))
    bov = _bias_cols(np.asarray(inputs["L1_attn_bo"]))
    wup = _host_pack(np.asarray(inputs["L1_ffn_w_up"]))
    wdp = _host_pack(np.asarray(inputs["L1_ffn_w_down"]))

    idn16 = np.eye(P, dtype=np.float16)
    idn32 = np.eye(P, dtype=np.float32)

    j = np.arange(S)
    nc = _build(N_SUBLAYERS)

    in_maps = []
    for c in range(8):
        b, half = c // 2, c % 2
        perm = np.concatenate([np.arange(half * OWN, half * OWN + OWN),
                               np.arange((1 - half) * OWN,
                                         (1 - half) * OWN + OWN)])
        x0 = np.ascontiguousarray(x[b][perm].T)
        gi = perm[:OWN]
        di = gi[:, None] - perm[None, :]
        local = np.abs(di) <= 64
        strided = (di % 4) == 0
        glob = (gi[:, None] == 0) | (perm[None, :] == 0)
        msl = np.where(local | glob, 0.0, NEGBIG).astype(np.float32)
        mst = np.where(strided | glob, 0.0, NEGBIG).astype(np.float32)
        mdlv = (msl - mst).astype(ml_dtypes.bfloat16)
        mssv = mst.astype(ml_dtypes.bfloat16)
        im = dict(x0=x0, w0p=w0p, w1p=w1p, b0v=b0v, b1v=b1v,
                  wqp0=packs[("q", 0)], wqp1=packs[("q", 1)],
                  wkp0=packs[("k", 0)], wkp1=packs[("k", 1)],
                  wvp0=packs[("v", 0)], wvp1=packs[("v", 1)],
                  wop0=packs[("o", 0)], wop1=packs[("o", 1)],
                  bqv=bqv, bkv=bkv, bvv=bvv, bov=bov,
                  wup=wup, wdp=wdp, mss=mssv, mdl=mdlv,
                  idn16=idn16, idn32=idn32)
        in_maps.append(im)

    res = run_bass_kernel_spmd(nc, in_maps, core_ids=list(range(8)))
    out = np.empty((4, S, H), np.float32)
    for c in range(8):
        b, half = c // 2, c % 2
        out[b, half * OWN:(half + 1) * OWN, :] = res.results[c]["y"].T
    return out


# version 05
# speedup vs baseline: 1.0028x; 1.0028x over previous
import numpy as np
from contextlib import ExitStack

import concourse.bass as bass
import concourse.mybir as mybir
from concourse.bass_utils import run_bass_kernel_spmd

F32 = mybir.dt.float32
F16 = mybir.dt.float16
BF16 = mybir.dt.bfloat16
AF = mybir.ActivationFunctionType
ALU = mybir.AluOpType
AX = mybir.AxisListType

P = 128
S = 2048
H = 2048
HC = 16
FF0C = 64
FF1C = 32
FF1 = 4096
OWN = 1024
LN_EPS = 1e-5
INV_H = 1.0 / H
T0 = 512
T1 = 256
NEGBIG = -1.0e38

N_SUBLAYERS = 6
DEV_ONE_TILE = False
DEV_DUMP = False
PARANOID = 0x6


def _build(n_sub=N_SUBLAYERS):
    nc = bass.Bass("TRN2", target_bir_lowering=False,
                   detect_race_conditions=False)
    es = ExitStack()
    sems, cnt = {}, {}

    def mksem(name):
        sems[name] = es.enter_context(nc.semaphore(name))
        cnt[name] = 0

    def inc(inst, name, k=1):
        inst.then_inc(sems[name], k)
        cnt[name] += k
        return cnt[name]

    def incd(eng, name, k=1):
        d = eng.drain()
        d.then_inc(sems[name], k)
        cnt[name] += k
        return cnt[name]

    def wall(eng, *names):
        for n in names:
            eng.wait_ge(sems[n], cnt[n])

    def din(name, shape, dt=F32):
        return nc.dram_tensor(name, list(shape), dt, kind="ExternalInput").ap()

    def dint(name, shape, dt=F32):
        return nc.dram_tensor(name, list(shape), dt, kind="Internal").ap()

    x0 = din("x0", [H, S])
    w0p = din("w0p", [FF0C, P, HC * P])
    w1p = din("w1p", [HC, P, FF0C * P])
    b0v = din("b0v", [P, FF0C])
    b1v = din("b1v", [P, HC])
    wqp = [din("wqp0", [HC, P, HC * P]), din("wqp1", [HC, P, HC * P])]
    wkp = [din("wkp0", [HC, P, HC * P]), din("wkp1", [HC, P, HC * P])]
    wvp = [din("wvp0", [HC, P, HC * P]), din("wvp1", [HC, P, HC * P])]
    wop = [din("wop0", [HC, P, HC * P]), din("wop1", [HC, P, HC * P])]
    bqv = din("bqv", [P, HC])
    bkv = din("bkv", [P, HC])
    bvv = din("bvv", [P, HC])
    bov = din("bov", [P, HC])
    wup = din("wup", [FF1C, P, HC * P])
    wdp = din("wdp", [HC, P, FF1C * P])
    mss = din("mss", [OWN, S], BF16)
    mdl = din("mdl", [OWN, S], BF16)
    idn16 = din("idn16", [P, P], F16)
    idn32 = din("idn32", [P, P], F32)

    y = nc.dram_tensor("y", [H, OWN], F32, kind="ExternalOutput").ap()

    xa = dint("xa", [H, S])
    xb = dint("xb", [H, S])
    qhi_d = dint("q_hi", [H, S], F16)
    qlo_d = dint("q_lo", [H, S], F16)
    khi_d = dint("k_hi", [H, S], F16)
    klo_d = dint("k_lo", [H, S], F16)
    vhi_d = dint("v_hi", [H, S], F16)
    vlo_d = dint("v_lo", [H, S], F16)
    athi_d = dint("at_hi", [H, S], F16)
    atlo_d = dint("at_lo", [H, S], F16)
    tup = dint("tup", [FF1C, P, HC * P], BF16)
    tdn = dint("tdn", [HC, P, FF1C * P], BF16)
    hscr = dint("hscr", [FF1, S])
    _dd = (lambda n, sh, dt=F32: nc.dram_tensor(n, list(sh), dt,
           kind="ExternalOutput").ap()) if DEV_DUMP else dint
    dbg1 = _dd("dbg1", [FF1, S], BF16)
    dbg2 = _dd("dbg2", [P, 2048])
    dbgX = _dd("dbgX", [P, 8192], F16)
    dbgD = _dd("dbgD", [P, 32768], F16)

    def sbt(name, shape, dt=F32):
        return es.enter_context(nc.sbuf_tensor(name, list(shape), dt))

    b0_sb = sbt("b0_sb", [P, FF0C])
    b1_sb = sbt("b1_sb", [P, HC])
    bq_sb = sbt("bq_sb", [P, HC])
    bk_sb = sbt("bk_sb", [P, HC])
    bv_sb = sbt("bv_sb", [P, HC])
    bo_sb = sbt("bo_sb", [P, HC])
    id16 = sbt("id16", [P, P], F16)
    id32 = sbt("id32", [P, P], F32)
    onescol = sbt("onescol", [1, P])
    ones128 = sbt("ones128", [P, 1])
    gcol = sbt("gcol", [P, 1])
    dacc = sbt("dacc", [P, 1])
    qsc = sbt("qsc", [P, 16])

    f16big = sbt("f16big", [P, 32768], F16)
    f16b = sbt("f16b", [P, 8192], F16)
    bf16a = sbt("bf16a", [P, 16384], BF16)
    bf16st = sbt("bf16st", [P, 2048], BF16)
    f32w = sbt("f32w", [P, 4096])
    f16w = sbt("f16w", [P, 8192], F16)
    f32g = sbt("f32g", [P, 7168])
    rows = sbt("rows", [1, 1024])
    rows2 = sbt("rows2", [1, 1024])

    ps = [es.enter_context(nc.psum_tensor(f"ps{i}", [P, 512], F32))
          for i in range(7)]
    psT = es.enter_context(nc.psum_tensor("psT", [P, 1024], F16))

    for s_ in ("ld0", "ld1", "g20", "g21", "mm", "sc", "ve", "so0", "so1"):
        mksem(s_)

    def snap(base):
        return (cnt[base + "0"], cnt[base + "1"])

    def pdma(eng, out, in_, base, j):
        return inc(eng.dma_start(out=out, in_=in_), base + str(j % 2), 16)

    def pval(base, sn, j):
        return base + str(j % 2), sn[j % 2] + 16 * (j // 2 + 1)

    def pwait(eng, base, sn, j):
        n, v = pval(base, sn, j)
        eng.wait_ge(sems[n], v)

    blk = es.enter_context(nc.Block())

    xsb = [f32g[:, 0:512], f32g[:, 512:1024]]
    sqb = [f32g[:, 1024:1536], f32g[:, 1536:2048]]
    A_ = f32g[:, 2048:2560]
    B_ = f32g[:, 2560:3072]
    xn2 = [f32g[:, 3072:3584], f32g[:, 3584:4096]]
    pe_ = f32g[:, 4096:6144]
    misc = f32g[:, 6144:7168]

    @blk.sync
    def _(sync):
        for dst, src in ((b0_sb, b0v), (b1_sb, b1v), (bq_sb, bqv),
                         (bk_sb, bkv), (bv_sb, bvv), (bo_sb, bov),
                         (id16, idn16), (id32, idn32)):
            inc(sync.dma_start(out=dst[:], in_=src[:]), "ld0", 16)

    @blk.vector
    def _(vector):
        wall(vector, "ld0", "ld1")
        vector.memset(onescol[:], 1.0)
        vector.memset(ones128[:], 1.0)
        vector.memset(misc[:, 1016:1024], 0.0)
        vector.memset(dacc[:], 0.0)
        incd(vector, "ve")

    def emit_barrier():
        for sec in (blk.sync, blk.gpsimd, blk.scalar, blk.vector, blk.tensor):
            @sec
            def _(eng):
                wall(eng, "ld0", "ld1", "g20", "g21", "so0", "so1",
                     "mm", "sc", "ve")

    def bcast_row(row_ap, width, dst_ap):
        @blk.tensor
        def _(tensor):
            wall(tensor, "ve", "sc")
            inc(tensor.matmul(ps[6][:, :width], onescol[:, :], row_ap,
                              start=True, stop=True), "mm")

        @blk.vector
        def _(vector):
            wall(vector, "mm")
            vector.tensor_copy(out=dst_ap, in_=ps[6][:, :width])
            incd(vector, "ve")

    def col_to_row(col_ap, row_dst_ap):
        @blk.tensor
        def _(tensor):
            wall(tensor, "ve", "sc")
            inc(tensor.transpose(out=ps[6][0:1, 0:P], in_=col_ap,
                                 identity=id32[:]), "mm")

        @blk.vector
        def _(vector):
            wall(vector, "mm")
            vector.tensor_copy(out=row_dst_ap, in_=ps[6][0:1, 0:P])
            incd(vector, "ve")

    def emit_ln(xin, t0, T, mode, density=False):
        ldsn, sc_b, mm_b = snap("ld"), cnt["sc"], cnt["mm"]

        @blk.sync
        def _(sync):
            wall(sync, "ve", "sc", "mm")
            for ci in range(HC):
                if ci >= 2:
                    sync.wait_ge(sems["mm"], mm_b + (ci - 1))
                pdma(sync, xsb[ci % 2][:, :T],
                     xin[ci * P:(ci + 1) * P, t0:t0 + T], "ld", ci)

        @blk.scalar
        def _(scalar):
            for ci in range(HC):
                pwait(scalar, "ld", ldsn, ci)
                if ci >= 2:
                    scalar.wait_ge(sems["mm"], mm_b + (ci - 1))
                scalar.activation(out=sqb[ci % 2][:, :T],
                                      in_=xsb[ci % 2][:, :T], func=AF.Square)
                incd(scalar, "sc")

        @blk.tensor
        def _(tensor):
            for ci in range(HC):
                tensor.wait_ge(sems["sc"], sc_b + ci + 1)
                tensor.matmul(ps[0][0:1, :T], ones128[:, :1],
                              xsb[ci % 2][:, :T],
                              start=(ci == 0), stop=(ci == HC - 1))
                inc(tensor.matmul(ps[1][0:1, :T], ones128[:, :1],
                                  sqb[ci % 2][:, :T],
                                  start=(ci == 0), stop=(ci == HC - 1)), "mm")

        @blk.vector
        def _(vector):
            wall(vector, "mm")
            r, r2 = rows[0:1, :], rows2[0:1, :]
            vector.tensor_single_scalar(out=r2[:, 0:T], in_=ps[0][0:1, :T],
                                        scalar=INV_H, op=ALU.mult)
            vector.tensor_single_scalar(out=r[:, 0:T], in_=ps[1][0:1, :T],
                                        scalar=INV_H, op=ALU.mult)
            vector.tensor_tensor(out=r2[:, 512:512 + T], in0=r2[:, 0:T],
                                 in1=r2[:, 0:T], op=ALU.mult)
            vector.tensor_tensor(out=r[:, 512:512 + T], in0=r[:, 0:T],
                                 in1=r2[:, 512:512 + T], op=ALU.subtract)
            vector.tensor_single_scalar(out=r[:, 0:T],
                                            in_=r[:, 512:512 + T],
                                            scalar=LN_EPS, op=ALU.add)
            incd(vector, "ve")

        @blk.scalar
        def _(scalar):
            wall(scalar, "ve")
            scalar.activation(out=rows2[0:1, 512:512 + T],
                                  in_=rows[0:1, 0:T], func=AF.Sqrt)
            incd(scalar, "sc")

        @blk.vector
        def _(vector):
            wall(vector, "sc")
            r, r2 = rows[0:1, :], rows2[0:1, :]
            vector.reciprocal(out=r2[:, 512:512 + T], in_=r2[:, 512:512 + T])
            vector.drain()
            vector.tensor_tensor(out=r[:, 512:512 + T], in0=r2[:, 512:512 + T],
                                 in1=r2[:, 512:512 + T], op=ALU.mult)
            vector.drain()
            vector.tensor_tensor(out=r[:, 512:512 + T], in0=r[:, 0:T],
                                 in1=r[:, 512:512 + T], op=ALU.mult)
            vector.drain()
            vector.tensor_scalar(out=r[:, 512:512 + T], in0=r[:, 512:512 + T],
                                 scalar1=-0.5, scalar2=1.5,
                                 op0=ALU.mult, op1=ALU.add)
            vector.drain()
            vector.tensor_tensor(out=r[:, 0:T], in0=r2[:, 512:512 + T],
                                 in1=r[:, 512:512 + T], op=ALU.mult)
            vector.drain()
            vector.tensor_tensor(out=r2[:, 512:512 + T], in0=r2[:, 0:T],
                                 in1=r[:, 0:T], op=ALU.mult)
            vector.drain()
            vector.tensor_single_scalar(out=r2[:, 512:512 + T],
                                            in_=r2[:, 512:512 + T],
                                            scalar=-1.0, op=ALU.mult)
            incd(vector, "ve")

        @blk.tensor
        def _(tensor):
            wall(tensor, "ve")
            tensor.matmul(ps[2][:, :T], onescol[:, :], rows[0:1, 0:T],
                          start=True, stop=True)
            inc(tensor.matmul(ps[3][:, :T], onescol[:, :],
                              rows2[0:1, 512:512 + T], start=True, stop=True),
                "mm")

        @blk.vector
        def _(vector):
            wall(vector, "mm")
            vector.tensor_copy(out=A_[:, :T], in_=ps[2][:, :T])
            vector.tensor_copy(out=B_[:, :T], in_=ps[3][:, :T])
            incd(vector, "ve")

        if mode == "stats":
            return

        g2sn, ve_b = snap("g2"), cnt["ve"]
        ops = {"f16": 2, "hilo": 3, "absmax": 3}[mode] + (5 if density else 0)

        @blk.gpsimd
        def _(g):
            wall(g, "ve")
            for ci in range(HC):
                if ci >= 2:
                    g.wait_ge(sems["ve"], ve_b + (ci - 1) * ops)
                pdma(g, xsb[ci % 2][:, :T],
                     xin[ci * P:(ci + 1) * P, t0:t0 + T], "g2", ci)

        @blk.vector
        def _(vector):
            wall(vector, "so0", "so1")
            if mode == "absmax":
                vector.memset(misc[:, 0:T], 0.0)
            for ci in range(HC):
                pwait(vector, "g2", g2sn, ci)
                vector.tensor_tensor(out=xn2[ci % 2][:, :T],
                                     in0=xsb[ci % 2][:, :T], in1=A_[:, :T],
                                     op=ALU.mult)
                if mode == "f16":
                    vector.tensor_tensor(out=f16b[:, ci * T0:ci * T0 + T],
                                             in0=xn2[ci % 2][:, :T],
                                             in1=B_[:, :T], op=ALU.add)
                    incd(vector, "ve")
                    vector.tensor_copy(out=misc[:, 1023:1024],
                                           in_=misc[:, 1023:1024])
                    incd(vector, "ve")
                elif mode == "absmax":
                    vector.tensor_tensor(out=xn2[ci % 2][:, :T],
                                         in0=xn2[ci % 2][:, :T],
                                         in1=B_[:, :T], op=ALU.add)
                    vector.tensor_single_scalar(out=sqb[ci % 2][:, :T],
                                                in_=xn2[ci % 2][:, :T],
                                                scalar=-1.0, op=ALU.mult)
                    vector.tensor_tensor(out=misc[:, 0:T],
                                             in0=misc[:, 0:T],
                                             in1=xn2[ci % 2][:, :T],
                                             op=ALU.max)
                    incd(vector, "ve")
                    vector.tensor_tensor(out=misc[:, 0:T],
                                             in0=misc[:, 0:T],
                                             in1=sqb[ci % 2][:, :T],
                                             op=ALU.max)
                    incd(vector, "ve")
                    vector.tensor_copy(out=misc[:, 1023:1024],
                                           in_=misc[:, 1023:1024])
                    incd(vector, "ve")
                else:
                    vector.tensor_tensor(out=xn2[ci % 2][:, :T],
                                         in0=xn2[ci % 2][:, :T],
                                         in1=B_[:, :T], op=ALU.add)
                    vector.tensor_copy(out=f16b[:, ci * T0:ci * T0 + T],
                                       in_=xn2[ci % 2][:, :T])
                    k = vector.tensor_tensor(
                        out=f16big[:, ci * T0:ci * T0 + T],
                        in0=xn2[ci % 2][:, :T],
                        in1=f16b[:, ci * T0:ci * T0 + T], op=ALU.subtract)
                    inc(k, "ve")
                    vector.tensor_copy(out=misc[:, 1023:1024],
                                           in_=misc[:, 1023:1024])
                    incd(vector, "ve")
                    vector.tensor_copy(out=misc[:, 1023:1024],
                                           in_=misc[:, 1023:1024])
                    incd(vector, "ve")
                    if density:
                        vector.tensor_single_scalar(out=sqb[ci % 2][:, :T],
                                                    in_=xn2[ci % 2][:, :T],
                                                    scalar=0.1, op=ALU.is_gt)
                        vector.tensor_single_scalar(out=sqb[(ci + 1) % 2][:, :T],
                                                    in_=xn2[ci % 2][:, :T],
                                                    scalar=-0.1, op=ALU.is_lt)
                        vector.tensor_tensor(out=sqb[ci % 2][:, :T],
                                             in0=sqb[ci % 2][:, :T],
                                             in1=sqb[(ci + 1) % 2][:, :T],
                                             op=ALU.add)
                        vector.reduce_sum(out=sqb[ci % 2][:, :1],
                                          in_=sqb[ci % 2][:, :T], axis=AX.X)
                        vector.drain()
                        vector.tensor_tensor(out=dacc[:], in0=dacc[:],
                                                 in1=sqb[ci % 2][:, :1],
                                                 op=ALU.add)
                        incd(vector, "ve")
                        for _pad in range(4):
                            vector.tensor_copy(out=misc[:, 1023:1024],
                                                   in_=misc[:, 1023:1024])
                            incd(vector, "ve")
        assert cnt["ve"] == ve_b + HC * ops, (cnt["ve"], ve_b, ops)


    stream_prev = {"ve": 0, "sc": 0}

    def stream_w(srcs, W, free_pred, pre_cb=None, hi_scalar=False):
        ldsn = snap("ld")
        lo_r = {}
        hi_sc = {}
        mm_guard = cnt["mm"]

        @blk.sync
        def _(sync):
            sync.wait_ge(sems["ve"], stream_prev["ve"])
            sync.wait_ge(sems["sc"], stream_prev["sc"])
            for j in range(len(srcs)):
                if j >= 2:
                    sync.wait_ge(sems["mm"], free_pred(j - 2))
                pdma(sync, f32w[:, (j % 2) * 2048:(j % 2) * 2048 + W],
                     srcs[j], "ld", j)

        if hi_scalar:
            @blk.scalar
            def _(scalar):
                scalar.wait_ge(sems["mm"], mm_guard)
                for j in range(len(srcs)):
                    pwait(scalar, "ld", ldsn, j)
                    if j >= 2:
                        scalar.wait_ge(sems["mm"], free_pred(j - 2))
                    scalar.copy(
                        out=f16w[:, (j % 2) * 4096:(j % 2) * 4096 + W],
                        in_=f32w[:, (j % 2) * 2048:(j % 2) * 2048 + W])
                    hi_sc[j] = incd(scalar, "sc")

        @blk.vector
        def _(vector):
            vector.wait_ge(sems["mm"], mm_guard)
            for j in range(len(srcs)):
                if pre_cb is not None:
                    pre_cb(vector, j)
                if hi_scalar:
                    vector.wait_ge(sems["sc"], hi_sc[j])
                else:
                    pwait(vector, "ld", ldsn, j)
                    if j >= 2:
                        vector.wait_ge(sems["mm"], free_pred(j - 2))
                    vector.tensor_copy(
                        out=f16w[:, (j % 2) * 4096:(j % 2) * 4096 + W],
                        in_=f32w[:, (j % 2) * 2048:(j % 2) * 2048 + W])
                vector.tensor_tensor(
                    out=f16w[:, (j % 2) * 4096 + 2048:
                             (j % 2) * 4096 + 2048 + W],
                    in0=f32w[:, (j % 2) * 2048:(j % 2) * 2048 + W],
                    in1=f16w[:, (j % 2) * 4096:(j % 2) * 4096 + W],
                    op=ALU.subtract)
                lo_r[j] = incd(vector, "ve")
            if pre_cb is not None:
                for j in range(len(srcs), len(srcs) + 8):
                    pre_cb(vector, j)
        stream_prev["ve"] = cnt["ve"]
        stream_prev["sc"] = cnt["sc"]
        return lo_r

    def whi(j, kk):
        b = (j % 2) * 4096
        return f16w[:, b + kk * P:b + (kk + 1) * P]

    def wlo(j, kk):
        b = (j % 2) * 4096 + 2048
        return f16w[:, b + kk * P:b + (kk + 1) * P]

    def emit_ffn0_tile(xin, xout, t0):
        if PARANOID & 1:
            emit_barrier()
        emit_ln(xin, t0, T0, "f16")
        if PARANOID & 2:
            emit_barrier()

        def x16(ci):
            return f16b[:, ci * T0:(ci + 1) * T0]

        def d16(h):
            return f16big[:, h * T0:(h + 1) * T0]

        if DEV_DUMP and t0 == 0:
            vehere = cnt["ve"]

            @blk.gpsimd
            def _(g):
                g.wait_ge(sems["ve"], vehere)
                inc(g.dma_start(out=dbgX[:], in_=f16b[:]), "so0", 16)

        mm_b = cnt["mm"]
        lo_r = stream_w([w0p[h] for h in range(FF0C)], HC * P,
                        lambda j: mm_b + (j + 1))
        sc_g = cnt["sc"]
        mm_h = {}

        @blk.tensor
        def _(tensor):
            for h in range(FF0C):
                tensor.wait_ge(sems["ve"], lo_r[h])
                if h >= 2:
                    tensor.wait_ge(sems["sc"], sc_g + (h - 1))
                pd = ps[4 + (h % 2)]
                i = None
                for pss, wf in ((0, whi), (1, wlo)):
                    for kk in range(HC):
                        i = tensor.matmul(pd[:, :T0], wf(h, kk), x16(kk),
                                          start=(pss == 0 and kk == 0),
                                          stop=(pss == 1 and kk == HC - 1))
                mm_h[h] = inc(i, "mm")

        @blk.scalar
        def _(scalar):
            for h in range(FF0C):
                scalar.wait_ge(sems["mm"], mm_h[h])
                scalar.activation(out=d16(h), in_=ps[4 + (h % 2)][:, :T0],
                                      func=AF.Gelu, bias=b0_sb[:, h:h + 1],
                                      scale=1.0)
                incd(scalar, "sc")

        if DEV_DUMP and t0 == 0:
            schere = cnt["sc"]

            @blk.gpsimd
            def _(g):
                g.wait_ge(sems["sc"], schere)
                inc(g.dma_start(out=dbgD[:], in_=f16big[:]), "so0", 16)

        if PARANOID & 4:
            emit_barrier()

        mm_b2 = cnt["mm"]
        g2sn2 = snap("g2")
        sosn2 = snap("so")
        evac_ve = {}
        qlist = [(e, qtr) for e in range(HC) for qtr in range(4)]

        def e_evac(vector, j):
            if j % 4 != 0:
                return
            e = j // 4 - 2
            if e < 0 or e >= HC:
                return
            vector.wait_ge(sems["mm"], mm_b2 + 4 * e + 4)
            n_, v_ = pval("g2", g2sn2, e)
            vector.wait_ge(sems[n_], v_)
            if e >= 2:
                n_, v_ = pval("so", sosn2, e - 2)
                vector.wait_ge(sems[n_], v_)
            e16 = f16b[:, (e % 2) * T0:(e % 2) * T0 + T0]
            vector.tensor_single_scalar(
                out=e16, in_=ps[6 if e % 2 == 0 else 0][:, :T0],
                scalar=b1_sb[:, e:e + 1], op=ALU.add)
            vector.scalar_tensor_tensor(
                out=xn2[e % 2][:, :T0], in0=e16, scalar=0.5,
                in1=xsb[e % 2][:, :T0], op0=ALU.mult, op1=ALU.add)
            evac_ve[e] = incd(vector, "ve")

        lo2 = stream_w(
            [w1p[e][:, qtr * 2048:(qtr + 1) * 2048] for e, qtr in qlist],
            HC * P, lambda j: mm_b2 + (j + 1), pre_cb=e_evac, hi_scalar=True)
        mm_e = {}

        @blk.tensor
        def _(tensor):
            for j, (e, qtr) in enumerate(qlist):
                tensor.wait_ge(sems["ve"], lo2[j])
                if qtr == 0 and e >= 2:
                    tensor.wait_ge(sems["ve"], evac_ve[e - 2])
                pd = ps[6 if e % 2 == 0 else 0]
                i = None
                for pss, wf in ((0, whi), (1, wlo)):
                    for kk in range(HC):
                        hh = qtr * HC + kk
                        i = tensor.matmul(pd[:, :T0], wf(j, kk), d16(hh),
                                          start=(qtr == 0 and pss == 0
                                                 and kk == 0),
                                          stop=(qtr == 3 and pss == 1
                                                and kk == HC - 1))
                mm_e[j] = inc(i, "mm")

        @blk.gpsimd
        def _(g):
            wall(g, "so0", "so1")
            for e in range(HC + 2):
                if e < HC:
                    if e >= 2:
                        g.wait_ge(sems["ve"], evac_ve[e - 2])
                    pdma(g, xsb[e % 2][:, :T0],
                         xin[e * P:(e + 1) * P, t0:t0 + T0], "g2", e)
                if e >= 2:
                    ee = e - 2
                    g.wait_ge(sems["ve"], evac_ve[ee])
                    pdma(g, xout[ee * P:(ee + 1) * P, t0:t0 + T0],
                         xn2[ee % 2][:, :T0], "so", ee)
            wall(g, "so0", "so1")

    def emit_proj(pack, dst_hi, dst_lo, t0, bias_sb=None, out_scale=None):
        if PARANOID & 8:
            emit_barrier()

        def xhi(ci):
            return f16b[:, ci * T0:(ci + 1) * T0]

        def xlo(ci):
            return f16big[:, ci * T0:(ci + 1) * T0]

        def sth(oc):
            b = 8192 + (oc % 2) * 512
            return f16big[:, b:b + T0]

        def stl(oc):
            b = 9216 + (oc % 2) * 512
            return f16big[:, b:b + T0]

        mm_b = cnt["mm"]
        sosn = snap("so")
        evac_ve = {}

        def p_evac(vector, j):
            oc = j - 2
            if oc < 0 or oc >= HC:
                return
            vector.wait_ge(sems["mm"], mm_b + oc + 1)
            if oc >= 2:
                jj = oc - 2
                vector.wait_ge(sems["so" + str(jj % 2)],
                               sosn[jj % 2] + 32 * (jj // 2 + 1))
            pd = ps[4 + (oc % 2)][:, :T0]
            src = pd
            if bias_sb is not None:
                vector.tensor_single_scalar(out=misc[:, :T0], in_=src,
                                            scalar=bias_sb[:, oc:oc + 1],
                                            op=ALU.add)
                src = misc[:, :T0]
            if out_scale is not None:
                vector.tensor_single_scalar(out=misc[:, :T0], in_=src,
                                            scalar=float(out_scale),
                                            op=ALU.mult)
                src = misc[:, :T0]
            vector.tensor_copy(out=sth(oc), in_=src)
            vector.tensor_tensor(
                out=stl(oc), in0=src, in1=sth(oc), op=ALU.subtract)
            evac_ve[oc] = incd(vector, "ve")

        lo_r = stream_w([pack[oc] for oc in range(HC)], HC * P,
                        lambda j: mm_b + (j + 1), pre_cb=p_evac,
                        hi_scalar=True)

        @blk.tensor
        def _(tensor):
            for oc in range(HC):
                tensor.wait_ge(sems["ve"], lo_r[oc])
                if oc >= 2:
                    tensor.wait_ge(sems["ve"], evac_ve[oc - 2])
                pd = ps[4 + (oc % 2)]
                i = None
                n = 0
                for wf, xf in ((whi, xhi), (whi, xlo), (wlo, xhi)):
                    for kk in range(HC):
                        i = tensor.matmul(pd[:, :T0], wf(oc, kk), xf(kk),
                                          start=(n == 0),
                                          stop=(n == 3 * HC - 1))
                        n += 1
                inc(i, "mm")

        @blk.gpsimd
        def _(g):
            for oc in range(HC):
                g.wait_ge(sems["ve"], evac_ve[oc])
                inc(g.dma_start(out=dst_hi[oc * P:(oc + 1) * P, t0:t0 + T0],
                                in_=sth(oc)), "so" + str(oc % 2), 16)
                inc(g.dma_start(out=dst_lo[oc * P:(oc + 1) * P, t0:t0 + T0],
                                in_=stl(oc)), "so" + str(oc % 2), 16)

    def emit_oproj(pack, t0, xin, xout, bias_sb=None):
        if PARANOID & 8:
            emit_barrier()
        g2at = cnt["g20"]

        @blk.gpsimd
        def _(g):
            wall(g, "ve", "mm", "so0", "so1")
            for ci in range(HC):
                inc(g.dma_start(out=f16big[:, ci * T0:(ci + 1) * T0],
                                in_=athi_d[ci * P:(ci + 1) * P, t0:t0 + T0]),
                    "g20", 16)
                inc(g.dma_start(
                    out=f16big[:, 16384 + ci * T0:16384 + (ci + 1) * T0],
                    in_=atlo_d[ci * P:(ci + 1) * P, t0:t0 + T0]), "g20", 16)

        def ahi(ci):
            return f16big[:, ci * T0:(ci + 1) * T0]

        def alo(ci):
            return f16big[:, 16384 + ci * T0:16384 + (ci + 1) * T0]

        mm_b = cnt["mm"]
        g2sn2 = snap("g2")
        sosn = snap("so")
        evac_ve = {}

        def o_evac(vector, j):
            oc = j - 2
            if oc < 0 or oc >= HC:
                return
            vector.wait_ge(sems["mm"], mm_b + oc + 1)
            n_, v_ = pval("g2", g2sn2, oc)
            vector.wait_ge(sems[n_], v_)
            if oc >= 2:
                n_, v_ = pval("so", sosn, oc - 2)
                vector.wait_ge(sems[n_], v_)
            pd = ps[4 + (oc % 2)][:, :T0]
            if bias_sb is not None:
                vector.tensor_single_scalar(out=misc[:, :T0], in_=pd,
                                            scalar=bias_sb[:, oc:oc + 1],
                                            op=ALU.add)
                vector.tensor_tensor(
                    out=xn2[oc % 2][:, :T0], in0=misc[:, :T0],
                    in1=xsb[oc % 2][:, :T0], op=ALU.add)
                evac_ve[oc] = incd(vector, "ve")
            else:
                vector.tensor_tensor(
                    out=xn2[oc % 2][:, :T0], in0=pd,
                    in1=xsb[oc % 2][:, :T0], op=ALU.add)
                evac_ve[oc] = incd(vector, "ve")

        lo_r = stream_w([pack[oc] for oc in range(HC)], HC * P,
                        lambda j: mm_b + (j + 1), pre_cb=o_evac,
                        hi_scalar=True)

        @blk.tensor
        def _(tensor):
            tensor.wait_ge(sems["g20"], g2at + 32 * HC)
            for oc in range(HC):
                tensor.wait_ge(sems["ve"], lo_r[oc])
                if oc >= 2:
                    tensor.wait_ge(sems["ve"], evac_ve[oc - 2])
                pd = ps[4 + (oc % 2)]
                i = None
                n = 0
                for wf, xf in ((whi, ahi), (whi, alo), (wlo, ahi)):
                    for kk in range(HC):
                        i = tensor.matmul(pd[:, :T0], wf(oc, kk), xf(kk),
                                          start=(n == 0),
                                          stop=(n == 3 * HC - 1))
                        n += 1
                inc(i, "mm")

        @blk.gpsimd
        def _(g):
            for oc in range(HC + 2):
                if oc < HC:
                    if oc >= 2:
                        g.wait_ge(sems["ve"], evac_ve[oc - 2])
                    pdma(g, xsb[oc % 2][:, :T0],
                         xin[oc * P:(oc + 1) * P, t0:t0 + T0], "g2", oc)
                if oc >= 2:
                    ee = oc - 2
                    g.wait_ge(sems["ve"], evac_ve[ee])
                    pdma(g, xout[ee * P:(ee + 1) * P, t0:t0 + T0],
                         xn2[ee % 2][:, :T0], "so", ee)
            wall(g, "so0", "so1")

    def emit_attn_core(layer, q0, q_tokens):
        khi = f16big[:, 0:2048]
        klo = f16big[:, 2048:4096]
        vhi = f16big[:, 4096:6144]
        vlo = f16big[:, 6144:8192]
        vthi = f16big[:, 8192:10240]
        vtlo = f16big[:, 10240:12288]
        pthi = f16big[:, 12288:20480]
        ptlo = f16big[:, 20480:28672]
        phi_ = f16big[:, 28672:30720]
        plo_ = f16big[:, 30720:32768]
        pt_hi_v = pthi.rearrange("p (j q) -> p j q", q=T0)
        pt_lo_v = ptlo.rearrange("p (j q) -> p j q", q=T0)
        vt_hi_v = vthi.rearrange("p (j d) -> p j d", d=P)
        vt_lo_v = vtlo.rearrange("p (j d) -> p j d", d=P)

        n_qg = q_tokens // T0
        for h in range(NHD):
            if PARANOID & 8:
                emit_barrier()
            ld_b = cnt["ld0"]

            @blk.sync
            def _(sync):
                wall(sync, "mm", "ve", "sc", "so0", "so1")
                for dst, src in ((khi, khi_d), (klo, klo_d),
                                 (vhi, vhi_d), (vlo, vlo_d)):
                    inc(sync.dma_start(out=dst,
                                       in_=src[h * P:(h + 1) * P, :]),
                        "ld0", 16)

            mm_b = cnt["mm"]
            ve_b = cnt["ve"]

            @blk.tensor
            def _(tensor):
                tensor.wait_ge(sems["ld0"], ld_b + 64)
                wall(tensor, "ve")
                for gi in range(4):
                    src = (vhi, vhi, vlo, vlo)[gi]
                    grp = gi % 2
                    if gi >= 1:
                        tensor.wait_ge(sems["ve"], ve_b + gi)
                    i = None
                    for j in range(8):
                        jj = grp * 8 + j
                        i = tensor.transpose(out=psT[:, j * P:(j + 1) * P],
                                             in_=src[:, jj * P:(jj + 1) * P],
                                             identity=id16[:])
                    inc(i, "mm")

            @blk.vector
            def _(vector):
                for gi in range(4):
                    dst = (vthi, vthi, vtlo, vtlo)[gi]
                    grp = gi % 2
                    vector.wait_ge(sems["mm"], mm_b + gi + 1)
                    vector.tensor_copy(
                        out=dst[:, grp * 1024:(grp + 1) * 1024],
                        in_=psT[:, 0:1024])
                    incd(vector, "ve")

            for qg in range(n_qg):
                qg0 = q0 + qg * T0
                for qt in range(4):
                    if PARANOID & 16:
                        emit_barrier()
                    qt0 = qg0 + qt * P
                    g2_b = cnt["g20"]

                    @blk.gpsimd
                    def _(g):
                        wall(g, "mm", "sc")
                        inc(g.dma_start(
                            out=f16b[:, 0:P],
                            in_=qhi_d[h * P:(h + 1) * P, qt0:qt0 + P]),
                            "g20", 16)
                        inc(g.dma_start(
                            out=f16b[:, P:2 * P],
                            in_=qlo_d[h * P:(h + 1) * P, qt0:qt0 + P]),
                            "g20", 16)

                    @blk.tensor
                    def _(tensor):
                        tensor.wait_ge(sems["g20"], g2_b + 32)
                        wall(tensor, "sc")
                        i = None
                        for pss, (qa, ka) in enumerate(
                                ((f16b[:, 0:P], khi),
                                 (f16b[:, P:2 * P], khi),
                                 (f16b[:, 0:P], klo))):
                            for jb in range(4):
                                i = tensor.matmul(
                                    ps[jb][:, :T0], qa,
                                    ka[:, jb * T0:(jb + 1) * T0],
                                    start=(pss == 0), stop=(pss == 2))
                        inc(i, "mm")

                    @blk.vector
                    def _(vector):
                        wall(vector, "mm")
                        if layer == 1:
                            mrow = (qt0 - q0) // P
                            mview = bf16a[:, mrow * 2048:(mrow + 1) * 2048]
                            for jb in range(4):
                                vector.tensor_tensor(
                                    out=ps[jb][:, :T0], in0=ps[jb][:, :T0],
                                    in1=mview[:, jb * T0:(jb + 1) * T0],
                                    op=ALU.add)
                        for jb in range(4):
                            vector.reduce_max(out=misc[:, jb:jb + 1],
                                              in_=ps[jb][:, :T0], axis=AX.X)
                        vector.drain()
                        vector.tensor_tensor(out=misc[:, 0:1],
                                             in0=misc[:, 0:1],
                                             in1=misc[:, 1:2], op=ALU.max)
                        vector.drain()
                        vector.tensor_tensor(out=misc[:, 2:3],
                                             in0=misc[:, 2:3],
                                             in1=misc[:, 3:4], op=ALU.max)
                        vector.drain()
                        vector.tensor_tensor(out=misc[:, 0:1],
                                             in0=misc[:, 0:1],
                                             in1=misc[:, 2:3], op=ALU.max)
                        vector.drain()
                        vector.tensor_single_scalar(
                            out=misc[:, 4:5], in_=misc[:, 0:1],
                            scalar=-1.0, op=ALU.mult)
                        incd(vector, "ve")

                    @blk.scalar
                    def _(scalar):
                        wall(scalar, "ve")
                        for jb in range(4):
                            scalar.activation(
                                out=pe_[:, jb * T0:(jb + 1) * T0],
                                in_=ps[jb][:, :T0], func=AF.Exp,
                                bias=misc[:, 4:5], scale=1.0,
                                accum_out=misc[:, 5 + jb:6 + jb])
                            incd(scalar, "sc")

                    @blk.vector
                    def _(vector):
                        wall(vector, "sc")
                        vector.tensor_tensor(out=misc[:, 9:10],
                                             in0=misc[:, 5:6],
                                             in1=misc[:, 6:7], op=ALU.add)
                        vector.tensor_tensor(out=misc[:, 10:11],
                                             in0=misc[:, 7:8],
                                             in1=misc[:, 8:9], op=ALU.add)
                        vector.drain()
                        vector.tensor_tensor(out=misc[:, 9:10],
                                             in0=misc[:, 9:10],
                                             in1=misc[:, 10:11], op=ALU.add)
                        vector.drain()
                        vector.reciprocal(out=misc[:, 10:11],
                                          in_=misc[:, 9:10])
                        vector.drain()
                        vector.tensor_tensor(out=misc[:, 11:12],
                                             in0=misc[:, 9:10],
                                             in1=misc[:, 10:11], op=ALU.mult)
                        vector.drain()
                        vector.tensor_scalar(out=misc[:, 11:12],
                                             in0=misc[:, 11:12], scalar1=-1.0,
                                             scalar2=2.0, op0=ALU.mult,
                                             op1=ALU.add)
                        vector.drain()
                        vector.tensor_tensor(out=misc[:, 10:11],
                                             in0=misc[:, 10:11],
                                             in1=misc[:, 11:12], op=ALU.mult)
                        vector.drain()
                        vector.tensor_single_scalar(out=pe_[:, :],
                                                    in_=pe_[:, :],
                                                    scalar=misc[:, 10:11],
                                                    op=ALU.mult)
                        vector.tensor_copy(out=phi_[:], in_=pe_[:])
                        vector.tensor_tensor(out=plo_[:], in0=pe_[:],
                                                 in1=phi_[:],
                                                 op=ALU.subtract)
                        incd(vector, "ve")

                    mm_b3 = cnt["mm"]
                    ve_b3 = cnt["ve"]

                    @blk.tensor
                    def _(tensor):
                        wall(tensor, "ve")
                        for gi in range(4):
                            src = (phi_, phi_, plo_, plo_)[gi]
                            grp = gi % 2
                            if gi >= 1:
                                tensor.wait_ge(sems["ve"], ve_b3 + gi)
                            i = None
                            for j in range(8):
                                jj = grp * 8 + j
                                i = tensor.transpose(
                                    out=psT[:, j * P:(j + 1) * P],
                                    in_=src[:, jj * P:(jj + 1) * P],
                                    identity=id16[:])
                            inc(i, "mm")

                    @blk.vector
                    def _(vector):
                        for gi in range(4):
                            dstv = (pt_hi_v, pt_hi_v, pt_lo_v, pt_lo_v)[gi]
                            grp = gi % 2
                            vector.wait_ge(sems["mm"], mm_b3 + gi + 1)
                            vector.tensor_copy(
                                out=dstv[:, grp * 8:(grp + 1) * 8,
                                         qt * P:(qt + 1) * P],
                                in_=psT[:, 0:1024].rearrange(
                                    "p (j q) -> p j q", q=P))
                            incd(vector, "ve")

                if PARANOID & 16:
                    emit_barrier()

                @blk.tensor
                def _(tensor):
                    wall(tensor, "ve")
                    pd = ps[4 + (qg % 2)]
                    i = None
                    n = 0
                    for vt, pt in ((vt_hi_v, pt_hi_v), (vt_hi_v, pt_lo_v),
                                   (vt_lo_v, pt_hi_v)):
                        for j in range(HC):
                            i = tensor.matmul(pd[:, :T0], vt[:, j],
                                              pt[:, j, :], start=(n == 0),
                                              stop=(n == 3 * HC - 1))
                            n += 1
                    inc(i, "mm")

                @blk.vector
                def _(vector):
                    wall(vector, "mm", "so0", "so1")
                    pd = ps[4 + (qg % 2)][:, :T0]
                    a16h = f16b[:, 1024 + (qg % 2) * 1024:
                                1024 + (qg % 2) * 1024 + T0]
                    a16l = f16b[:, 1536 + (qg % 2) * 1024:
                                1536 + (qg % 2) * 1024 + T0]
                    vector.tensor_copy(out=a16h, in_=pd)
                    vector.tensor_tensor(out=a16l, in0=pd, in1=a16h,
                                             op=ALU.subtract)
                    incd(vector, "ve")

                @blk.gpsimd
                def _(g):
                    wall(g, "ve")
                    a16h = f16b[:, 1024 + (qg % 2) * 1024:
                                1024 + (qg % 2) * 1024 + T0]
                    a16l = f16b[:, 1536 + (qg % 2) * 1024:
                                1536 + (qg % 2) * 1024 + T0]
                    inc(g.dma_start(
                        out=athi_d[h * P:(h + 1) * P, qg0:qg0 + T0],
                        in_=a16h), "so0", 16)
                    inc(g.dma_start(
                        out=atlo_d[h * P:(h + 1) * P, qg0:qg0 + T0],
                        in_=a16l), "so0", 16)

    NHD = 16

    def emit_mask_build(q0):
        col_to_row(dacc[:], rows[0:1, 0:P])

        @blk.vector
        def _(vector):
            wall(vector, "ve")
            vector.reduce_sum(out=rows[0:1, 128:129], in_=rows[0:1, 0:P],
                              axis=AX.X)
            vector.drain()
            vector.tensor_single_scalar(out=rows[0:1, 129:130],
                                            in_=rows[0:1, 128:129],
                                            scalar=float(S) * float(H) * 0.5,
                                            op=ALU.is_gt)
            incd(vector, "ve")

        bcast_row(rows[0:1, 129:130], 1, gcol[:, 0:1])

        ld_b = cnt["ld0"]
        ve_b = cnt["ve"]

        @blk.sync
        def _(sync):
            wall(sync, "ve")
            for r in range(8):
                if r >= 1:
                    sync.wait_ge(sems["ve"], ve_b + r)
                inc(sync.dma_start(out=bf16a[:, r * 2048:(r + 1) * 2048],
                                   in_=mss[r * P:(r + 1) * P, :]), "ld0", 16)
                inc(sync.dma_start(out=bf16st[:],
                                   in_=mdl[r * P:(r + 1) * P, :]), "ld0", 16)

        @blk.vector
        def _(vector):
            for r in range(8):
                vector.wait_ge(sems["ld0"], ld_b + 32 * (r + 1))
                vector.scalar_tensor_tensor(
                    out=bf16a[:, r * 2048:(r + 1) * 2048], in0=bf16st[:],
                    scalar=gcol[:, 0:1],
                    in1=bf16a[:, r * 2048:(r + 1) * 2048],
                    op0=ALU.mult, op1=ALU.add)
                incd(vector, "ve")

    def emit_wquant():
        for wi, (pack, tpack, n_oc, W) in enumerate(
                ((wup, tup, FF1C, 2048), (wdp, tdn, HC, 4096))):
            srcs = []
            dsts = []
            for oc in range(n_oc):
                if W == 2048:
                    srcs.append(pack[oc])
                    dsts.append(tpack[oc])
                else:
                    srcs.append(pack[oc][:, 0:2048])
                    srcs.append(pack[oc][:, 2048:4096])
                    dsts.append(tpack[oc][:, 0:2048])
                    dsts.append(tpack[oc][:, 2048:4096])
            ldsn = snap("ld")
            ve_b = cnt["ve"]

            @blk.sync
            def _(sync):
                wall(sync, "ve", "sc", "mm")
                for j in range(len(srcs)):
                    if j >= 2:
                        sync.wait_ge(sems["ve"], ve_b + (j - 1))
                    pdma(sync, f32w[:, (j % 2) * 2048:(j % 2) * 2048 + 2048],
                         srcs[j], "ld", j)

            @blk.vector
            def _(vector):
                vector.memset(misc[:, 512:513], 0.0)
                for j in range(len(srcs)):
                    pwait(vector, "ld", ldsn, j)
                    vector.reduce_sum(out=misc[:, 513:514],
                                      in_=f32w[:, (j % 2) * 2048:
                                               (j % 2) * 2048 + 2048],
                                      axis=AX.X, apply_absolute_value=True)
                    vector.drain()
                    vector.tensor_tensor(out=misc[:, 512:513],
                                             in0=misc[:, 512:513],
                                             in1=misc[:, 513:514],
                                             op=ALU.add)
                    incd(vector, "ve")

            col_to_row(misc[:, 512:513], rows[0:1, 0:P])

            @blk.vector
            def _(vector):
                wall(vector, "ve")
                vector.reduce_sum(out=rows[0:1, 128:129], in_=rows[0:1, 0:P],
                                  axis=AX.X)
                vector.drain()
                vector.tensor_single_scalar(out=rows[0:1, 128:129],
                                            in_=rows[0:1, 128:129],
                                            scalar=1.0 / (4096.0 * 2048.0),
                                            op=ALU.mult)
                vector.drain()
                vector.tensor_single_scalar(out=rows[0:1, 128:129],
                                                in_=rows[0:1, 128:129],
                                                scalar=1e-5, op=ALU.max)
                incd(vector, "ve")

            @blk.vector
            def _(vector):
                wall(vector, "ve")
                vector.reciprocal(out=rows[0:1, 129:130],
                                      in_=rows[0:1, 128:129])
                incd(vector, "ve")
                vector.tensor_tensor(out=rows[0:1, 130:131],
                                     in0=rows[0:1, 128:129],
                                     in1=rows[0:1, 129:130], op=ALU.mult)
                vector.drain()
                vector.tensor_scalar(out=rows[0:1, 130:131],
                                     in0=rows[0:1, 130:131], scalar1=-1.0,
                                     scalar2=2.0, op0=ALU.mult, op1=ALU.add)
                vector.drain()
                vector.tensor_tensor(out=rows[0:1, 129:130],
                                     in0=rows[0:1, 129:130],
                                     in1=rows[0:1, 130:131], op=ALU.mult)
                vector.drain()
                if wi == 0:
                    vector.tensor_single_scalar(out=rows[0:1, 131:132],
                                                in_=rows[0:1, 128:129],
                                                scalar=1.0 / 127.0,
                                                op=ALU.mult)
                    vector.drain()
                    vector.tensor_tensor(out=rows[0:1, 131:132],
                                             in0=rows[0:1, 131:132],
                                             in1=rows[0:1, 131:132],
                                             op=ALU.mult)
                    incd(vector, "ve")
                else:
                    vector.tensor_copy(out=rows[0:1, 131:132],
                                           in_=rows[0:1, 128:129])
                    incd(vector, "ve")

            if wi == 0:
                bcast_row(rows[0:1, 131:132], 1, qsc[:, 12:13])
                bcast_row(rows[0:1, 129:130], 1, qsc[:, 14:15])
            else:
                bcast_row(rows[0:1, 128:129], 1, qsc[:, 13:14])
                bcast_row(rows[0:1, 129:130], 1, qsc[:, 15:16])

            rcol = qsc[:, 14:15] if wi == 0 else qsc[:, 15:16]
            ldsn2 = snap("ld")
            ve_b2 = cnt["ve"]
            so_b2 = cnt["so0"]

            @blk.sync
            def _(sync):
                wall(sync, "ve")
                for j in range(len(srcs)):
                    if j >= 2:
                        sync.wait_ge(sems["ve"], ve_b2 + (j - 1))
                    pdma(sync, f32w[:, (j % 2) * 2048:(j % 2) * 2048 + 2048],
                         srcs[j], "ld", j)

            @blk.vector
            def _(vector):
                for j in range(len(srcs)):
                    pwait(vector, "ld", ldsn2, j)
                    if j >= 1:
                        vector.wait_ge(sems["so0"], so_b2 + 16 * j)
                    wv = f32w[:, (j % 2) * 2048:(j % 2) * 2048 + 2048]
                    vector.tensor_single_scalar(out=wv, in_=wv, scalar=rcol,
                                                op=ALU.mult)
                    vector.drain()
                    for hh in range(2):
                        seg = wv[:, hh * 1024:(hh + 1) * 1024]
                        sga = f32g[:, 1024:2048]
                        sgb = f32g[:, 3072:4096]
                        vector.tensor_single_scalar(out=sga, in_=seg,
                                                    scalar=0.5, op=ALU.is_ge)
                        vector.drain()
                        vector.tensor_single_scalar(out=sgb, in_=seg,
                                                    scalar=-0.5, op=ALU.is_le)
                        vector.drain()
                        vector.tensor_tensor(
                            out=bf16st[:, hh * 1024:(hh + 1) * 1024],
                            in0=sga, in1=sgb, op=ALU.subtract)
                        vector.drain()
                    vector.tensor_copy(out=misc[:, 512:513],
                                           in_=misc[:, 512:513])
                    incd(vector, "ve")

            @blk.gpsimd
            def _(g):
                for j in range(len(srcs)):
                    g.wait_ge(sems["ve"], ve_b2 + (j + 1))
                    inc(g.dma_start(out=dsts[j], in_=bf16st[:]), "so0", 16)

            @blk.sync
            def _(sync):
                wall(sync, "so0", "so1")

    def emit_ffn1_tile(xin, xout, t0):
        if PARANOID & 8:
            emit_barrier()
        emit_ln(xin, t0, T1, "absmax")
        macc = misc[:, 0:T1]
        AQ = f32g[:, 2048 + 256:2048 + 512]
        CQ = f32g[:, 2560 + 256:2560 + 512]

        def k8(ci):
            return bf16a[:, ci * T1:(ci + 1) * T1]

        def k82(oc):
            return bf16a[:, 4096 + oc * T1:4096 + (oc + 1) * T1]

        def quant_scales(cmode):
            for tb in range(T1 // P):
                @blk.tensor
                def _(tensor):
                    wall(tensor, "ve", "sc")
                    inc(tensor.transpose(out=ps[6][:, 0:P],
                                         in_=macc[:, tb * P:(tb + 1) * P],
                                         identity=id32[:]), "mm")

                @blk.vector
                def _(vector):
                    wall(vector, "mm")
                    vector.reduce_max(out=qsc[:, 0:1], in_=ps[6][:, 0:P],
                                      axis=AX.X)
                    vector.drain()
                    if cmode == "x":
                        vector.tensor_single_scalar(out=qsc[:, 1:2],
                                                        in_=qsc[:, 0:1],
                                                        scalar=1e-5,
                                                        op=ALU.max)
                        incd(vector, "ve")
                    else:
                        m1s = qsc[:, 2 + tb:3 + tb]
                        vector.tensor_tensor(out=qsc[:, 4:5], in0=m1s,
                                             in1=m1s, op=ALU.mult)
                        vector.drain()
                        vector.tensor_tensor(out=qsc[:, 4:5],
                                             in0=qsc[:, 4:5],
                                             in1=qsc[:, 12:13], op=ALU.mult)
                        vector.drain()
                        vector.tensor_tensor(out=qsc[:, 1:2],
                                             in0=qsc[:, 4:5],
                                             in1=qsc[:, 0:1], op=ALU.mult)
                        vector.drain()
                        vector.tensor_single_scalar(out=qsc[:, 1:2],
                                                        in_=qsc[:, 1:2],
                                                        scalar=1e-5,
                                                        op=ALU.max)
                        incd(vector, "ve")

                @blk.vector
                def _(vector):
                    wall(vector, "ve")
                    vector.reciprocal(out=qsc[:, 5:6], in_=qsc[:, 1:2])
                    vector.drain()
                    vector.tensor_tensor(out=qsc[:, 6:7], in0=qsc[:, 1:2],
                                         in1=qsc[:, 5:6], op=ALU.mult)
                    vector.drain()
                    vector.tensor_scalar(out=qsc[:, 6:7], in0=qsc[:, 6:7],
                                         scalar1=-1.0, scalar2=2.0,
                                         op0=ALU.mult, op1=ALU.add)
                    vector.drain()
                    vector.tensor_tensor(out=qsc[:, 5:6], in0=qsc[:, 5:6],
                                         in1=qsc[:, 6:7], op=ALU.mult)
                    vector.drain()
                    if cmode == "x":
                        vector.tensor_copy(out=qsc[:, 2 + tb:3 + tb],
                                           in_=qsc[:, 1:2])
                        vector.drain()
                        vector.tensor_single_scalar(
                            out=qsc[:, 7:8], in_=qsc[:, 5:6], scalar=127.0,
                            op=ALU.mult)
                        incd(vector, "ve")
                    else:
                        vector.tensor_tensor(out=qsc[:, 7:8],
                                             in0=qsc[:, 4:5],
                                             in1=qsc[:, 5:6], op=ALU.mult)
                        vector.drain()
                        vector.tensor_single_scalar(out=qsc[:, 7:8],
                                                    in_=qsc[:, 7:8],
                                                    scalar=127.0,
                                                    op=ALU.mult)
                        vector.drain()
                        vector.tensor_tensor(out=qsc[:, 8:9],
                                             in0=qsc[:, 1:2],
                                             in1=qsc[:, 13:14], op=ALU.mult)
                        vector.drain()
                        vector.tensor_single_scalar(
                            out=qsc[:, 8:9], in_=qsc[:, 8:9],
                            scalar=0.5 / 127.0, op=ALU.mult)
                        incd(vector, "ve")

                col_to_row(qsc[:, 7:8], rows[0:1, 512 + tb * P:512 + (tb + 1) * P])
                if cmode == "h":
                    col_to_row(qsc[:, 8:9],
                               rows2[0:1, tb * P:(tb + 1) * P])
            bcast_row(rows[0:1, 512:512 + T1], T1, AQ)
            if cmode == "h":
                bcast_row(rows2[0:1, 0:T1], T1, CQ)

        def peel_inline(vector, src_ap, dst_ap):
            vector.tensor_tensor(out=sqb[0][:, :T1], in0=src_ap,
                                 in1=AQ, op=ALU.mult)
            vector.tensor_scalar(out=sqb[0][:, :T1], in0=sqb[0][:, :T1],
                                 scalar1=127.25, scalar2=-127.25,
                                 op0=ALU.min, op1=ALU.max)
            vector.tensor_copy(out=sqb[1][:, :T1], in_=sqb[0][:, :T1])
            for b in range(7, -1, -1):
                thr = float(2 ** b)
                vector.tensor_single_scalar(out=misc[:, 256:256 + T1],
                                            in_=sqb[1][:, :T1],
                                            scalar=thr - 128.5,
                                            op=ALU.is_ge)
                vector.scalar_tensor_tensor(out=sqb[1][:, :T1],
                                            in0=misc[:, 256:256 + T1],
                                            scalar=-thr, in1=sqb[1][:, :T1],
                                            op0=ALU.mult, op1=ALU.add)
            vector.scalar_tensor_tensor(out=sqb[1][:, :T1],
                                        in0=sqb[1][:, :T1], scalar=-1.0,
                                        in1=sqb[0][:, :T1],
                                        op0=ALU.mult, op1=ALU.add)
            vector.tensor_single_scalar(out=dst_ap, in_=sqb[1][:, :T1],
                                            scalar=-128.0, op=ALU.add)
            incd(vector, "ve")

        if PARANOID & 8:
            emit_barrier()
        quant_scales("x")
        g2sn = snap("g2")
        ve_b = cnt["ve"]

        @blk.gpsimd
        def _(g):
            wall(g, "ve")
            for ci in range(HC):
                if ci >= 2:
                    g.wait_ge(sems["ve"], ve_b + 2 * (ci - 1))
                pdma(g, xsb[ci % 2][:, :T1],
                     xin[ci * P:(ci + 1) * P, t0:t0 + T1], "g2", ci)

        @blk.vector
        def _(vector):
            wall(vector, "mm")
            for ci in range(HC):
                pwait(vector, "g2", g2sn, ci)
                vector.tensor_tensor(out=xn2[0][:, :T1],
                                     in0=xsb[ci % 2][:, :T1],
                                     in1=A_[:, :T1], op=ALU.mult)
                vector.tensor_tensor(out=xn2[0][:, :T1],
                                         in0=xn2[0][:, :T1],
                                         in1=B_[:, :T1], op=ALU.add)
                incd(vector, "ve")
                peel_inline(vector, xn2[0][:, :T1], k8(ci))
        k8_done = cnt["ve"]

        if PARANOID & 8:
            emit_barrier()
        sc_b = cnt["sc"]
        ve_b1 = cnt["ve"]
        sosn1 = snap("so")
        mm_1 = {}
        mm_1b = cnt["mm"]

        def mm_1_pred(oc):
            return mm_1b + oc + 1

        ldsn1 = snap("ld")

        @blk.sync
        def _(sync):
            wall(sync, "ve")
            for oc in range(FF1C):
                if oc >= 2:
                    sync.wait_ge(sems["mm"], mm_1_pred(oc - 2))
                pdma(sync, bf16a[:, 12288 + (oc % 2) * 2048:
                                 12288 + (oc % 2) * 2048 + 2048],
                     tup[oc], "ld", oc)

        @blk.tensor
        def _(tensor):
            tensor.wait_ge(sems["ve"], k8_done)
            for oc in range(FF1C):
                pwait(tensor, "ld", ldsn1, oc)
                if oc >= 2:
                    tensor.wait_ge(sems["sc"], sc_b + (oc - 1))
                pd = ps[4 + (oc % 2)]
                i = None
                for kk in range(HC):
                    tb = bf16a[:, 12288 + (oc % 2) * 2048 + kk * P:
                               12288 + (oc % 2) * 2048 + (kk + 1) * P]
                    i = tensor.matmul(pd[:, :T1], tb, k8(kk),
                                      start=(kk == 0), stop=(kk == HC - 1))
                mm_1[oc] = inc(i, "mm")

        @blk.vector
        def _(vector):
            vector.memset(macc, 0.0)
            vector.tensor_copy(out=misc[:, 1023:1024],
                                   in_=misc[:, 1023:1024])
            incd(vector, "ve")

        @blk.scalar
        def _(scalar):
            for oc in range(FF1C):
                scalar.wait_ge(sems["mm"], mm_1[oc])
                if oc >= 2:
                    scalar.wait_ge(sems["ve"], ve_b1 + 1 + (oc - 1))
                    pwait(scalar, "so", sosn1, oc - 2)
                scalar.activation(out=misc[:, 512:512 + T1],
                                  in_=ps[4 + (oc % 2)][:, :T1], func=AF.Relu)
                scalar.activation(out=xn2[oc % 2][:, :T1],
                                      in_=misc[:, 512:512 + T1],
                                      func=AF.Square)
                incd(scalar, "sc")

        @blk.vector
        def _(vector):
            for oc in range(FF1C):
                vector.wait_ge(sems["sc"], sc_b + (oc + 1))
                vector.tensor_tensor(out=macc, in0=macc,
                                         in1=xn2[oc % 2][:, :T1],
                                         op=ALU.max)
                incd(vector, "ve")

        @blk.gpsimd
        def _(g):
            for oc in range(FF1C):
                g.wait_ge(sems["ve"], ve_b1 + 1 + (oc + 1))
                pdma(g, hscr[oc * P:(oc + 1) * P, t0:t0 + T1],
                     xn2[oc % 2][:, :T1], "so", oc)

        if PARANOID & 8:
            emit_barrier()
        quant_scales("h")
        g2sn2 = snap("g2")
        ve_b2 = cnt["ve"]

        def k2_pred(oc):
            return ve_b2 + oc + 1

        @blk.gpsimd
        def _(g):
            wall(g, "ve", "so0", "so1")
            for oc in range(FF1C):
                if oc >= 2:
                    g.wait_ge(sems["ve"], k2_pred(oc - 2))
                pdma(g, xsb[oc % 2][:, :T1],
                     hscr[oc * P:(oc + 1) * P, t0:t0 + T1], "g2", oc)

        @blk.vector
        def _(vector):
            wall(vector, "mm")
            for oc in range(FF1C):
                pwait(vector, "g2", g2sn2, oc)
                peel_inline(vector, xsb[oc % 2][:, :T1], k82(oc))
        k2_done = cnt["ve"]

        if DEV_DUMP:
            @blk.gpsimd
            def _(g):
                wall(g, "ve")
                for oc in range(FF1C):
                    inc(g.dma_start(out=dbg1[oc * P:(oc + 1) * P, t0:t0 + T1],
                                    in_=k82(oc)), "so0", 16)
                inc(g.dma_start(out=dbg2[:, 0:T1], in_=AQ), "so0", 16)
                inc(g.dma_start(out=dbg2[:, 256:256 + T1], in_=CQ), "so0", 16)
                wall(g, "so0")

        if PARANOID & 8:
            emit_barrier()
        ldsn2 = snap("ld")
        mm_2 = {}
        mm_2b = cnt["mm"]

        def mm_2_pred(j):
            return mm_2b + j + 1

        @blk.sync
        def _(sync):
            wall(sync, "ve")
            jj = 0
            for e in range(HC):
                for hf in range(2):
                    if jj >= 2:
                        sync.wait_ge(sems["mm"], mm_2_pred(jj - 2))
                    pdma(sync, bf16a[:, 12288 + (jj % 2) * 2048:
                                     12288 + (jj % 2) * 2048 + 2048],
                         tdn[e][:, hf * 2048:(hf + 1) * 2048], "ld", jj)
                    jj += 1

        ve_b3 = cnt["ve"]
        g2sn3 = snap("g2")
        sosn3 = snap("so")

        @blk.tensor
        def _(tensor):
            tensor.wait_ge(sems["ve"], k2_done)
            jj = 0
            for e in range(HC):
                if e >= 2:
                    tensor.wait_ge(sems["ve"], ve_b3 + (e - 1))
                for hf in range(2):
                    pwait(tensor, "ld", ldsn2, jj)
                    i = None
                    for kk in range(HC):
                        oc = hf * HC + kk
                        tb = bf16a[:, 12288 + (jj % 2) * 2048 + kk * P:
                                   12288 + (jj % 2) * 2048 + (kk + 1) * P]
                        i = tensor.matmul(ps[4 + (e % 2)][:, :T1], tb,
                                          k82(oc),
                                          start=(hf == 0 and kk == 0),
                                          stop=(hf == 1 and kk == HC - 1))
                    mm_2[jj] = inc(i, "mm")
                    jj += 1

        @blk.gpsimd
        def _(g):
            g.wait_ge(sems["ve"], k2_done)
            for e in range(HC + 2):
                if e < HC:
                    if e >= 2:
                        g.wait_ge(sems["ve"], ve_b3 + (e - 1))
                    pdma(g, sqb[e % 2][:, :T1],
                         xin[e * P:(e + 1) * P, t0:t0 + T1], "g2", e)
                if e >= 2:
                    ee = e - 2
                    g.wait_ge(sems["ve"], ve_b3 + (ee + 1))
                    pdma(g, xout[ee * P:(ee + 1) * P, t0:t0 + T1],
                         xn2[ee % 2][:, :T1], "so", ee)

        @blk.vector
        def _(vector):
            for e in range(HC):
                vector.wait_ge(sems["mm"], mm_2[2 * e + 1])
                pwait(vector, "g2", g2sn3, e)
                if e >= 2:
                    pwait(vector, "so", sosn3, e - 2)
                vector.tensor_tensor(out=misc[:, 512:512 + T1],
                                     in0=ps[4 + (e % 2)][:, :T1],
                                     in1=CQ, op=ALU.mult)
                vector.tensor_tensor(out=xn2[e % 2][:, :T1],
                                         in0=misc[:, 512:512 + T1],
                                         in1=sqb[e % 2][:, :T1],
                                         op=ALU.add)
                incd(vector, "ve")

        @blk.gpsimd
        def _(g):
            for ee in (HC - 2, HC - 1):
                g.wait_ge(sems["ve"], ve_b3 + (ee + 1))
                pdma(g, xout[ee * P:(ee + 1) * P, t0:t0 + T1],
                     xn2[ee % 2][:, :T1], "so", ee)
            wall(g, "so0", "so1")

    def emit_output(xcur, col0):
        ldsn = snap("ld")
        sosn = snap("so")
        if DEV_ONE_TILE:
            jlist = [(ci, 0) for ci in range(HC)]
        else:
            jlist = [(j // 2, j % 2) for j in range(2 * HC)]

        @blk.sync
        def _(sync):
            wall(sync, "ve", "sc", "mm", "so0", "so1", "g20", "g21")
            for j, (ci, half) in enumerate(jlist):
                if j >= 2:
                    pwait(sync, "so", sosn, j - 2)
                pdma(sync, xsb[j % 2][:, :T0],
                     xcur[ci * P:(ci + 1) * P,
                          col0 + half * T0:col0 + (half + 1) * T0], "ld", j)

        @blk.gpsimd
        def _(g):
            for j, (ci, half) in enumerate(jlist):
                pwait(g, "ld", ldsn, j)
                pdma(g, y[ci * P:(ci + 1) * P, half * T0:(half + 1) * T0],
                     xsb[j % 2][:, :T0], "so", j)
            wall(g, "so0", "so1")

        @blk.sync
        def _(sync):
            wall(sync, "so0", "so1")

    half_off = None
    emit_wquant()

    seq = []
    if n_sub >= 1:
        seq.append(("ffn0", x0, xa))
    if n_sub >= 2:
        seq.append(("attn0", xa, xb))
    if n_sub >= 3:
        seq.append(("ffn0", xb, xa))
    if n_sub >= 4:
        seq.append(("ffn1", xa, xb))
    if n_sub >= 5:
        seq.append(("attn1", xb, xa))
    if n_sub >= 6:
        seq.append(("ffn1_own", xa, xb))

    OWN0 = None

    _lim = (lambda stop, step: [0] if DEV_ONE_TILE else
            list(range(0, stop, step)))
    for name, xi, xo in seq:
        if name == "ffn0":
            for t0 in _lim(S, T0):
                emit_ffn0_tile(xi, xo, t0)
        elif name == "ffn1":
            for t0 in _lim(S, T1):
                emit_ffn1_tile(xi, xo, t0)
        elif name == "ffn1_own":
            for t0 in range(OWN_OFF, OWN_OFF + OWN, T1):
                emit_ffn1_tile(xi, xo, t0)
        elif name == "attn0":
            for t0 in range(0, S, T0):
                emit_ln(xi, t0, T0, "hilo")
                emit_proj(wqp[0], qhi_d, qlo_d, t0, out_scale=0.08838834764831845)
                emit_proj(wkp[0], khi_d, klo_d, t0)
                emit_proj(wvp[0], vhi_d, vlo_d, t0)
            emit_attn_core(0, 0, S)
            for t0 in range(0, S, T0):
                emit_oproj(wop[0], t0, xi, xo)
        elif name == "attn1":
            for t0 in range(0, S, T0):
                emit_ln(xi, t0, T0, "hilo", density=True)
                emit_proj(wkp[1], khi_d, klo_d, t0, bias_sb=bk_sb)
                emit_proj(wvp[1], vhi_d, vlo_d, t0, bias_sb=bv_sb)
                if OWN_OFF <= t0 < OWN_OFF + OWN:
                    emit_proj(wqp[1], qhi_d, qlo_d, t0, bias_sb=bq_sb, out_scale=0.08838834764831845)
            emit_mask_build(OWN_OFF)
            emit_attn_core(1, OWN_OFF, OWN)
            for t0 in range(OWN_OFF, OWN_OFF + OWN, T0):
                emit_oproj(wop[1], t0, xi, xo, bias_sb=bo_sb)

    xfinal = seq[-1][2] if seq else x0
    emit_output(xfinal, OWN_OFF if n_sub >= 5 else OWN_OFF)

    es.close()
    return nc


OWN_OFF = 0


def _host_pack(W):
    WT = np.ascontiguousarray(W.T.astype(np.float32))
    KC = WT.shape[0] // P
    OC = WT.shape[1] // P
    return np.ascontiguousarray(
        WT.reshape(KC, P, OC, P).transpose(2, 1, 0, 3).reshape(OC, P, KC * P))


def _bias_cols(b):
    n = b.shape[0] // P
    return np.ascontiguousarray(b.reshape(n, P).T.astype(np.float32))


def kernel(**inputs):
    import ml_dtypes

    x = np.asarray(inputs["L0_attn_hidden_states"], np.float32)
    w0p = _host_pack(np.asarray(inputs["L0_ffn_W0"]))
    w1p = _host_pack(np.asarray(inputs["L0_ffn_W1"]))
    b0v = _bias_cols(np.asarray(inputs["L0_ffn_bias0"]))
    b1v = _bias_cols(np.asarray(inputs["L0_ffn_bias1"]))
    packs = {}
    for L in (0, 1):
        pre = "L%d_attn_" % L
        packs[("q", L)] = _host_pack(np.asarray(inputs[pre + "Wq"]))
        packs[("k", L)] = _host_pack(np.asarray(inputs[pre + "Wk"]))
        packs[("v", L)] = _host_pack(np.asarray(inputs[pre + "Wv"]))
        packs[("o", L)] = _host_pack(np.asarray(inputs[pre + "Wo"]))
    bqv = _bias_cols(np.asarray(inputs["L1_attn_bq"]))
    bkv = _bias_cols(np.asarray(inputs["L1_attn_bk"]))
    bvv = _bias_cols(np.asarray(inputs["L1_attn_bv"]))
    bov = _bias_cols(np.asarray(inputs["L1_attn_bo"]))
    wup = _host_pack(np.asarray(inputs["L1_ffn_w_up"]))
    wdp = _host_pack(np.asarray(inputs["L1_ffn_w_down"]))

    idn16 = np.eye(P, dtype=np.float16)
    idn32 = np.eye(P, dtype=np.float32)

    j = np.arange(S)
    nc = _build(N_SUBLAYERS)

    in_maps = []
    for c in range(8):
        b, half = c // 2, c % 2
        perm = np.concatenate([np.arange(half * OWN, half * OWN + OWN),
                               np.arange((1 - half) * OWN,
                                         (1 - half) * OWN + OWN)])
        x0 = np.ascontiguousarray(x[b][perm].T)
        gi = perm[:OWN]
        di = gi[:, None] - perm[None, :]
        local = np.abs(di) <= 64
        strided = (di % 4) == 0
        glob = (gi[:, None] == 0) | (perm[None, :] == 0)
        msl = np.where(local | glob, 0.0, NEGBIG).astype(np.float32)
        mst = np.where(strided | glob, 0.0, NEGBIG).astype(np.float32)
        mdlv = (msl - mst).astype(ml_dtypes.bfloat16)
        mssv = mst.astype(ml_dtypes.bfloat16)
        im = dict(x0=x0, w0p=w0p, w1p=w1p, b0v=b0v, b1v=b1v,
                  wqp0=packs[("q", 0)], wqp1=packs[("q", 1)],
                  wkp0=packs[("k", 0)], wkp1=packs[("k", 1)],
                  wvp0=packs[("v", 0)], wvp1=packs[("v", 1)],
                  wop0=packs[("o", 0)], wop1=packs[("o", 1)],
                  bqv=bqv, bkv=bkv, bvv=bvv, bov=bov,
                  wup=wup, wdp=wdp, mss=mssv, mdl=mdlv,
                  idn16=idn16, idn32=idn32)
        in_maps.append(im)

    res = run_bass_kernel_spmd(nc, in_maps, core_ids=list(range(8)))
    out = np.empty((4, S, H), np.float32)
    for c in range(8):
        b, half = c // 2, c % 2
        out[b, half * OWN:(half + 1) * OWN, :] = res.results[c]["y"].T
    return out


# version 06 (final)
# speedup vs baseline: 1.0203x; 1.0175x over previous
import numpy as np
from contextlib import ExitStack

import concourse.bass as bass
import concourse.mybir as mybir
from concourse.bass_utils import run_bass_kernel_spmd

F32 = mybir.dt.float32
F16 = mybir.dt.float16
BF16 = mybir.dt.bfloat16
AF = mybir.ActivationFunctionType
ALU = mybir.AluOpType
AX = mybir.AxisListType

P = 128
S = 2048
H = 2048
HC = 16
FF0C = 64
FF1C = 32
FF1 = 4096
OWN = 1024
LN_EPS = 1e-5
INV_H = 1.0 / H
T0 = 512
T1 = 256
NEGBIG = -1.0e38

N_SUBLAYERS = 6
DEV_ONE_TILE = False
DEV_DUMP = False
PARANOID = 0x0


def _build(n_sub=N_SUBLAYERS):
    nc = bass.Bass("TRN2", target_bir_lowering=False,
                   detect_race_conditions=False)
    es = ExitStack()
    sems, cnt = {}, {}

    def mksem(name):
        sems[name] = es.enter_context(nc.semaphore(name))
        cnt[name] = 0

    def inc(inst, name, k=1):
        inst.then_inc(sems[name], k)
        cnt[name] += k
        return cnt[name]

    def incd(eng, name, k=1):
        d = eng.drain()
        d.then_inc(sems[name], k)
        cnt[name] += k
        return cnt[name]

    def wall(eng, *names):
        for n in names:
            eng.wait_ge(sems[n], cnt[n])

    def din(name, shape, dt=F32):
        return nc.dram_tensor(name, list(shape), dt, kind="ExternalInput").ap()

    def dint(name, shape, dt=F32):
        return nc.dram_tensor(name, list(shape), dt, kind="Internal").ap()

    x0 = din("x0", [H, S])
    w0p = din("w0p", [FF0C, P, HC * P])
    w1p = din("w1p", [HC, P, FF0C * P])
    b0v = din("b0v", [P, FF0C])
    b1v = din("b1v", [P, HC])
    wqp = [din("wqp0", [HC, P, HC * P]), din("wqp1", [HC, P, HC * P])]
    wkp = [din("wkp0", [HC, P, HC * P]), din("wkp1", [HC, P, HC * P])]
    wvp = [din("wvp0", [HC, P, HC * P]), din("wvp1", [HC, P, HC * P])]
    wop = [din("wop0", [HC, P, HC * P]), din("wop1", [HC, P, HC * P])]
    bqv = din("bqv", [P, HC])
    bkv = din("bkv", [P, HC])
    bvv = din("bvv", [P, HC])
    bov = din("bov", [P, HC])
    wup = din("wup", [FF1C, P, HC * P])
    wdp = din("wdp", [HC, P, FF1C * P])
    mss = din("mss", [OWN, S], BF16)
    mdl = din("mdl", [OWN, S], BF16)
    idn16 = din("idn16", [P, P], F16)
    idn32 = din("idn32", [P, P], F32)

    y = nc.dram_tensor("y", [H, OWN], F32, kind="ExternalOutput").ap()

    xa = dint("xa", [H, S])
    xb = dint("xb", [H, S])
    qhi_d = dint("q_hi", [H, S], F16)
    qlo_d = dint("q_lo", [H, S], F16)
    khi_d = dint("k_hi", [H, S], F16)
    klo_d = dint("k_lo", [H, S], F16)
    vhi_d = dint("v_hi", [H, S], F16)
    vlo_d = dint("v_lo", [H, S], F16)
    athi_d = dint("at_hi", [H, S], F16)
    atlo_d = dint("at_lo", [H, S], F16)
    tup = dint("tup", [FF1C, P, HC * P], BF16)
    tdn = dint("tdn", [HC, P, FF1C * P], BF16)
    hscr = dint("hscr", [FF1, S])
    _dd = (lambda n, sh, dt=F32: nc.dram_tensor(n, list(sh), dt,
           kind="ExternalOutput").ap()) if DEV_DUMP else dint
    dbg1 = _dd("dbg1", [FF1, S], BF16)
    dbg2 = _dd("dbg2", [P, 2048])
    dbgX = _dd("dbgX", [P, 8192], F16)
    dbgD = _dd("dbgD", [P, 32768], F16)

    def sbt(name, shape, dt=F32):
        return es.enter_context(nc.sbuf_tensor(name, list(shape), dt))

    b0_sb = sbt("b0_sb", [P, FF0C])
    b1_sb = sbt("b1_sb", [P, HC])
    bq_sb = sbt("bq_sb", [P, HC])
    bk_sb = sbt("bk_sb", [P, HC])
    bv_sb = sbt("bv_sb", [P, HC])
    bo_sb = sbt("bo_sb", [P, HC])
    id16 = sbt("id16", [P, P], F16)
    id32 = sbt("id32", [P, P], F32)
    onescol = sbt("onescol", [1, P])
    ones128 = sbt("ones128", [P, 1])
    gcol = sbt("gcol", [P, 1])
    dacc = sbt("dacc", [P, 1])
    qsc = sbt("qsc", [P, 16])

    f16big = sbt("f16big", [P, 32768], F16)
    f16b = sbt("f16b", [P, 8192], F16)
    bf16a = sbt("bf16a", [P, 16384], BF16)
    bf16st = sbt("bf16st", [P, 2048], BF16)
    f32w = sbt("f32w", [P, 4096])
    f16w = sbt("f16w", [P, 8192], F16)
    f32g = sbt("f32g", [P, 7168])
    rows = sbt("rows", [1, 1024])
    rows2 = sbt("rows2", [1, 1024])

    ps = [es.enter_context(nc.psum_tensor(f"ps{i}", [P, 512], F32))
          for i in range(7)]
    psT = es.enter_context(nc.psum_tensor("psT", [P, 1024], F16))

    for s_ in ("ld0", "ld1", "g20", "g21", "mm", "sc", "ve", "so0", "so1"):
        mksem(s_)

    def snap(base):
        return (cnt[base + "0"], cnt[base + "1"])

    def pdma(eng, out, in_, base, j):
        return inc(eng.dma_start(out=out, in_=in_), base + str(j % 2), 16)

    def pval(base, sn, j):
        return base + str(j % 2), sn[j % 2] + 16 * (j // 2 + 1)

    def pwait(eng, base, sn, j):
        n, v = pval(base, sn, j)
        eng.wait_ge(sems[n], v)

    blk = es.enter_context(nc.Block())

    xsb = [f32g[:, 0:512], f32g[:, 512:1024]]
    sqb = [f32g[:, 1024:1536], f32g[:, 1536:2048]]
    A_ = f32g[:, 2048:2560]
    B_ = f32g[:, 2560:3072]
    xn2 = [f32g[:, 3072:3584], f32g[:, 3584:4096]]
    pe_ = f32g[:, 4096:6144]
    misc = f32g[:, 6144:7168]

    @blk.sync
    def _(sync):
        for dst, src in ((b0_sb, b0v), (b1_sb, b1v), (bq_sb, bqv),
                         (bk_sb, bkv), (bv_sb, bvv), (bo_sb, bov),
                         (id16, idn16), (id32, idn32)):
            inc(sync.dma_start(out=dst[:], in_=src[:]), "ld0", 16)

    @blk.vector
    def _(vector):
        wall(vector, "ld0", "ld1")
        vector.memset(onescol[:], 1.0)
        vector.memset(ones128[:], 1.0)
        vector.memset(misc[:, 1016:1024], 0.0)
        vector.memset(dacc[:], 0.0)
        incd(vector, "ve")

    def emit_barrier():
        for sec in (blk.sync, blk.gpsimd, blk.scalar, blk.vector, blk.tensor):
            @sec
            def _(eng):
                wall(eng, "ld0", "ld1", "g20", "g21", "so0", "so1",
                     "mm", "sc", "ve")

    def bcast_row(row_ap, width, dst_ap):
        @blk.tensor
        def _(tensor):
            wall(tensor, "ve", "sc")
            inc(tensor.matmul(ps[6][:, :width], onescol[:, :], row_ap,
                              start=True, stop=True), "mm")

        @blk.vector
        def _(vector):
            wall(vector, "mm")
            vector.tensor_copy(out=dst_ap, in_=ps[6][:, :width])
            incd(vector, "ve")

    def col_to_row(col_ap, row_dst_ap):
        @blk.tensor
        def _(tensor):
            wall(tensor, "ve", "sc")
            inc(tensor.transpose(out=ps[6][0:1, 0:P], in_=col_ap,
                                 identity=id32[:]), "mm")

        @blk.vector
        def _(vector):
            wall(vector, "mm")
            vector.tensor_copy(out=row_dst_ap, in_=ps[6][0:1, 0:P])
            incd(vector, "ve")

    def emit_ln(xin, t0, T, mode, density=False):
        ldsn, sc_b, mm_b = snap("ld"), cnt["sc"], cnt["mm"]

        @blk.sync
        def _(sync):
            wall(sync, "ve", "sc", "mm")
            for ci in range(HC):
                if ci >= 2:
                    sync.wait_ge(sems["mm"], mm_b + (ci - 1))
                pdma(sync, xsb[ci % 2][:, :T],
                     xin[ci * P:(ci + 1) * P, t0:t0 + T], "ld", ci)

        @blk.scalar
        def _(scalar):
            for ci in range(HC):
                pwait(scalar, "ld", ldsn, ci)
                if ci >= 2:
                    scalar.wait_ge(sems["mm"], mm_b + (ci - 1))
                scalar.activation(out=sqb[ci % 2][:, :T],
                                      in_=xsb[ci % 2][:, :T], func=AF.Square)
                incd(scalar, "sc")

        @blk.tensor
        def _(tensor):
            for ci in range(HC):
                tensor.wait_ge(sems["sc"], sc_b + ci + 1)
                tensor.matmul(ps[0][0:1, :T], ones128[:, :1],
                              xsb[ci % 2][:, :T],
                              start=(ci == 0), stop=(ci == HC - 1))
                inc(tensor.matmul(ps[1][0:1, :T], ones128[:, :1],
                                  sqb[ci % 2][:, :T],
                                  start=(ci == 0), stop=(ci == HC - 1)), "mm")

        @blk.vector
        def _(vector):
            wall(vector, "mm")
            r, r2 = rows[0:1, :], rows2[0:1, :]
            vector.tensor_single_scalar(out=r2[:, 0:T], in_=ps[0][0:1, :T],
                                        scalar=INV_H, op=ALU.mult)
            vector.tensor_single_scalar(out=r[:, 0:T], in_=ps[1][0:1, :T],
                                        scalar=INV_H, op=ALU.mult)
            vector.tensor_tensor(out=r2[:, 512:512 + T], in0=r2[:, 0:T],
                                 in1=r2[:, 0:T], op=ALU.mult)
            vector.tensor_tensor(out=r[:, 512:512 + T], in0=r[:, 0:T],
                                 in1=r2[:, 512:512 + T], op=ALU.subtract)
            vector.tensor_single_scalar(out=r[:, 0:T],
                                            in_=r[:, 512:512 + T],
                                            scalar=LN_EPS, op=ALU.add)
            incd(vector, "ve")

        @blk.scalar
        def _(scalar):
            wall(scalar, "ve")
            scalar.activation(out=rows2[0:1, 512:512 + T],
                                  in_=rows[0:1, 0:T], func=AF.Sqrt)
            incd(scalar, "sc")

        @blk.vector
        def _(vector):
            wall(vector, "sc")
            r, r2 = rows[0:1, :], rows2[0:1, :]
            vector.reciprocal(out=r2[:, 512:512 + T], in_=r2[:, 512:512 + T])
            vector.drain()
            vector.tensor_tensor(out=r[:, 512:512 + T], in0=r2[:, 512:512 + T],
                                 in1=r2[:, 512:512 + T], op=ALU.mult)
            vector.drain()
            vector.tensor_tensor(out=r[:, 512:512 + T], in0=r[:, 0:T],
                                 in1=r[:, 512:512 + T], op=ALU.mult)
            vector.drain()
            vector.tensor_scalar(out=r[:, 512:512 + T], in0=r[:, 512:512 + T],
                                 scalar1=-0.5, scalar2=1.5,
                                 op0=ALU.mult, op1=ALU.add)
            vector.drain()
            vector.tensor_tensor(out=r[:, 0:T], in0=r2[:, 512:512 + T],
                                 in1=r[:, 512:512 + T], op=ALU.mult)
            vector.drain()
            vector.tensor_tensor(out=r2[:, 512:512 + T], in0=r2[:, 0:T],
                                 in1=r[:, 0:T], op=ALU.mult)
            vector.drain()
            vector.tensor_single_scalar(out=r2[:, 512:512 + T],
                                            in_=r2[:, 512:512 + T],
                                            scalar=-1.0, op=ALU.mult)
            incd(vector, "ve")

        @blk.tensor
        def _(tensor):
            wall(tensor, "ve")
            tensor.matmul(ps[2][:, :T], onescol[:, :], rows[0:1, 0:T],
                          start=True, stop=True)
            inc(tensor.matmul(ps[3][:, :T], onescol[:, :],
                              rows2[0:1, 512:512 + T], start=True, stop=True),
                "mm")

        @blk.vector
        def _(vector):
            wall(vector, "mm")
            vector.tensor_copy(out=A_[:, :T], in_=ps[2][:, :T])
            vector.tensor_copy(out=B_[:, :T], in_=ps[3][:, :T])
            incd(vector, "ve")

        if mode == "stats":
            return

        g2sn, ve_b = snap("g2"), cnt["ve"]
        ops = {"f16": 2, "hilo": 3, "absmax": 3}[mode] + (5 if density else 0)

        @blk.gpsimd
        def _(g):
            wall(g, "ve")
            for ci in range(HC):
                if ci >= 2:
                    g.wait_ge(sems["ve"], ve_b + (ci - 1) * ops)
                pdma(g, xsb[ci % 2][:, :T],
                     xin[ci * P:(ci + 1) * P, t0:t0 + T], "g2", ci)

        @blk.vector
        def _(vector):
            wall(vector, "so0", "so1")
            if mode == "absmax":
                vector.memset(misc[:, 0:T], 0.0)
            for ci in range(HC):
                pwait(vector, "g2", g2sn, ci)
                vector.tensor_tensor(out=xn2[ci % 2][:, :T],
                                     in0=xsb[ci % 2][:, :T], in1=A_[:, :T],
                                     op=ALU.mult)
                if mode == "f16":
                    vector.tensor_tensor(out=f16b[:, ci * T0:ci * T0 + T],
                                             in0=xn2[ci % 2][:, :T],
                                             in1=B_[:, :T], op=ALU.add)
                    incd(vector, "ve")
                    vector.tensor_copy(out=misc[:, 1023:1024],
                                           in_=misc[:, 1023:1024])
                    incd(vector, "ve")
                elif mode == "absmax":
                    vector.tensor_tensor(out=xn2[ci % 2][:, :T],
                                         in0=xn2[ci % 2][:, :T],
                                         in1=B_[:, :T], op=ALU.add)
                    vector.tensor_single_scalar(out=sqb[ci % 2][:, :T],
                                                in_=xn2[ci % 2][:, :T],
                                                scalar=-1.0, op=ALU.mult)
                    vector.tensor_tensor(out=misc[:, 0:T],
                                             in0=misc[:, 0:T],
                                             in1=xn2[ci % 2][:, :T],
                                             op=ALU.max)
                    incd(vector, "ve")
                    vector.tensor_tensor(out=misc[:, 0:T],
                                             in0=misc[:, 0:T],
                                             in1=sqb[ci % 2][:, :T],
                                             op=ALU.max)
                    incd(vector, "ve")
                    vector.tensor_copy(out=misc[:, 1023:1024],
                                           in_=misc[:, 1023:1024])
                    incd(vector, "ve")
                else:
                    vector.tensor_tensor(out=xn2[ci % 2][:, :T],
                                         in0=xn2[ci % 2][:, :T],
                                         in1=B_[:, :T], op=ALU.add)
                    vector.tensor_copy(out=f16b[:, ci * T0:ci * T0 + T],
                                       in_=xn2[ci % 2][:, :T])
                    k = vector.tensor_tensor(
                        out=f16big[:, ci * T0:ci * T0 + T],
                        in0=xn2[ci % 2][:, :T],
                        in1=f16b[:, ci * T0:ci * T0 + T], op=ALU.subtract)
                    inc(k, "ve")
                    vector.tensor_copy(out=misc[:, 1023:1024],
                                           in_=misc[:, 1023:1024])
                    incd(vector, "ve")
                    vector.tensor_copy(out=misc[:, 1023:1024],
                                           in_=misc[:, 1023:1024])
                    incd(vector, "ve")
                    if density:
                        vector.tensor_single_scalar(out=sqb[ci % 2][:, :T],
                                                    in_=xn2[ci % 2][:, :T],
                                                    scalar=0.1, op=ALU.is_gt)
                        vector.tensor_single_scalar(out=sqb[(ci + 1) % 2][:, :T],
                                                    in_=xn2[ci % 2][:, :T],
                                                    scalar=-0.1, op=ALU.is_lt)
                        vector.tensor_tensor(out=sqb[ci % 2][:, :T],
                                             in0=sqb[ci % 2][:, :T],
                                             in1=sqb[(ci + 1) % 2][:, :T],
                                             op=ALU.add)
                        vector.reduce_sum(out=sqb[ci % 2][:, :1],
                                          in_=sqb[ci % 2][:, :T], axis=AX.X)
                        vector.drain()
                        vector.tensor_tensor(out=dacc[:], in0=dacc[:],
                                                 in1=sqb[ci % 2][:, :1],
                                                 op=ALU.add)
                        incd(vector, "ve")
                        for _pad in range(4):
                            vector.tensor_copy(out=misc[:, 1023:1024],
                                                   in_=misc[:, 1023:1024])
                            incd(vector, "ve")
        assert cnt["ve"] == ve_b + HC * ops, (cnt["ve"], ve_b, ops)


    def stream_w(srcs, W, free_pred, pre_cb=None):
        ldsn = snap("ld")
        lo_r = {}
        mm_guard = cnt["mm"]

        @blk.sync
        def _(sync):
            wall(sync, "sc", "ve")
            for j in range(len(srcs)):
                if j >= 2:
                    sync.wait_ge(sems["mm"], free_pred(j - 2))
                pdma(sync, f32w[:, (j % 2) * 2048:(j % 2) * 2048 + W],
                     srcs[j], "ld", j)

        @blk.vector
        def _(vector):
            vector.wait_ge(sems["mm"], mm_guard)
            for j in range(len(srcs)):
                if pre_cb is not None:
                    pre_cb(vector, j)
                pwait(vector, "ld", ldsn, j)
                if j >= 2:
                    vector.wait_ge(sems["mm"], free_pred(j - 2))
                vector.tensor_copy(
                    out=f16w[:, (j % 2) * 4096:(j % 2) * 4096 + W],
                    in_=f32w[:, (j % 2) * 2048:(j % 2) * 2048 + W])
                vector.tensor_tensor(
                    out=f16w[:, (j % 2) * 4096 + 2048:
                             (j % 2) * 4096 + 2048 + W],
                    in0=f32w[:, (j % 2) * 2048:(j % 2) * 2048 + W],
                    in1=f16w[:, (j % 2) * 4096:(j % 2) * 4096 + W],
                    op=ALU.subtract)
                lo_r[j] = incd(vector, "ve")
            if pre_cb is not None:
                for j in range(len(srcs), len(srcs) + 8):
                    pre_cb(vector, j)
        return lo_r

    def whi(j, kk):
        b = (j % 2) * 4096
        return f16w[:, b + kk * P:b + (kk + 1) * P]

    def wlo(j, kk):
        b = (j % 2) * 4096 + 2048
        return f16w[:, b + kk * P:b + (kk + 1) * P]

    def emit_ffn0_tile(xin, xout, t0):
        if PARANOID & 1:
            emit_barrier()
        emit_ln(xin, t0, T0, "f16")
        if PARANOID & 2:
            emit_barrier()

        def x16(ci):
            return f16b[:, ci * T0:(ci + 1) * T0]

        def d16(h):
            return f16big[:, h * T0:(h + 1) * T0]

        if DEV_DUMP and t0 == 0:
            vehere = cnt["ve"]

            @blk.gpsimd
            def _(g):
                g.wait_ge(sems["ve"], vehere)
                inc(g.dma_start(out=dbgX[:], in_=f16b[:]), "so0", 16)

        mm_b = cnt["mm"]
        lo_r = stream_w([w0p[h] for h in range(FF0C)], HC * P,
                        lambda j: mm_b + (j + 1))
        sc_g = cnt["sc"]
        mm_h = {}

        @blk.tensor
        def _(tensor):
            tensor.wait_ge(sems["sc"], sc_g)
            for h in range(FF0C):
                tensor.wait_ge(sems["ve"], lo_r[h])
                if h >= 2:
                    tensor.wait_ge(sems["sc"], sc_g + (h - 1))
                pd = ps[4 + (h % 2)]
                i = None
                for pss, wf in ((0, whi), (1, wlo)):
                    for kk in range(HC):
                        i = tensor.matmul(pd[:, :T0], wf(h, kk), x16(kk),
                                          start=(pss == 0 and kk == 0),
                                          stop=(pss == 1 and kk == HC - 1))
                mm_h[h] = inc(i, "mm")

        @blk.scalar
        def _(scalar):
            for h in range(FF0C):
                scalar.wait_ge(sems["mm"], mm_h[h])
                scalar.activation(out=d16(h), in_=ps[4 + (h % 2)][:, :T0],
                                      func=AF.Gelu, bias=b0_sb[:, h:h + 1],
                                      scale=1.0)
                incd(scalar, "sc")

        if DEV_DUMP and t0 == 0:
            schere = cnt["sc"]

            @blk.gpsimd
            def _(g):
                g.wait_ge(sems["sc"], schere)
                inc(g.dma_start(out=dbgD[:], in_=f16big[:]), "so0", 16)

        if PARANOID & 4:
            emit_barrier()

        mm_b2 = cnt["mm"]
        g2sn2 = snap("g2")
        sosn2 = snap("so")
        evac_ve = {}
        qlist = [(e, qtr) for e in range(HC) for qtr in range(4)]

        def e_evac(vector, j):
            if j % 4 != 0:
                return
            e = j // 4 - 2
            if e < 0 or e >= HC:
                return
            vector.wait_ge(sems["mm"], mm_b2 + 4 * e + 4)
            n_, v_ = pval("g2", g2sn2, e)
            vector.wait_ge(sems[n_], v_)
            if e >= 2:
                n_, v_ = pval("so", sosn2, e - 2)
                vector.wait_ge(sems[n_], v_)
            e16 = f16b[:, (e % 2) * T0:(e % 2) * T0 + T0]
            vector.tensor_single_scalar(
                out=e16, in_=ps[6 if e % 2 == 0 else 0][:, :T0],
                scalar=b1_sb[:, e:e + 1], op=ALU.add)
            vector.scalar_tensor_tensor(
                out=xn2[e % 2][:, :T0], in0=e16, scalar=0.5,
                in1=xsb[e % 2][:, :T0], op0=ALU.mult, op1=ALU.add)
            evac_ve[e] = incd(vector, "ve")

        lo2 = stream_w(
            [w1p[e][:, qtr * 2048:(qtr + 1) * 2048] for e, qtr in qlist],
            HC * P, lambda j: mm_b2 + (j + 1), pre_cb=e_evac)
        mm_e = {}

        @blk.tensor
        def _(tensor):
            for j, (e, qtr) in enumerate(qlist):
                if j < 4:
                    tensor.wait_ge(sems["sc"], sc_g + (qtr + 1) * HC)
                tensor.wait_ge(sems["ve"], lo2[j])
                if qtr == 0 and e >= 2:
                    tensor.wait_ge(sems["ve"], evac_ve[e - 2])
                pd = ps[6 if e % 2 == 0 else 0]
                i = None
                for pss, wf in ((0, whi), (1, wlo)):
                    for kk in range(HC):
                        hh = qtr * HC + kk
                        i = tensor.matmul(pd[:, :T0], wf(j, kk), d16(hh),
                                          start=(qtr == 0 and pss == 0
                                                 and kk == 0),
                                          stop=(qtr == 3 and pss == 1
                                                and kk == HC - 1))
                mm_e[j] = inc(i, "mm")

        @blk.gpsimd
        def _(g):
            wall(g, "so0", "so1")
            g.wait_ge(sems["sc"], sc_g)
            g.wait_ge(sems["ve"], lo_r[0])
            for e in range(HC + 2):
                if e < HC:
                    if e >= 2:
                        g.wait_ge(sems["ve"], evac_ve[e - 2])
                    pdma(g, xsb[e % 2][:, :T0],
                         xin[e * P:(e + 1) * P, t0:t0 + T0], "g2", e)
                if e >= 2:
                    ee = e - 2
                    g.wait_ge(sems["ve"], evac_ve[ee])
                    pdma(g, xout[ee * P:(ee + 1) * P, t0:t0 + T0],
                         xn2[ee % 2][:, :T0], "so", ee)
            wall(g, "so0", "so1")

    def emit_proj(pack, dst_hi, dst_lo, t0, bias_sb=None, out_scale=None):
        if PARANOID & 8:
            emit_barrier()

        def xhi(ci):
            return f16b[:, ci * T0:(ci + 1) * T0]

        def xlo(ci):
            return f16big[:, ci * T0:(ci + 1) * T0]

        def sth(oc):
            b = 8192 + (oc % 2) * 512
            return f16big[:, b:b + T0]

        def stl(oc):
            b = 9216 + (oc % 2) * 512
            return f16big[:, b:b + T0]

        mm_b = cnt["mm"]
        sosn = snap("so")
        evac_ve = {}

        def p_evac(vector, j):
            oc = j - 2
            if oc < 0 or oc >= HC:
                return
            vector.wait_ge(sems["mm"], mm_b + oc + 1)
            if oc >= 2:
                jj = oc - 2
                vector.wait_ge(sems["so" + str(jj % 2)],
                               sosn[jj % 2] + 32 * (jj // 2 + 1))
            pd = ps[4 + (oc % 2)][:, :T0]
            src = pd
            if bias_sb is not None:
                vector.tensor_single_scalar(out=misc[:, :T0], in_=src,
                                            scalar=bias_sb[:, oc:oc + 1],
                                            op=ALU.add)
                src = misc[:, :T0]
            if out_scale is not None:
                vector.tensor_single_scalar(out=misc[:, :T0], in_=src,
                                            scalar=float(out_scale),
                                            op=ALU.mult)
                src = misc[:, :T0]
            vector.tensor_copy(out=sth(oc), in_=src)
            vector.tensor_tensor(
                out=stl(oc), in0=src, in1=sth(oc), op=ALU.subtract)
            evac_ve[oc] = incd(vector, "ve")

        lo_r = stream_w([pack[oc] for oc in range(HC)], HC * P,
                        lambda j: mm_b + (j + 1), pre_cb=p_evac)

        @blk.tensor
        def _(tensor):
            for oc in range(HC):
                tensor.wait_ge(sems["ve"], lo_r[oc])
                if oc >= 2:
                    tensor.wait_ge(sems["ve"], evac_ve[oc - 2])
                pd = ps[4 + (oc % 2)]
                i = None
                n = 0
                for wf, xf in ((whi, xhi), (whi, xlo), (wlo, xhi)):
                    for kk in range(HC):
                        i = tensor.matmul(pd[:, :T0], wf(oc, kk), xf(kk),
                                          start=(n == 0),
                                          stop=(n == 3 * HC - 1))
                        n += 1
                inc(i, "mm")

        @blk.gpsimd
        def _(g):
            for oc in range(HC):
                g.wait_ge(sems["ve"], evac_ve[oc])
                inc(g.dma_start(out=dst_hi[oc * P:(oc + 1) * P, t0:t0 + T0],
                                in_=sth(oc)), "so" + str(oc % 2), 16)
                inc(g.dma_start(out=dst_lo[oc * P:(oc + 1) * P, t0:t0 + T0],
                                in_=stl(oc)), "so" + str(oc % 2), 16)

    def emit_oproj(pack, t0, xin, xout, bias_sb=None):
        if PARANOID & 8:
            emit_barrier()
        g2at = cnt["g20"]

        @blk.gpsimd
        def _(g):
            wall(g, "ve", "mm", "so0", "so1")
            for ci in range(HC):
                inc(g.dma_start(out=f16big[:, ci * T0:(ci + 1) * T0],
                                in_=athi_d[ci * P:(ci + 1) * P, t0:t0 + T0]),
                    "g20", 16)
                inc(g.dma_start(
                    out=f16big[:, 16384 + ci * T0:16384 + (ci + 1) * T0],
                    in_=atlo_d[ci * P:(ci + 1) * P, t0:t0 + T0]), "g20", 16)

        def ahi(ci):
            return f16big[:, ci * T0:(ci + 1) * T0]

        def alo(ci):
            return f16big[:, 16384 + ci * T0:16384 + (ci + 1) * T0]

        mm_b = cnt["mm"]
        g2sn2 = snap("g2")
        sosn = snap("so")
        evac_ve = {}

        def o_evac(vector, j):
            oc = j - 2
            if oc < 0 or oc >= HC:
                return
            vector.wait_ge(sems["mm"], mm_b + oc + 1)
            n_, v_ = pval("g2", g2sn2, oc)
            vector.wait_ge(sems[n_], v_)
            if oc >= 2:
                n_, v_ = pval("so", sosn, oc - 2)
                vector.wait_ge(sems[n_], v_)
            pd = ps[4 + (oc % 2)][:, :T0]
            if bias_sb is not None:
                vector.tensor_single_scalar(out=misc[:, :T0], in_=pd,
                                            scalar=bias_sb[:, oc:oc + 1],
                                            op=ALU.add)
                vector.tensor_tensor(
                    out=xn2[oc % 2][:, :T0], in0=misc[:, :T0],
                    in1=xsb[oc % 2][:, :T0], op=ALU.add)
                evac_ve[oc] = incd(vector, "ve")
            else:
                vector.tensor_tensor(
                    out=xn2[oc % 2][:, :T0], in0=pd,
                    in1=xsb[oc % 2][:, :T0], op=ALU.add)
                evac_ve[oc] = incd(vector, "ve")

        lo_r = stream_w([pack[oc] for oc in range(HC)], HC * P,
                        lambda j: mm_b + (j + 1), pre_cb=o_evac)

        @blk.tensor
        def _(tensor):
            tensor.wait_ge(sems["g20"], g2at + 32 * HC)
            for oc in range(HC):
                tensor.wait_ge(sems["ve"], lo_r[oc])
                if oc >= 2:
                    tensor.wait_ge(sems["ve"], evac_ve[oc - 2])
                pd = ps[4 + (oc % 2)]
                i = None
                n = 0
                for wf, xf in ((whi, ahi), (whi, alo), (wlo, ahi)):
                    for kk in range(HC):
                        i = tensor.matmul(pd[:, :T0], wf(oc, kk), xf(kk),
                                          start=(n == 0),
                                          stop=(n == 3 * HC - 1))
                        n += 1
                inc(i, "mm")

        @blk.gpsimd
        def _(g):
            for oc in range(HC + 2):
                if oc < HC:
                    if oc >= 2:
                        g.wait_ge(sems["ve"], evac_ve[oc - 2])
                    pdma(g, xsb[oc % 2][:, :T0],
                         xin[oc * P:(oc + 1) * P, t0:t0 + T0], "g2", oc)
                if oc >= 2:
                    ee = oc - 2
                    g.wait_ge(sems["ve"], evac_ve[ee])
                    pdma(g, xout[ee * P:(ee + 1) * P, t0:t0 + T0],
                         xn2[ee % 2][:, :T0], "so", ee)
            wall(g, "so0", "so1")

    def emit_attn_core(layer, q0, q_tokens):
        khi = f16big[:, 0:2048]
        klo = f16big[:, 2048:4096]
        vhi = f16big[:, 4096:6144]
        vlo = f16big[:, 6144:8192]
        vthi = f16big[:, 8192:10240]
        vtlo = f16big[:, 10240:12288]
        pthi = f16big[:, 12288:20480]
        ptlo = f16big[:, 20480:28672]
        phi_ = f16big[:, 28672:30720]
        plo_ = f16big[:, 30720:32768]
        pt_hi_v = pthi.rearrange("p (j q) -> p j q", q=T0)
        pt_lo_v = ptlo.rearrange("p (j q) -> p j q", q=T0)
        vt_hi_v = vthi.rearrange("p (j d) -> p j d", d=P)
        vt_lo_v = vtlo.rearrange("p (j d) -> p j d", d=P)

        n_qg = q_tokens // T0
        for h in range(NHD):
            if PARANOID & 8:
                emit_barrier()
            ld_b = cnt["ld0"]

            @blk.sync
            def _(sync):
                wall(sync, "mm", "ve", "sc", "so0", "so1")
                for dst, src in ((khi, khi_d), (klo, klo_d),
                                 (vhi, vhi_d), (vlo, vlo_d)):
                    inc(sync.dma_start(out=dst,
                                       in_=src[h * P:(h + 1) * P, :]),
                        "ld0", 16)

            mm_b = cnt["mm"]
            ve_b = cnt["ve"]

            @blk.tensor
            def _(tensor):
                tensor.wait_ge(sems["ld0"], ld_b + 64)
                wall(tensor, "ve")
                for gi in range(4):
                    src = (vhi, vhi, vlo, vlo)[gi]
                    grp = gi % 2
                    if gi >= 1:
                        tensor.wait_ge(sems["ve"], ve_b + gi)
                    i = None
                    for j in range(8):
                        jj = grp * 8 + j
                        i = tensor.transpose(out=psT[:, j * P:(j + 1) * P],
                                             in_=src[:, jj * P:(jj + 1) * P],
                                             identity=id16[:])
                    inc(i, "mm")

            @blk.vector
            def _(vector):
                for gi in range(4):
                    dst = (vthi, vthi, vtlo, vtlo)[gi]
                    grp = gi % 2
                    vector.wait_ge(sems["mm"], mm_b + gi + 1)
                    vector.tensor_copy(
                        out=dst[:, grp * 1024:(grp + 1) * 1024],
                        in_=psT[:, 0:1024])
                    incd(vector, "ve")

            for qg in range(n_qg):
                qg0 = q0 + qg * T0
                for qt in range(4):
                    if PARANOID & 16:
                        emit_barrier()
                    qt0 = qg0 + qt * P
                    g2_b = cnt["g20"]

                    @blk.gpsimd
                    def _(g):
                        wall(g, "mm", "sc")
                        inc(g.dma_start(
                            out=f16b[:, 0:P],
                            in_=qhi_d[h * P:(h + 1) * P, qt0:qt0 + P]),
                            "g20", 16)
                        inc(g.dma_start(
                            out=f16b[:, P:2 * P],
                            in_=qlo_d[h * P:(h + 1) * P, qt0:qt0 + P]),
                            "g20", 16)

                    @blk.tensor
                    def _(tensor):
                        tensor.wait_ge(sems["g20"], g2_b + 32)
                        wall(tensor, "sc")
                        i = None
                        for pss, (qa, ka) in enumerate(
                                ((f16b[:, 0:P], khi),
                                 (f16b[:, P:2 * P], khi),
                                 (f16b[:, 0:P], klo))):
                            for jb in range(4):
                                i = tensor.matmul(
                                    ps[jb][:, :T0], qa,
                                    ka[:, jb * T0:(jb + 1) * T0],
                                    start=(pss == 0), stop=(pss == 2))
                        inc(i, "mm")

                    @blk.vector
                    def _(vector):
                        wall(vector, "mm")
                        if layer == 1:
                            mrow = (qt0 - q0) // P
                            mview = bf16a[:, mrow * 2048:(mrow + 1) * 2048]
                            for jb in range(4):
                                vector.tensor_tensor(
                                    out=ps[jb][:, :T0], in0=ps[jb][:, :T0],
                                    in1=mview[:, jb * T0:(jb + 1) * T0],
                                    op=ALU.add)
                        for jb in range(4):
                            vector.reduce_max(out=misc[:, jb:jb + 1],
                                              in_=ps[jb][:, :T0], axis=AX.X)
                        vector.drain()
                        vector.tensor_tensor(out=misc[:, 0:1],
                                             in0=misc[:, 0:1],
                                             in1=misc[:, 1:2], op=ALU.max)
                        vector.drain()
                        vector.tensor_tensor(out=misc[:, 2:3],
                                             in0=misc[:, 2:3],
                                             in1=misc[:, 3:4], op=ALU.max)
                        vector.drain()
                        vector.tensor_tensor(out=misc[:, 0:1],
                                             in0=misc[:, 0:1],
                                             in1=misc[:, 2:3], op=ALU.max)
                        vector.drain()
                        vector.tensor_single_scalar(
                            out=misc[:, 4:5], in_=misc[:, 0:1],
                            scalar=-1.0, op=ALU.mult)
                        incd(vector, "ve")

                    @blk.scalar
                    def _(scalar):
                        wall(scalar, "ve")
                        for jb in range(4):
                            scalar.activation(
                                out=pe_[:, jb * T0:(jb + 1) * T0],
                                in_=ps[jb][:, :T0], func=AF.Exp,
                                bias=misc[:, 4:5], scale=1.0,
                                accum_out=misc[:, 5 + jb:6 + jb])
                            incd(scalar, "sc")

                    @blk.vector
                    def _(vector):
                        wall(vector, "sc")
                        vector.tensor_tensor(out=misc[:, 9:10],
                                             in0=misc[:, 5:6],
                                             in1=misc[:, 6:7], op=ALU.add)
                        vector.tensor_tensor(out=misc[:, 10:11],
                                             in0=misc[:, 7:8],
                                             in1=misc[:, 8:9], op=ALU.add)
                        vector.drain()
                        vector.tensor_tensor(out=misc[:, 9:10],
                                             in0=misc[:, 9:10],
                                             in1=misc[:, 10:11], op=ALU.add)
                        vector.drain()
                        vector.reciprocal(out=misc[:, 10:11],
                                          in_=misc[:, 9:10])
                        vector.drain()
                        vector.tensor_tensor(out=misc[:, 11:12],
                                             in0=misc[:, 9:10],
                                             in1=misc[:, 10:11], op=ALU.mult)
                        vector.drain()
                        vector.tensor_scalar(out=misc[:, 11:12],
                                             in0=misc[:, 11:12], scalar1=-1.0,
                                             scalar2=2.0, op0=ALU.mult,
                                             op1=ALU.add)
                        vector.drain()
                        vector.tensor_tensor(out=misc[:, 10:11],
                                             in0=misc[:, 10:11],
                                             in1=misc[:, 11:12], op=ALU.mult)
                        vector.drain()
                        vector.tensor_single_scalar(out=pe_[:, :],
                                                    in_=pe_[:, :],
                                                    scalar=misc[:, 10:11],
                                                    op=ALU.mult)
                        vector.tensor_copy(out=phi_[:], in_=pe_[:])
                        vector.tensor_tensor(out=plo_[:], in0=pe_[:],
                                                 in1=phi_[:],
                                                 op=ALU.subtract)
                        incd(vector, "ve")

                    mm_b3 = cnt["mm"]
                    ve_b3 = cnt["ve"]

                    @blk.tensor
                    def _(tensor):
                        wall(tensor, "ve")
                        for gi in range(4):
                            src = (phi_, phi_, plo_, plo_)[gi]
                            grp = gi % 2
                            if gi >= 1:
                                tensor.wait_ge(sems["ve"], ve_b3 + gi)
                            i = None
                            for j in range(8):
                                jj = grp * 8 + j
                                i = tensor.transpose(
                                    out=psT[:, j * P:(j + 1) * P],
                                    in_=src[:, jj * P:(jj + 1) * P],
                                    identity=id16[:])
                            inc(i, "mm")

                    @blk.vector
                    def _(vector):
                        for gi in range(4):
                            dstv = (pt_hi_v, pt_hi_v, pt_lo_v, pt_lo_v)[gi]
                            grp = gi % 2
                            vector.wait_ge(sems["mm"], mm_b3 + gi + 1)
                            vector.tensor_copy(
                                out=dstv[:, grp * 8:(grp + 1) * 8,
                                         qt * P:(qt + 1) * P],
                                in_=psT[:, 0:1024].rearrange(
                                    "p (j q) -> p j q", q=P))
                            incd(vector, "ve")

                if PARANOID & 16:
                    emit_barrier()

                @blk.tensor
                def _(tensor):
                    wall(tensor, "ve")
                    pd = ps[4 + (qg % 2)]
                    i = None
                    n = 0
                    for vt, pt in ((vt_hi_v, pt_hi_v), (vt_hi_v, pt_lo_v),
                                   (vt_lo_v, pt_hi_v)):
                        for j in range(HC):
                            i = tensor.matmul(pd[:, :T0], vt[:, j],
                                              pt[:, j, :], start=(n == 0),
                                              stop=(n == 3 * HC - 1))
                            n += 1
                    inc(i, "mm")

                @blk.vector
                def _(vector):
                    wall(vector, "mm", "so0", "so1")
                    pd = ps[4 + (qg % 2)][:, :T0]
                    a16h = f16b[:, 1024 + (qg % 2) * 1024:
                                1024 + (qg % 2) * 1024 + T0]
                    a16l = f16b[:, 1536 + (qg % 2) * 1024:
                                1536 + (qg % 2) * 1024 + T0]
                    vector.tensor_copy(out=a16h, in_=pd)
                    vector.tensor_tensor(out=a16l, in0=pd, in1=a16h,
                                             op=ALU.subtract)
                    incd(vector, "ve")

                @blk.gpsimd
                def _(g):
                    wall(g, "ve")
                    a16h = f16b[:, 1024 + (qg % 2) * 1024:
                                1024 + (qg % 2) * 1024 + T0]
                    a16l = f16b[:, 1536 + (qg % 2) * 1024:
                                1536 + (qg % 2) * 1024 + T0]
                    inc(g.dma_start(
                        out=athi_d[h * P:(h + 1) * P, qg0:qg0 + T0],
                        in_=a16h), "so0", 16)
                    inc(g.dma_start(
                        out=atlo_d[h * P:(h + 1) * P, qg0:qg0 + T0],
                        in_=a16l), "so0", 16)

    NHD = 16

    def emit_mask_build(q0):
        col_to_row(dacc[:], rows[0:1, 0:P])

        @blk.vector
        def _(vector):
            wall(vector, "ve")
            vector.reduce_sum(out=rows[0:1, 128:129], in_=rows[0:1, 0:P],
                              axis=AX.X)
            vector.drain()
            vector.tensor_single_scalar(out=rows[0:1, 129:130],
                                            in_=rows[0:1, 128:129],
                                            scalar=float(S) * float(H) * 0.5,
                                            op=ALU.is_gt)
            incd(vector, "ve")

        bcast_row(rows[0:1, 129:130], 1, gcol[:, 0:1])

        ld_b = cnt["ld0"]
        ve_b = cnt["ve"]

        @blk.sync
        def _(sync):
            wall(sync, "ve")
            for r in range(8):
                if r >= 1:
                    sync.wait_ge(sems["ve"], ve_b + r)
                inc(sync.dma_start(out=bf16a[:, r * 2048:(r + 1) * 2048],
                                   in_=mss[r * P:(r + 1) * P, :]), "ld0", 16)
                inc(sync.dma_start(out=bf16st[:],
                                   in_=mdl[r * P:(r + 1) * P, :]), "ld0", 16)

        @blk.vector
        def _(vector):
            for r in range(8):
                vector.wait_ge(sems["ld0"], ld_b + 32 * (r + 1))
                vector.scalar_tensor_tensor(
                    out=bf16a[:, r * 2048:(r + 1) * 2048], in0=bf16st[:],
                    scalar=gcol[:, 0:1],
                    in1=bf16a[:, r * 2048:(r + 1) * 2048],
                    op0=ALU.mult, op1=ALU.add)
                incd(vector, "ve")

    def emit_wquant():
        for wi, (pack, tpack, n_oc, W) in enumerate(
                ((wup, tup, FF1C, 2048), (wdp, tdn, HC, 4096))):
            srcs = []
            dsts = []
            for oc in range(n_oc):
                if W == 2048:
                    srcs.append(pack[oc])
                    dsts.append(tpack[oc])
                else:
                    srcs.append(pack[oc][:, 0:2048])
                    srcs.append(pack[oc][:, 2048:4096])
                    dsts.append(tpack[oc][:, 0:2048])
                    dsts.append(tpack[oc][:, 2048:4096])
            ldsn = snap("ld")
            ve_b = cnt["ve"]

            @blk.sync
            def _(sync):
                wall(sync, "ve", "sc", "mm")
                for j in range(len(srcs)):
                    if j >= 2:
                        sync.wait_ge(sems["ve"], ve_b + (j - 1))
                    pdma(sync, f32w[:, (j % 2) * 2048:(j % 2) * 2048 + 2048],
                         srcs[j], "ld", j)

            @blk.vector
            def _(vector):
                vector.memset(misc[:, 512:513], 0.0)
                for j in range(len(srcs)):
                    pwait(vector, "ld", ldsn, j)
                    vector.reduce_sum(out=misc[:, 513:514],
                                      in_=f32w[:, (j % 2) * 2048:
                                               (j % 2) * 2048 + 2048],
                                      axis=AX.X, apply_absolute_value=True)
                    vector.drain()
                    vector.tensor_tensor(out=misc[:, 512:513],
                                             in0=misc[:, 512:513],
                                             in1=misc[:, 513:514],
                                             op=ALU.add)
                    incd(vector, "ve")

            col_to_row(misc[:, 512:513], rows[0:1, 0:P])

            @blk.vector
            def _(vector):
                wall(vector, "ve")
                vector.reduce_sum(out=rows[0:1, 128:129], in_=rows[0:1, 0:P],
                                  axis=AX.X)
                vector.drain()
                vector.tensor_single_scalar(out=rows[0:1, 128:129],
                                            in_=rows[0:1, 128:129],
                                            scalar=1.0 / (4096.0 * 2048.0),
                                            op=ALU.mult)
                vector.drain()
                vector.tensor_single_scalar(out=rows[0:1, 128:129],
                                                in_=rows[0:1, 128:129],
                                                scalar=1e-5, op=ALU.max)
                incd(vector, "ve")

            @blk.vector
            def _(vector):
                wall(vector, "ve")
                vector.reciprocal(out=rows[0:1, 129:130],
                                      in_=rows[0:1, 128:129])
                incd(vector, "ve")
                vector.tensor_tensor(out=rows[0:1, 130:131],
                                     in0=rows[0:1, 128:129],
                                     in1=rows[0:1, 129:130], op=ALU.mult)
                vector.drain()
                vector.tensor_scalar(out=rows[0:1, 130:131],
                                     in0=rows[0:1, 130:131], scalar1=-1.0,
                                     scalar2=2.0, op0=ALU.mult, op1=ALU.add)
                vector.drain()
                vector.tensor_tensor(out=rows[0:1, 129:130],
                                     in0=rows[0:1, 129:130],
                                     in1=rows[0:1, 130:131], op=ALU.mult)
                vector.drain()
                if wi == 0:
                    vector.tensor_single_scalar(out=rows[0:1, 131:132],
                                                in_=rows[0:1, 128:129],
                                                scalar=1.0 / 127.0,
                                                op=ALU.mult)
                    vector.drain()
                    vector.tensor_tensor(out=rows[0:1, 131:132],
                                             in0=rows[0:1, 131:132],
                                             in1=rows[0:1, 131:132],
                                             op=ALU.mult)
                    incd(vector, "ve")
                else:
                    vector.tensor_copy(out=rows[0:1, 131:132],
                                           in_=rows[0:1, 128:129])
                    incd(vector, "ve")

            if wi == 0:
                bcast_row(rows[0:1, 131:132], 1, qsc[:, 12:13])
                bcast_row(rows[0:1, 129:130], 1, qsc[:, 14:15])
            else:
                bcast_row(rows[0:1, 128:129], 1, qsc[:, 13:14])
                bcast_row(rows[0:1, 129:130], 1, qsc[:, 15:16])

            rcol = qsc[:, 14:15] if wi == 0 else qsc[:, 15:16]
            ldsn2 = snap("ld")
            ve_b2 = cnt["ve"]
            so_b2 = cnt["so0"]

            @blk.sync
            def _(sync):
                wall(sync, "ve")
                for j in range(len(srcs)):
                    if j >= 2:
                        sync.wait_ge(sems["ve"], ve_b2 + (j - 1))
                    pdma(sync, f32w[:, (j % 2) * 2048:(j % 2) * 2048 + 2048],
                         srcs[j], "ld", j)

            @blk.vector
            def _(vector):
                for j in range(len(srcs)):
                    pwait(vector, "ld", ldsn2, j)
                    if j >= 1:
                        vector.wait_ge(sems["so0"], so_b2 + 16 * j)
                    wv = f32w[:, (j % 2) * 2048:(j % 2) * 2048 + 2048]
                    vector.tensor_single_scalar(out=wv, in_=wv, scalar=rcol,
                                                op=ALU.mult)
                    vector.drain()
                    for hh in range(2):
                        seg = wv[:, hh * 1024:(hh + 1) * 1024]
                        sga = f32g[:, 1024:2048]
                        sgb = f32g[:, 3072:4096]
                        vector.tensor_single_scalar(out=sga, in_=seg,
                                                    scalar=0.5, op=ALU.is_ge)
                        vector.drain()
                        vector.tensor_single_scalar(out=sgb, in_=seg,
                                                    scalar=-0.5, op=ALU.is_le)
                        vector.drain()
                        vector.tensor_tensor(
                            out=bf16st[:, hh * 1024:(hh + 1) * 1024],
                            in0=sga, in1=sgb, op=ALU.subtract)
                        vector.drain()
                    vector.tensor_copy(out=misc[:, 512:513],
                                           in_=misc[:, 512:513])
                    incd(vector, "ve")

            @blk.gpsimd
            def _(g):
                for j in range(len(srcs)):
                    g.wait_ge(sems["ve"], ve_b2 + (j + 1))
                    inc(g.dma_start(out=dsts[j], in_=bf16st[:]), "so0", 16)

            @blk.sync
            def _(sync):
                wall(sync, "so0", "so1")

    def emit_ffn1_tile(xin, xout, t0):
        if PARANOID & 8:
            emit_barrier()
        emit_ln(xin, t0, T1, "absmax")
        macc = misc[:, 0:T1]
        AQ = f32g[:, 2048 + 256:2048 + 512]
        CQ = f32g[:, 2560 + 256:2560 + 512]

        def k8(ci):
            return bf16a[:, ci * T1:(ci + 1) * T1]

        def k82(oc):
            return bf16a[:, 4096 + oc * T1:4096 + (oc + 1) * T1]

        def quant_scales(cmode):
            for tb in range(T1 // P):
                @blk.tensor
                def _(tensor):
                    wall(tensor, "ve", "sc")
                    inc(tensor.transpose(out=ps[6][:, 0:P],
                                         in_=macc[:, tb * P:(tb + 1) * P],
                                         identity=id32[:]), "mm")

                @blk.vector
                def _(vector):
                    wall(vector, "mm")
                    vector.reduce_max(out=qsc[:, 0:1], in_=ps[6][:, 0:P],
                                      axis=AX.X)
                    vector.drain()
                    if cmode == "x":
                        vector.tensor_single_scalar(out=qsc[:, 1:2],
                                                        in_=qsc[:, 0:1],
                                                        scalar=1e-5,
                                                        op=ALU.max)
                        incd(vector, "ve")
                    else:
                        m1s = qsc[:, 2 + tb:3 + tb]
                        vector.tensor_tensor(out=qsc[:, 4:5], in0=m1s,
                                             in1=m1s, op=ALU.mult)
                        vector.drain()
                        vector.tensor_tensor(out=qsc[:, 4:5],
                                             in0=qsc[:, 4:5],
                                             in1=qsc[:, 12:13], op=ALU.mult)
                        vector.drain()
                        vector.tensor_tensor(out=qsc[:, 1:2],
                                             in0=qsc[:, 4:5],
                                             in1=qsc[:, 0:1], op=ALU.mult)
                        vector.drain()
                        vector.tensor_single_scalar(out=qsc[:, 1:2],
                                                        in_=qsc[:, 1:2],
                                                        scalar=1e-5,
                                                        op=ALU.max)
                        incd(vector, "ve")

                @blk.vector
                def _(vector):
                    wall(vector, "ve")
                    vector.reciprocal(out=qsc[:, 5:6], in_=qsc[:, 1:2])
                    vector.drain()
                    vector.tensor_tensor(out=qsc[:, 6:7], in0=qsc[:, 1:2],
                                         in1=qsc[:, 5:6], op=ALU.mult)
                    vector.drain()
                    vector.tensor_scalar(out=qsc[:, 6:7], in0=qsc[:, 6:7],
                                         scalar1=-1.0, scalar2=2.0,
                                         op0=ALU.mult, op1=ALU.add)
                    vector.drain()
                    vector.tensor_tensor(out=qsc[:, 5:6], in0=qsc[:, 5:6],
                                         in1=qsc[:, 6:7], op=ALU.mult)
                    vector.drain()
                    if cmode == "x":
                        vector.tensor_copy(out=qsc[:, 2 + tb:3 + tb],
                                           in_=qsc[:, 1:2])
                        vector.drain()
                        vector.tensor_single_scalar(
                            out=qsc[:, 7:8], in_=qsc[:, 5:6], scalar=127.0,
                            op=ALU.mult)
                        incd(vector, "ve")
                    else:
                        vector.tensor_tensor(out=qsc[:, 7:8],
                                             in0=qsc[:, 4:5],
                                             in1=qsc[:, 5:6], op=ALU.mult)
                        vector.drain()
                        vector.tensor_single_scalar(out=qsc[:, 7:8],
                                                    in_=qsc[:, 7:8],
                                                    scalar=127.0,
                                                    op=ALU.mult)
                        vector.drain()
                        vector.tensor_tensor(out=qsc[:, 8:9],
                                             in0=qsc[:, 1:2],
                                             in1=qsc[:, 13:14], op=ALU.mult)
                        vector.drain()
                        vector.tensor_single_scalar(
                            out=qsc[:, 8:9], in_=qsc[:, 8:9],
                            scalar=0.5 / 127.0, op=ALU.mult)
                        incd(vector, "ve")

                col_to_row(qsc[:, 7:8], rows[0:1, 512 + tb * P:512 + (tb + 1) * P])
                if cmode == "h":
                    col_to_row(qsc[:, 8:9],
                               rows2[0:1, tb * P:(tb + 1) * P])
            bcast_row(rows[0:1, 512:512 + T1], T1, AQ)
            if cmode == "h":
                bcast_row(rows2[0:1, 0:T1], T1, CQ)

        def peel_inline(vector, src_ap, dst_ap):
            vector.tensor_tensor(out=sqb[0][:, :T1], in0=src_ap,
                                 in1=AQ, op=ALU.mult)
            vector.tensor_scalar(out=sqb[0][:, :T1], in0=sqb[0][:, :T1],
                                 scalar1=127.25, scalar2=-127.25,
                                 op0=ALU.min, op1=ALU.max)
            vector.tensor_copy(out=sqb[1][:, :T1], in_=sqb[0][:, :T1])
            for b in range(7, -1, -1):
                thr = float(2 ** b)
                vector.tensor_single_scalar(out=misc[:, 256:256 + T1],
                                            in_=sqb[1][:, :T1],
                                            scalar=thr - 128.5,
                                            op=ALU.is_ge)
                vector.scalar_tensor_tensor(out=sqb[1][:, :T1],
                                            in0=misc[:, 256:256 + T1],
                                            scalar=-thr, in1=sqb[1][:, :T1],
                                            op0=ALU.mult, op1=ALU.add)
            vector.scalar_tensor_tensor(out=sqb[1][:, :T1],
                                        in0=sqb[1][:, :T1], scalar=-1.0,
                                        in1=sqb[0][:, :T1],
                                        op0=ALU.mult, op1=ALU.add)
            vector.tensor_single_scalar(out=dst_ap, in_=sqb[1][:, :T1],
                                            scalar=-128.0, op=ALU.add)
            incd(vector, "ve")

        if PARANOID & 8:
            emit_barrier()
        quant_scales("x")
        g2sn = snap("g2")
        ve_b = cnt["ve"]

        @blk.gpsimd
        def _(g):
            wall(g, "ve")
            for ci in range(HC):
                if ci >= 2:
                    g.wait_ge(sems["ve"], ve_b + 2 * (ci - 1))
                pdma(g, xsb[ci % 2][:, :T1],
                     xin[ci * P:(ci + 1) * P, t0:t0 + T1], "g2", ci)

        @blk.vector
        def _(vector):
            wall(vector, "mm")
            for ci in range(HC):
                pwait(vector, "g2", g2sn, ci)
                vector.tensor_tensor(out=xn2[0][:, :T1],
                                     in0=xsb[ci % 2][:, :T1],
                                     in1=A_[:, :T1], op=ALU.mult)
                vector.tensor_tensor(out=xn2[0][:, :T1],
                                         in0=xn2[0][:, :T1],
                                         in1=B_[:, :T1], op=ALU.add)
                incd(vector, "ve")
                peel_inline(vector, xn2[0][:, :T1], k8(ci))
        k8_done = cnt["ve"]

        if PARANOID & 8:
            emit_barrier()
        sc_b = cnt["sc"]
        ve_b1 = cnt["ve"]
        sosn1 = snap("so")
        mm_1 = {}
        mm_1b = cnt["mm"]

        def mm_1_pred(oc):
            return mm_1b + oc + 1

        ldsn1 = snap("ld")

        @blk.sync
        def _(sync):
            wall(sync, "ve")
            for oc in range(FF1C):
                if oc >= 2:
                    sync.wait_ge(sems["mm"], mm_1_pred(oc - 2))
                pdma(sync, bf16a[:, 12288 + (oc % 2) * 2048:
                                 12288 + (oc % 2) * 2048 + 2048],
                     tup[oc], "ld", oc)

        @blk.tensor
        def _(tensor):
            tensor.wait_ge(sems["ve"], k8_done)
            for oc in range(FF1C):
                pwait(tensor, "ld", ldsn1, oc)
                if oc >= 2:
                    tensor.wait_ge(sems["sc"], sc_b + (oc - 1))
                pd = ps[4 + (oc % 2)]
                i = None
                for kk in range(HC):
                    tb = bf16a[:, 12288 + (oc % 2) * 2048 + kk * P:
                               12288 + (oc % 2) * 2048 + (kk + 1) * P]
                    i = tensor.matmul(pd[:, :T1], tb, k8(kk),
                                      start=(kk == 0), stop=(kk == HC - 1))
                mm_1[oc] = inc(i, "mm")

        @blk.vector
        def _(vector):
            vector.memset(macc, 0.0)
            vector.tensor_copy(out=misc[:, 1023:1024],
                                   in_=misc[:, 1023:1024])
            incd(vector, "ve")

        @blk.scalar
        def _(scalar):
            for oc in range(FF1C):
                scalar.wait_ge(sems["mm"], mm_1[oc])
                if oc >= 2:
                    scalar.wait_ge(sems["ve"], ve_b1 + 1 + (oc - 1))
                    pwait(scalar, "so", sosn1, oc - 2)
                scalar.activation(out=misc[:, 512:512 + T1],
                                  in_=ps[4 + (oc % 2)][:, :T1], func=AF.Relu)
                scalar.activation(out=xn2[oc % 2][:, :T1],
                                      in_=misc[:, 512:512 + T1],
                                      func=AF.Square)
                incd(scalar, "sc")

        @blk.vector
        def _(vector):
            for oc in range(FF1C):
                vector.wait_ge(sems["sc"], sc_b + (oc + 1))
                vector.tensor_tensor(out=macc, in0=macc,
                                         in1=xn2[oc % 2][:, :T1],
                                         op=ALU.max)
                incd(vector, "ve")

        @blk.gpsimd
        def _(g):
            for oc in range(FF1C):
                g.wait_ge(sems["ve"], ve_b1 + 1 + (oc + 1))
                pdma(g, hscr[oc * P:(oc + 1) * P, t0:t0 + T1],
                     xn2[oc % 2][:, :T1], "so", oc)

        if PARANOID & 8:
            emit_barrier()
        quant_scales("h")
        g2sn2 = snap("g2")
        ve_b2 = cnt["ve"]

        def k2_pred(oc):
            return ve_b2 + oc + 1

        @blk.gpsimd
        def _(g):
            wall(g, "ve", "so0", "so1")
            for oc in range(FF1C):
                if oc >= 2:
                    g.wait_ge(sems["ve"], k2_pred(oc - 2))
                pdma(g, xsb[oc % 2][:, :T1],
                     hscr[oc * P:(oc + 1) * P, t0:t0 + T1], "g2", oc)

        @blk.vector
        def _(vector):
            wall(vector, "mm")
            for oc in range(FF1C):
                pwait(vector, "g2", g2sn2, oc)
                peel_inline(vector, xsb[oc % 2][:, :T1], k82(oc))
        k2_done = cnt["ve"]

        if DEV_DUMP:
            @blk.gpsimd
            def _(g):
                wall(g, "ve")
                for oc in range(FF1C):
                    inc(g.dma_start(out=dbg1[oc * P:(oc + 1) * P, t0:t0 + T1],
                                    in_=k82(oc)), "so0", 16)
                inc(g.dma_start(out=dbg2[:, 0:T1], in_=AQ), "so0", 16)
                inc(g.dma_start(out=dbg2[:, 256:256 + T1], in_=CQ), "so0", 16)
                wall(g, "so0")

        if PARANOID & 8:
            emit_barrier()
        ldsn2 = snap("ld")
        mm_2 = {}
        mm_2b = cnt["mm"]

        def mm_2_pred(j):
            return mm_2b + j + 1

        @blk.sync
        def _(sync):
            wall(sync, "ve")
            jj = 0
            for e in range(HC):
                for hf in range(2):
                    if jj >= 2:
                        sync.wait_ge(sems["mm"], mm_2_pred(jj - 2))
                    pdma(sync, bf16a[:, 12288 + (jj % 2) * 2048:
                                     12288 + (jj % 2) * 2048 + 2048],
                         tdn[e][:, hf * 2048:(hf + 1) * 2048], "ld", jj)
                    jj += 1

        ve_b3 = cnt["ve"]
        g2sn3 = snap("g2")
        sosn3 = snap("so")

        @blk.tensor
        def _(tensor):
            tensor.wait_ge(sems["ve"], k2_done)
            jj = 0
            for e in range(HC):
                if e >= 2:
                    tensor.wait_ge(sems["ve"], ve_b3 + (e - 1))
                for hf in range(2):
                    pwait(tensor, "ld", ldsn2, jj)
                    i = None
                    for kk in range(HC):
                        oc = hf * HC + kk
                        tb = bf16a[:, 12288 + (jj % 2) * 2048 + kk * P:
                                   12288 + (jj % 2) * 2048 + (kk + 1) * P]
                        i = tensor.matmul(ps[4 + (e % 2)][:, :T1], tb,
                                          k82(oc),
                                          start=(hf == 0 and kk == 0),
                                          stop=(hf == 1 and kk == HC - 1))
                    mm_2[jj] = inc(i, "mm")
                    jj += 1

        @blk.gpsimd
        def _(g):
            g.wait_ge(sems["ve"], k2_done)
            for e in range(HC + 2):
                if e < HC:
                    if e >= 2:
                        g.wait_ge(sems["ve"], ve_b3 + (e - 1))
                    pdma(g, sqb[e % 2][:, :T1],
                         xin[e * P:(e + 1) * P, t0:t0 + T1], "g2", e)
                if e >= 2:
                    ee = e - 2
                    g.wait_ge(sems["ve"], ve_b3 + (ee + 1))
                    pdma(g, xout[ee * P:(ee + 1) * P, t0:t0 + T1],
                         xn2[ee % 2][:, :T1], "so", ee)

        @blk.vector
        def _(vector):
            for e in range(HC):
                vector.wait_ge(sems["mm"], mm_2[2 * e + 1])
                pwait(vector, "g2", g2sn3, e)
                if e >= 2:
                    pwait(vector, "so", sosn3, e - 2)
                vector.tensor_tensor(out=misc[:, 512:512 + T1],
                                     in0=ps[4 + (e % 2)][:, :T1],
                                     in1=CQ, op=ALU.mult)
                vector.tensor_tensor(out=xn2[e % 2][:, :T1],
                                         in0=misc[:, 512:512 + T1],
                                         in1=sqb[e % 2][:, :T1],
                                         op=ALU.add)
                incd(vector, "ve")

        @blk.gpsimd
        def _(g):
            for ee in (HC - 2, HC - 1):
                g.wait_ge(sems["ve"], ve_b3 + (ee + 1))
                pdma(g, xout[ee * P:(ee + 1) * P, t0:t0 + T1],
                     xn2[ee % 2][:, :T1], "so", ee)
            wall(g, "so0", "so1")

    def emit_output(xcur, col0):
        ldsn = snap("ld")
        sosn = snap("so")
        if DEV_ONE_TILE:
            jlist = [(ci, 0) for ci in range(HC)]
        else:
            jlist = [(j // 2, j % 2) for j in range(2 * HC)]

        @blk.sync
        def _(sync):
            wall(sync, "ve", "sc", "mm", "so0", "so1", "g20", "g21")
            for j, (ci, half) in enumerate(jlist):
                if j >= 2:
                    pwait(sync, "so", sosn, j - 2)
                pdma(sync, xsb[j % 2][:, :T0],
                     xcur[ci * P:(ci + 1) * P,
                          col0 + half * T0:col0 + (half + 1) * T0], "ld", j)

        @blk.gpsimd
        def _(g):
            for j, (ci, half) in enumerate(jlist):
                pwait(g, "ld", ldsn, j)
                pdma(g, y[ci * P:(ci + 1) * P, half * T0:(half + 1) * T0],
                     xsb[j % 2][:, :T0], "so", j)
            wall(g, "so0", "so1")

        @blk.sync
        def _(sync):
            wall(sync, "so0", "so1")

    half_off = None
    emit_wquant()

    seq = []
    if n_sub >= 1:
        seq.append(("ffn0", x0, xa))
    if n_sub >= 2:
        seq.append(("attn0", xa, xb))
    if n_sub >= 3:
        seq.append(("ffn0", xb, xa))
    if n_sub >= 4:
        seq.append(("ffn1", xa, xb))
    if n_sub >= 5:
        seq.append(("attn1", xb, xa))
    if n_sub >= 6:
        seq.append(("ffn1_own", xa, xb))

    OWN0 = None

    _lim = (lambda stop, step: [0] if DEV_ONE_TILE else
            list(range(0, stop, step)))
    for name, xi, xo in seq:
        if name == "ffn0":
            for t0 in _lim(S, T0):
                emit_ffn0_tile(xi, xo, t0)
        elif name == "ffn1":
            for t0 in _lim(S, T1):
                emit_ffn1_tile(xi, xo, t0)
        elif name == "ffn1_own":
            for t0 in range(OWN_OFF, OWN_OFF + OWN, T1):
                emit_ffn1_tile(xi, xo, t0)
        elif name == "attn0":
            for t0 in range(0, S, T0):
                emit_ln(xi, t0, T0, "hilo")
                emit_proj(wqp[0], qhi_d, qlo_d, t0, out_scale=0.08838834764831845)
                emit_proj(wkp[0], khi_d, klo_d, t0)
                emit_proj(wvp[0], vhi_d, vlo_d, t0)
            emit_attn_core(0, 0, S)
            for t0 in range(0, S, T0):
                emit_oproj(wop[0], t0, xi, xo)
        elif name == "attn1":
            for t0 in range(0, S, T0):
                emit_ln(xi, t0, T0, "hilo", density=True)
                emit_proj(wkp[1], khi_d, klo_d, t0, bias_sb=bk_sb)
                emit_proj(wvp[1], vhi_d, vlo_d, t0, bias_sb=bv_sb)
                if OWN_OFF <= t0 < OWN_OFF + OWN:
                    emit_proj(wqp[1], qhi_d, qlo_d, t0, bias_sb=bq_sb, out_scale=0.08838834764831845)
            emit_mask_build(OWN_OFF)
            emit_attn_core(1, OWN_OFF, OWN)
            for t0 in range(OWN_OFF, OWN_OFF + OWN, T0):
                emit_oproj(wop[1], t0, xi, xo, bias_sb=bo_sb)

    xfinal = seq[-1][2] if seq else x0
    emit_output(xfinal, OWN_OFF if n_sub >= 5 else OWN_OFF)

    es.close()
    return nc


OWN_OFF = 0


def _host_pack(W):
    WT = np.ascontiguousarray(W.T.astype(np.float32))
    KC = WT.shape[0] // P
    OC = WT.shape[1] // P
    return np.ascontiguousarray(
        WT.reshape(KC, P, OC, P).transpose(2, 1, 0, 3).reshape(OC, P, KC * P))


def _bias_cols(b):
    n = b.shape[0] // P
    return np.ascontiguousarray(b.reshape(n, P).T.astype(np.float32))


def kernel(**inputs):
    import ml_dtypes

    x = np.asarray(inputs["L0_attn_hidden_states"], np.float32)
    w0p = _host_pack(np.asarray(inputs["L0_ffn_W0"]))
    w1p = _host_pack(np.asarray(inputs["L0_ffn_W1"]))
    b0v = _bias_cols(np.asarray(inputs["L0_ffn_bias0"]))
    b1v = _bias_cols(np.asarray(inputs["L0_ffn_bias1"]))
    packs = {}
    for L in (0, 1):
        pre = "L%d_attn_" % L
        packs[("q", L)] = _host_pack(np.asarray(inputs[pre + "Wq"]))
        packs[("k", L)] = _host_pack(np.asarray(inputs[pre + "Wk"]))
        packs[("v", L)] = _host_pack(np.asarray(inputs[pre + "Wv"]))
        packs[("o", L)] = _host_pack(np.asarray(inputs[pre + "Wo"]))
    bqv = _bias_cols(np.asarray(inputs["L1_attn_bq"]))
    bkv = _bias_cols(np.asarray(inputs["L1_attn_bk"]))
    bvv = _bias_cols(np.asarray(inputs["L1_attn_bv"]))
    bov = _bias_cols(np.asarray(inputs["L1_attn_bo"]))
    wup = _host_pack(np.asarray(inputs["L1_ffn_w_up"]))
    wdp = _host_pack(np.asarray(inputs["L1_ffn_w_down"]))

    idn16 = np.eye(P, dtype=np.float16)
    idn32 = np.eye(P, dtype=np.float32)

    j = np.arange(S)
    nc = _build(N_SUBLAYERS)

    in_maps = []
    for c in range(8):
        b, half = c // 2, c % 2
        perm = np.concatenate([np.arange(half * OWN, half * OWN + OWN),
                               np.arange((1 - half) * OWN,
                                         (1 - half) * OWN + OWN)])
        x0 = np.ascontiguousarray(x[b][perm].T)
        gi = perm[:OWN]
        di = gi[:, None] - perm[None, :]
        local = np.abs(di) <= 64
        strided = (di % 4) == 0
        glob = (gi[:, None] == 0) | (perm[None, :] == 0)
        msl = np.where(local | glob, 0.0, NEGBIG).astype(np.float32)
        mst = np.where(strided | glob, 0.0, NEGBIG).astype(np.float32)
        mdlv = (msl - mst).astype(ml_dtypes.bfloat16)
        mssv = mst.astype(ml_dtypes.bfloat16)
        im = dict(x0=x0, w0p=w0p, w1p=w1p, b0v=b0v, b1v=b1v,
                  wqp0=packs[("q", 0)], wqp1=packs[("q", 1)],
                  wkp0=packs[("k", 0)], wkp1=packs[("k", 1)],
                  wvp0=packs[("v", 0)], wvp1=packs[("v", 1)],
                  wop0=packs[("o", 0)], wop1=packs[("o", 1)],
                  bqv=bqv, bkv=bkv, bvv=bvv, bov=bov,
                  wup=wup, wdp=wdp, mss=mssv, mdl=mdlv,
                  idn16=idn16, idn32=idn32)
        in_maps.append(im)

    res = run_bass_kernel_spmd(nc, in_maps, core_ids=list(range(8)))
    out = np.empty((4, S, H), np.float32)
    for c in range(8):
        b, half = c // 2, c % 2
        out[b, half * OWN:(half + 1) * OWN, :] = res.results[c]["y"].T
    return out
